# version 02
# baseline (speedup 1.0000x reference)
import contextlib
import numpy as np
import ml_dtypes
import concourse.bass as bass
import concourse.mybir as mybir
from concourse.bass import AutoArena
from concourse.bass_utils import run_bass_kernel_spmd

F32 = mybir.dt.float32
BF16 = mybir.dt.bfloat16
I32 = mybir.dt.int32
FP8 = mybir.dt.float8e4
ACT = mybir.ActivationFunctionType
ALU = mybir.AluOpType
AX = mybir.AxisListType

NCORE = 8
T = 8192
D = 2048
TL = T // NCORE
HIST = 64
TW = TL + HIST
LF = 64
S = 16
HID = 5632
II = 8192
HD = 128
NH = 16
DC = D // 128
NTT = (TW + 127) // 128
EPS_LN = 1e-5

ENGS = ('sync', 'vector', 'scalar', 'tensor', 'gpsimd')
DMA_Q = {'sync': 'qs', 'gpsimd': 'qg', 'vector': 'qv', 'scalar': 'qc'}
SEM_NAMES = ('qs', 'qg', 'qv', 'qc', 'v', 's', 't', 'g')
ENG_SEM = {'vector': 'v', 'scalar': 's', 'tensor': 't', 'gpsimd': 'g', 'sync': 'qs'}
COMPUTE_SEMS = ('v', 's', 't', 'g')


class Buf:
    __slots__ = ('wtok', 'rtoks', 'name')
    def __init__(self, name=''):
        self.wtok = None
        self.rtoks = []
        self.name = name


class P:
    def __init__(self, nc):
        self.nc = nc
        self.items = {e: [] for e in ENGS}
        self.cnt = {sm: 0 for sm in SEM_NAMES}
        self.fence_deps = {}

    def do(self, eng, fn, reads=(), writes=(), inc=1, sem=None, sreads=()):
        sem = sem or ENG_SEM[eng]
        for b in sreads:
            if b.wtok is not None and b.wtok[0] in ('v', 's', 'g') and b.wtok[1] >= self.cnt[b.wtok[0]]:
                self._settle(b.wtok[0])
        deps = dict(self.fence_deps)
        def need(tok):
            if tok is None:
                return
            sm, val = tok
            if sm == sem and sm in COMPUTE_SEMS:
                return
            if deps.get(sm, 0) < val:
                deps[sm] = val
        def need_s(tok):
            if tok is None:
                return
            sm, val = tok
            if sm == sem and sm in COMPUTE_SEMS:
                return
            if deps.get(sm, 0) < val:
                deps[sm] = val
        for b in reads:
            need(b.wtok)
        for b in sreads:
            need_s(b.wtok)
        for b in writes:
            need(b.wtok)
            for t_ in b.rtoks:
                need(t_)
        if sem in deps and sem in COMPUTE_SEMS:
            del deps[sem]
        for sm in ('v', 's', 'g'):
            if sm in deps and sm != sem:
                if deps[sm] >= self.cnt[sm]:
                    self._settle(sm)
                deps[sm] = min(deps[sm] + 1, self.cnt[sm])
        self.items[eng].append((deps, fn, sem, inc))
        self.cnt[sem] += inc
        tok = (sem, self.cnt[sem])
        for b in reads:
            b.rtoks.append(tok)
        for b in sreads:
            b.rtoks.append(tok)
        for b in writes:
            b.wtok = tok
            b.rtoks = []
        return tok

    def _settle(self, sm):
        eng = {'v': 'vector', 's': 'scalar', 'g': 'gpsimd'}[sm]
        st = self.settle_tile
        if sm == 'v':
            fn = lambda e: e.tensor_copy(out=st[0:1, 0:1], in_=st[0:1, 1:2])
        elif sm == 's':
            fn = lambda e: e.activation(out=st[0:1, 0:1], in_=st[0:1, 1:2], func=ACT.Copy)
        else:
            fn = lambda e: e.memset(st[0:1, 0:1], 0.0)
        self.items[eng].append(({}, fn, sm, 1))
        self.cnt[sm] += 1

    def dma(self, eng, out, in_, reads=(), writes=(), slow=False):
        q = DMA_Q[eng]
        if slow:
            fn = lambda e, o=out, i=in_: e.dma_start(out=o, in_=i, allow_slow_non_contiguous=True)
        else:
            fn = lambda e, o=out, i=in_: e.dma_start(out=o, in_=i)
        return self.do(eng, fn, reads=reads, writes=writes, inc=16, sem=q)

    def pad(self, sm, n):
        for _ in range(n):
            self._settle(sm)

    def fence(self):
        self.fence_deps = {sm: self.cnt[sm] for sm in SEM_NAMES if self.cnt[sm] > 0}

    def barrier_tok(self):
        return [(sm, self.cnt[sm]) for sm in SEM_NAMES if self.cnt[sm] > 0]

    def mk_emitter(self, its, sems, finals):
        def g(e):
            cur = {sm: 0 for sm in SEM_NAMES}
            for deps, fn, sem, inc in its:
                for sm in sorted(deps):
                    val = deps[sm]
                    if val > cur[sm]:
                        e.wait_ge(sems[sm], val)
                        cur[sm] = val
                fn(e).then_inc(sems[sem], inc)
            for sm, val in finals:
                if val > cur[sm]:
                    e.wait_ge(sems[sm], val)
                    cur[sm] = val
        return g


def build_program(phases=('A', 'B', 'C', 'D'), dbg=()):
    nc = bass.Bass("TRN2", target_bir_lowering=False)
    p = P(nc)

    def DI(name, shape, dt=F32):
        return nc.dram_tensor(name, shape, dt, kind="ExternalInput").ap()
    out_names = []
    def DO(name, shape, dt=F32):
        out_names.append(name)
        return nc.dram_tensor(name, shape, dt, kind="ExternalOutput").ap()

    i_x = DI("i_x", [D, TW])
    i_piw = DI("i_piw", [D, D]); i_pib = DI("i_pib", [D, 1])
    i_pow = DI("i_pow", [D, D]); i_pob = DI("i_pob", [D, 1])
    i_A = DI("i_A", [D, S]); i_BwT = DI("i_BwT", [D, S])
    i_C2 = DI("i_C2", [LF * S, D]); i_Dp = DI("i_Dp", [D, 1])
    i_hm = DI("i_hm", [S, HIST])
    i_idf = DI("i_idf", [128, 128], F32)
    i_idb = DI("i_idb", [128, 128], BF16)
    i_const = DI("i_const", [128, 8], F32)
    i_rep16 = DI("i_rep16", [16, 128], F32)
    i_ones = DI("i_ones", [128, 128], F32)
    i_wg = DI("i_wg", [D, HID], I32)
    i_wu = DI("i_wu", [D, HID], I32)
    i_wd = DI("i_wd", [HID, D], I32)
    i_sg = DI("i_sg", [HID, 1]); i_su = DI("i_su", [HID, 1]); i_sd2 = DI("i_sd2", [D, 1])
    i_wq = DI("i_wq", [D, D]); i_wk = DI("i_wk", [D, D]); i_wv = DI("i_wv", [D, D]); i_wo = DI("i_wo", [D, D])
    i_cos = DI("i_cos", [128, TL]); i_sin = DI("i_sin", [128, TL])
    i_gw = DI("i_gw", [D, II]); i_uw = DI("i_uw", [D, II]); i_dw = DI("i_dw", [II, D])
    i_gwn = DI("i_gwn", [TL, D]); i_uwn = DI("i_uwn", [TL, D]); i_dwn = DI("i_dwn", [D // NCORE, II])
    i_g2p = DI("i_g2p", [4, 128])

    d_xp = nc.dram_tensor("d_xp", [D, TW], F32).ap()
    d_s1 = nc.dram_tensor("d_s1", [D, TL], F32).ap()
    d_s2 = nc.dram_tensor("d_s2", [D, TL], F32).ap()
    d_s3 = nc.dram_tensor("d_s3", [D, TL], F32).ap()
    d_kin = nc.dram_tensor("d_kin", [D, TL], BF16).ap()
    d_kag = nc.dram_tensor("d_kag", [NCORE * D, TL], BF16).ap()
    d_vin = nc.dram_tensor("d_vin", [TL, D], BF16).ap()
    d_vag = nc.dram_tensor("d_vag", [T, D], BF16).ap()
    d_kmi = nc.dram_tensor("d_kmi", [D, 1], F32).ap()
    d_rm = nc.dram_tensor("d_rm", [128, 16], F32).ap()
    d_gsci = nc.dram_tensor("d_gsci", [TL, 128], F32).ap()
    d_gsco = nc.dram_tensor("d_gsco", [II, 128], F32).ap()
    d_usci = nc.dram_tensor("d_usci", [TL, 128], F32).ap()
    d_usco = nc.dram_tensor("d_usco", [II, 128], F32).ap()
    d_dsci = nc.dram_tensor("d_dsci", [D // NCORE, 512], F32).ap()
    d_dsco = nc.dram_tensor("d_dsco", [D, 512], F32).ap()
    d_hq = nc.dram_tensor("d_hq", [II, TL], BF16).ap()
    d_kmo = nc.dram_tensor("d_kmo", [D, 1], F32).ap()

    dbg_outs = {}
    if 'A' in dbg:
        dbg_outs['o_s1'] = DO("o_s1", [D, TL])
    if 'B' in dbg:
        dbg_outs['o_s2'] = DO("o_s2", [D, TL])
    if 'C' in dbg:
        dbg_outs['o_s3'] = DO("o_s3", [D, TL])
    if 'Ctaps' in dbg:
        dbg_outs['o_ao'] = DO("o_ao", [128, 8 * D], BF16)
        dbg_outs['o_negm'] = DO("o_negm", [128, NH])
        dbg_outs['o_x4b'] = DO("o_x4b", [128, DC * TL], BF16)
        dbg_outs['o_qb'] = DO("o_qb", [128, NH * TL], BF16)
        dbg_outs['o_x8f'] = DO("o_x8f", [128, DC * TL], BF16)
        dbg_outs['o_x2ln'] = DO("o_x2ln", [128, DC * TL])
        dbg_outs['o_cm'] = DO("o_cm", [128, TL])
        dbg_outs['o_rm'] = DO("o_rm", [128, 16])
        dbg_outs['o_srow'] = DO("o_srow", [1, 2 * TL])
        dbg_outs['o_sb4'] = DO("o_sb4", [128, TL])
        dbg_outs['o_rsb4'] = DO("o_rsb4", [128, TL])
    if 'Ataps' in dbg:
        dbg_outs['o_x'] = DO("o_x", [128, DC * TW])
        dbg_outs['o_ln'] = DO("o_ln", [128, DC * TW], BF16)
        dbg_outs['o_xp'] = DO("o_xp", [D, TW])
        dbg_outs['o_dlt'] = DO("o_dlt", [16, TW])
        dbg_outs['o_esk'] = DO("o_esk", [128, 8 * TL], BF16)
        dbg_outs['o_yb'] = DO("o_yb", [128, DC * TL], BF16)
    final_out = DO("out_shard", [D, TL])

    stack = contextlib.ExitStack()
    sems = {sm: stack.enter_context(nc.semaphore("sem_" + sm)) for sm in SEM_NAMES}
    psF = stack.enter_context(nc.psum_tensor("psF", [128, 3584], F32))
    psB = stack.enter_context(nc.psum_tensor("psB", [128, 1024], BF16))

    pers = AutoArena(nc)
    t_settle = pers.sbuf_tensor("t_settle", [128, 4], F32)
    p.settle_tile = t_settle
    t_idf = pers.sbuf_tensor("t_idf", [128, 128], F32)
    t_idb = pers.sbuf_tensor("t_idb", [128, 128], BF16)
    t_const = pers.sbuf_tensor("t_const", [128, 8], F32)
    t_rep16 = pers.sbuf_tensor("t_rep16", [16, 128], F32)
    t_ones = pers.sbuf_tensor("t_ones", [128, 128], F32)
    pers.allocate()

    bufs = {}
    def B(name):
        if name not in bufs:
            bufs[name] = Buf(name)
        return bufs[name]
    pbank = [B(f"psF{i}") for i in range(7)]
    pbB = B("psB")

    def emit_ln2(tSQ, tRow, tMB, tRB, tNrm, xc, w_total, lnb, blnb, bx, tag):
        bSQ = [B(tag + '_sq0'), B(tag + '_sq1')]
        bRow = B(tag + '_row'); bMB = B(tag + '_mb'); bRB = B(tag + '_rb'); bNrm = B(tag + '_nrm')
        nbl = [(n0, min(512, w_total - n0)) for n0 in range(0, w_total, 512)]
        for (n0, nw) in nbl:
            pbx = pbank[4]; pbq = pbank[5]
            for c in range(DC):
                db = c % 2
                p.do('scalar', lambda e, c=c, n0=n0, nw=nw, db=db: e.activation(
                    out=tSQ[db][:, 0:nw], in_=xc(c)[:, n0:n0 + nw], func=ACT.Square),
                    reads=[bx], writes=[bSQ[db]])
                p.do('tensor', lambda e, c=c, n0=n0, nw=nw: e.matmul(
                    out=psF[0:1, 2048:2048 + nw], lhsT=t_ones[:, 0:1], rhs=xc(c)[:, n0:n0 + nw],
                    start=(c == 0), stop=(c == DC - 1)),
                    reads=[bx, B('ones')], writes=[pbx])
                p.do('tensor', lambda e, c=c, nw=nw, db=db: e.matmul(
                    out=psF[0:1, 2560:2560 + nw], lhsT=t_ones[:, 0:1], rhs=tSQ[db][:, 0:nw],
                    start=(c == 0), stop=(c == DC - 1)),
                    reads=[bSQ[db], B('ones')], writes=[pbq])
            p.do('vector', lambda e, nw=nw: e.tensor_scalar(out=tRow[0:1, 0:nw], in0=psF[0:1, 2048:2048 + nw],
                 scalar1=1.0 / D, scalar2=None, op0=ALU.mult), reads=[pbx], writes=[bRow])
            p.do('vector', lambda e, nw=nw: e.tensor_scalar(out=tRow[0:1, 512:512 + nw], in0=psF[0:1, 2560:2560 + nw],
                 scalar1=1.0 / D, scalar2=None, op0=ALU.mult), reads=[pbq], writes=[bRow])
            p.do('vector', lambda e, nw=nw: e.tensor_tensor(out=tRow[0:1, 1024:1024 + nw], in0=tRow[0:1, 0:nw],
                 in1=tRow[0:1, 0:nw], op=ALU.mult), reads=[bRow], writes=[bRow])
            p.do('vector', lambda e, nw=nw: e.tensor_tensor(out=tRow[0:1, 1536:1536 + nw], in0=tRow[0:1, 512:512 + nw],
                 in1=tRow[0:1, 1024:1024 + nw], op=ALU.subtract), reads=[bRow], writes=[bRow])
            p.do('scalar', lambda e, nw=nw: e.activation(out=tRow[0:1, 2048:2048 + nw], in_=tRow[0:1, 1536:1536 + nw],
                 func=ACT.Sqrt, bias=t_const[0:1, 0:1], scale=1.0), reads=[bRow], sreads=[B('const')], writes=[bRow])
            p.do('vector', lambda e, nw=nw: e.reciprocal(out=tRow[0:1, 1536:1536 + nw], in_=tRow[0:1, 2048:2048 + nw]),
                 reads=[bRow], writes=[bRow])
            p.do('tensor', lambda e, nw=nw: e.matmul(
                out=psF[:, 0:nw], lhsT=t_ones[0:1, 0:128], rhs=tRow[0:1, 0:nw], start=True, stop=True),
                reads=[bRow, B('ones')], writes=[pbank[0]])
            p.do('vector', lambda e, nw=nw: e.tensor_copy(out=tMB[:, 0:nw], in_=psF[:, 0:nw]),
                 reads=[pbank[0]], writes=[bMB])
            p.do('tensor', lambda e, nw=nw: e.matmul(
                out=psF[:, 512:512 + nw], lhsT=t_ones[0:1, 0:128], rhs=tRow[0:1, 1536:1536 + nw], start=True, stop=True),
                reads=[bRow, B('ones')], writes=[pbank[1]])
            p.do('vector', lambda e, nw=nw: e.tensor_copy(out=tRB[:, 0:nw], in_=psF[:, 512:512 + nw]),
                 reads=[pbank[1]], writes=[bRB])
            for c in range(DC):
                p.do('vector', lambda e, c=c, n0=n0, nw=nw: e.tensor_tensor(
                    out=tNrm[:, 0:nw], in0=xc(c)[:, n0:n0 + nw], in1=tMB[:, 0:nw], op=ALU.subtract),
                    reads=[bx, bMB], writes=[bNrm])
                p.do('vector', lambda e, c=c, n0=n0, nw=nw: e.tensor_tensor(
                    out=lnb(c)[:, n0:n0 + nw], in0=tNrm[:, 0:nw], in1=tRB[:, 0:nw], op=ALU.mult),
                    reads=[bNrm, bRB], writes=[blnb])


    p.dma('sync', t_idf[:, :], i_idf[:, :], writes=[B('idf')])
    p.dma('sync', t_idb[:, :], i_idb[:, :], writes=[B('idb')])
    p.dma('sync', t_const[:, :], i_const[:, :], writes=[B('const')])
    p.dma('sync', t_rep16[:, :], i_rep16[:, :], writes=[B('rep16')])
    p.dma('sync', t_ones[:, :], i_ones[:, :], writes=[B('ones')])

    if 'A' in phases:
        arL = AutoArena(nc)
        tLNB = arL.sbuf_tensor("A_LNB", [128, DC * TW], BF16)
        tEskB = arL.sbuf_tensor("A_eskB", [128, 8 * TL], BF16)
        tPib = arL.sbuf_tensor("A_pib", [128, DC], F32)
        tPob = arL.sbuf_tensor("A_pob", [128, DC], F32)
        tDp = arL.sbuf_tensor("A_dp", [128, DC], F32)
        tHm = arL.sbuf_tensor("A_hm", [16, HIST], F32)
        tBwf = arL.sbuf_tensor("A_bwf", [128, DC * S], F32)
        tAf = arL.sbuf_tensor("A_af", [128, DC * S], F32)
        arL.allocate()

        p.dma('sync', tPib[:, :].rearrange("p (c o) -> p c o", o=1), i_pib.rearrange("(c p) o -> p c o", p=128), writes=[B('A_pib')], slow=True)
        p.dma('sync', tPob[:, :].rearrange("p (c o) -> p c o", o=1), i_pob.rearrange("(c p) o -> p c o", p=128), writes=[B('A_pob')], slow=True)
        p.dma('sync', tDp[:, :].rearrange("p (c o) -> p c o", o=1), i_Dp.rearrange("(c p) o -> p c o", p=128), writes=[B('A_dp')], slow=True)
        p.dma('sync', tHm[:, :], i_hm[:, :], writes=[B('A_hm')])
        p.dma('sync', tAf[:, :].rearrange("p (c s) -> p c s", c=DC), i_A.rearrange("(c p) s -> p c s", p=128), writes=[B('A_af')], slow=True)
        p.dma('sync', tBwf[:, :].rearrange("p (c s) -> p c s", c=DC), i_BwT.rearrange("(c p) s -> p c s", p=128), writes=[B('A_bwf')], slow=True)

        ar1 = AutoArena(nc)
        tX = ar1.sbuf_tensor("A_X", [128, DC * TW], F32)
        tSQ = [ar1.sbuf_tensor(f"A_sq{i}", [128, 512], F32) for i in range(2)]
        tRow = ar1.sbuf_tensor("A_row", [1, 512 * 5], F32)
        tMB = ar1.sbuf_tensor("A_mb", [128, 512], F32)
        tRB = ar1.sbuf_tensor("A_rb", [128, 512], F32)
        tNrm = ar1.sbuf_tensor("A_nrm", [128, 512], F32)
        ar1.allocate()

        p.dma('sync', tX[:, :].rearrange("p (c t) -> p c t", c=DC),
              i_x.rearrange("(c p) t -> p c t", p=128), writes=[B('A_X')])
        bLNB = B('A_LNB')

        emit_ln2(tSQ, tRow, tMB, tRB, tNrm,
                 lambda c: tX[:, c * TW:(c + 1) * TW], TW,
                 lambda c: tLNB[:, c * TW:(c + 1) * TW], bLNB, B('A_X'), 'Aln')
        p.fence()
        ar1.free()

        ar2 = AutoArena(nc)
        tXPB = ar2.sbuf_tensor("A_XPB", [128, DC * TW], BF16)
        tP = ar2.sbuf_tensor("A_P", [128, DC * 8 * 128], BF16)
        tWf = [ar2.sbuf_tensor(f"A_wf{i}", [128, D], F32) for i in range(2)]
        tWb = [ar2.sbuf_tensor(f"A_wb{i}", [128, D], BF16) for i in range(2)]
        tSc = [ar2.sbuf_tensor(f"A_sc{i}", [128, TW], F32) for i in range(2)]
        tBwb = ar2.sbuf_tensor("A_bwb", [128, DC * S], BF16)
        tDlt = ar2.sbuf_tensor("A_dlt", [16, TW], F32)
        tDla = ar2.sbuf_tensor("A_dla", [16, TW], F32)
        tDrep = ar2.sbuf_tensor("A_drep", [128, TW], F32)
        tSig = ar2.sbuf_tensor("A_sig", [128, DC * S], F32)
        tPrev = ar2.sbuf_tensor("A_prev", [128, DC * S], F32)
        tEt = [ar2.sbuf_tensor(f"A_et{i}", [128, TW], F32) for i in range(2)]
        tEsk = ar2.sbuf_tensor("A_esk", [128, TL], F32)
        ar2.allocate()

        NBL = [(0, 512), (512, 512), (1024, TW - 1024)]
        bXPB = B('A_XPB')
        for m in range(DC):
            db = m % 2
            bWf = B(f'A_wf{db}'); bWb = B(f'A_wb{db}'); bSc = B(f'A_sc{db}')
            p.dma('sync', tWf[db][:, :].rearrange("p (c k) -> p c k", c=DC),
                  i_piw.rearrange("(c p) mm -> p c mm", p=128)[:, :, m * 128:(m + 1) * 128],
                  writes=[bWf])
            p.do('vector', lambda e, db=db: e.tensor_copy(out=tWb[db][:, :], in_=tWf[db][:, :]),
                 reads=[bWf], writes=[bWb])
            for (n0, nw) in NBL:
                pb = pbank[2 + n0 // 512]
                for c in range(DC):
                    p.do('tensor', lambda e, c=c, n0=n0, nw=nw, db=db: e.matmul(
                        out=psF[:, 1024 + n0: 1024 + n0 + nw],
                        lhsT=tWb[db][:, c * 128:(c + 1) * 128],
                        rhs=tLNB[:, c * TW + n0: c * TW + n0 + nw],
                        start=(c == 0), stop=(c == DC - 1)),
                        reads=[bWb, bLNB], writes=[pb])
                p.do('vector', lambda e, m=m, n0=n0, nw=nw, db=db: e.tensor_scalar(
                    out=tSc[db][:, n0:n0 + nw], in0=psF[:, 1024 + n0:1024 + n0 + nw],
                    scalar1=tPib[:, m:m + 1], scalar2=None, op0=ALU.add),
                    reads=[pb], sreads=[B('A_pib')], writes=[bSc])
            p.do('vector', lambda e, m=m, db=db: e.tensor_copy(
                out=tXPB[:, m * TW:(m + 1) * TW], in_=tSc[db][:, :]),
                reads=[bSc], writes=[bXPB])
            p.dma('sync', d_xp[m * 128:(m + 1) * 128, :], tSc[db][:, :], reads=[bSc], writes=[B('d_xp')])

        if 'Ataps' in dbg:
            for m in range(DC):
                p.dma('gpsimd', dbg_outs['o_xp'][m * 128:(m + 1) * 128, :], d_xp[m * 128:(m + 1) * 128, :],
                      reads=[B('d_xp')], writes=[B('o_xp')])
        bBwb = B('A_bwb')
        p.do('vector', lambda e: e.tensor_copy(out=tBwb[:, :], in_=tBwf[:, :]), reads=[B('A_bwf')], writes=[bBwb])
        bDlt = B('A_dlt'); bDla = B('A_dla')
        for (n0, nw) in NBL:
            pb = pbank[n0 // 512]
            for c in range(DC):
                p.do('tensor', lambda e, c=c, n0=n0, nw=nw: e.matmul(
                    out=psF[0:16, n0:n0 + nw],
                    lhsT=tBwb[:, c * S:(c + 1) * S],
                    rhs=tXPB[:, c * TW + n0: c * TW + n0 + nw],
                    start=(c == 0), stop=(c == DC - 1)),
                    reads=[bBwb, bXPB], writes=[pb])
            p.do('scalar', lambda e, n0=n0, nw=nw: e.activation(out=tDla[:, n0:n0 + nw], in_=psF[0:16, n0:n0 + nw], func=ACT.Abs),
                 reads=[pb], writes=[bDla])
            p.do('scalar', lambda e, n0=n0, nw=nw: e.activation(out=tDla[:, n0:n0 + nw], in_=tDla[:, n0:n0 + nw], func=ACT.Exp, scale=-1.0),
                 reads=[bDla], writes=[bDla])
            p.do('scalar', lambda e, n0=n0, nw=nw: e.activation(out=tDla[:, n0:n0 + nw], in_=tDla[:, n0:n0 + nw], func=ACT.Ln, bias=1.0, scale=1.0),
                 reads=[bDla], writes=[bDla])
            p.do('scalar', lambda e, n0=n0, nw=nw: e.activation(out=tDlt[:, n0:n0 + nw], in_=psF[0:16, n0:n0 + nw], func=ACT.Relu),
                 reads=[pb], writes=[bDlt])
            p.do('vector', lambda e, n0=n0, nw=nw: e.tensor_tensor(out=tDlt[:, n0:n0 + nw], in0=tDlt[:, n0:n0 + nw], in1=tDla[:, n0:n0 + nw], op=ALU.add),
                 reads=[bDla], writes=[bDlt])
        p.do('vector', lambda e: e.tensor_tensor(out=tDlt[:, 0:HIST], in0=tDlt[:, 0:HIST], in1=tHm[:, :], op=ALU.mult),
             reads=[B('A_hm')], writes=[bDlt])
        bDrep = B('A_drep')
        for (n0, nw) in NBL:
            pb = pbank[3 + n0 // 512]
            p.do('tensor', lambda e, n0=n0, nw=nw: e.matmul(
                out=psF[:, 1536 + n0:1536 + n0 + nw], lhsT=t_rep16[:, :], rhs=tDlt[:, n0:n0 + nw],
                start=True, stop=True),
                reads=[B('rep16'), bDlt], writes=[pb])
            p.do('vector', lambda e, n0=n0, nw=nw: e.tensor_copy(out=tDrep[:, n0:n0 + nw], in_=psF[:, 1536 + n0:1536 + n0 + nw]),
                 reads=[pb], writes=[bDrep])

        if 'Ataps' in dbg:
            p.dma('gpsimd', dbg_outs['o_dlt'][:, :], tDlt[:, :], reads=[bDlt], writes=[B('o_dlt')])
        bSig = B('A_sig'); bPrev = B('A_prev'); bP = B('A_P')
        p.do('scalar', lambda e: e.activation(out=tSig[:, :], in_=tAf[:, :], func=ACT.Sigmoid), reads=[B('A_af')], writes=[bSig])
        p.do('gpsimd', lambda e: e.memset(tPrev[:, :], 1.0), writes=[bPrev])
        for l in range(LF):
            j, jj = l // 8, l % 8
            p.do('vector', lambda e, j=j, jj=jj: e.tensor_copy(
                out=tP[:, :].rearrange("p (c q) -> p c q", q=8 * 128)[:, :, j * 128 + jj * 16: j * 128 + jj * 16 + 16],
                in_=tPrev[:, :].rearrange("p (c s) -> p c s", c=DC)),
                reads=[bPrev], writes=[bP])
            if l < LF - 1:
                p.do('vector', lambda e: e.tensor_tensor(out=tPrev[:, :], in0=tPrev[:, :], in1=tSig[:, :], op=ALU.mult),
                     reads=[bSig], writes=[bPrev])

        bEskB = B('A_eskB')
        for j in range(8):
            db = j % 2
            bEt = B(f'A_et{db}'); bEsk = B('A_esk')
            for (n0, nw) in NBL:
                pb = pbank[3 + n0 // 512]
                for c in range(DC):
                    p.do('tensor', lambda e, c=c, j=j, n0=n0, nw=nw: e.matmul(
                        out=psF[:, 1536 + n0:1536 + n0 + nw],
                        lhsT=tP[:, (c * 8 + j) * 128:(c * 8 + j + 1) * 128],
                        rhs=tXPB[:, c * TW + n0:c * TW + n0 + nw],
                        start=(c == 0), stop=(c == DC - 1)),
                        reads=[bP, bXPB], writes=[pb])
                p.do('vector', lambda e, n0=n0, nw=nw, db=db: e.tensor_tensor(
                    out=tEt[db][:, n0:n0 + nw], in0=psF[:, 1536 + n0:1536 + n0 + nw],
                    in1=tDrep[:, n0:n0 + nw], op=ALU.mult),
                    reads=[pb, bDrep], writes=[bEt])
            for jj in range(8):
                l = 8 * j + jj
                p.dma('gpsimd', tEsk[16 * jj:16 * (jj + 1), :],
                      tEt[db][16 * jj:16 * (jj + 1), HIST - l: HIST - l + TL],
                      reads=[bEt], writes=[bEsk])
            p.do('vector', lambda e, j=j: e.tensor_copy(out=tEskB[:, j * TL:(j + 1) * TL], in_=tEsk[:, :]),
                 reads=[bEsk], writes=[bEskB])
        if 'Ataps' in dbg:
            p.dma('gpsimd', dbg_outs['o_esk'][:, :], tEskB[:, :], reads=[bEskB], writes=[B('o_esk')])
        p.fence()
        ar2.free()

        ar3 = AutoArena(nc)
        tC2f = ar3.sbuf_tensor("A_c2f", [128, D], F32)
        tC2b = ar3.sbuf_tensor("A_c2b", [128, 8 * D], BF16)
        tYB = ar3.sbuf_tensor("A_yb", [128, DC * TL], BF16)
        tWf3 = [ar3.sbuf_tensor(f"A3_wf{i}", [128, D], F32) for i in range(2)]
        tWb3 = [ar3.sbuf_tensor(f"A3_wb{i}", [128, D], BF16) for i in range(2)]
        tSc3 = [ar3.sbuf_tensor(f"A3_sc{i}", [128, TL], F32) for i in range(2)]
        tXPm = [ar3.sbuf_tensor(f"A3_xpm{i}", [128, TL], F32) for i in range(2)]
        tXm = [ar3.sbuf_tensor(f"A3_xm{i}", [128, TL], F32) for i in range(2)]
        ar3.allocate()

        bC2b = B('A_c2b')
        for k in range(8):
            bc2f = B('A_c2f')
            p.dma('sync', tC2f[:, :], i_C2[k * 128:(k + 1) * 128, :], writes=[bc2f])
            p.do('vector', lambda e, k=k: e.tensor_copy(out=tC2b[:, k * D:(k + 1) * D], in_=tC2f[:, :]),
                 reads=[bc2f], writes=[bC2b])
        bYB = B('A_yb')
        for m in range(DC):
            db = m % 2
            bXPm = B(f'A3_xpm{db}'); bSc = B(f'A3_sc{db}')
            p.dma('sync', tXPm[db][:, :], d_xp[m * 128:(m + 1) * 128, HIST:TW], reads=[B('d_xp')], writes=[bXPm])
            p.do('vector', lambda e, m=m, db=db: e.tensor_scalar(
                out=tXPm[db][:, :], in0=tXPm[db][:, :], scalar1=tDp[:, m:m + 1], scalar2=None, op0=ALU.mult),
                sreads=[B('A_dp')], writes=[bXPm])
            for nn in range(2):
                pb = pbank[nn]
                for k in range(8):
                    p.do('tensor', lambda e, k=k, m=m, nn=nn: e.matmul(
                        out=psF[:, nn * 512:(nn + 1) * 512],
                        lhsT=tC2b[:, k * D + m * 128: k * D + (m + 1) * 128],
                        rhs=tEskB[:, k * TL + nn * 512: k * TL + (nn + 1) * 512],
                        start=(k == 0), stop=(k == 7)),
                        reads=[bC2b, bEskB], writes=[pb])
                p.do('vector', lambda e, m=m, nn=nn, db=db: e.tensor_tensor(
                    out=tSc3[db][:, nn * 512:(nn + 1) * 512], in0=psF[:, nn * 512:(nn + 1) * 512],
                    in1=tXPm[db][:, nn * 512:(nn + 1) * 512], op=ALU.add),
                    reads=[pbank[nn], bXPm], writes=[bSc])
            p.do('vector', lambda e, m=m, db=db: e.tensor_copy(
                out=tYB[:, m * TL:(m + 1) * TL], in_=tSc3[db][:, :]),
                reads=[bSc], writes=[bYB])

        if 'Ataps' in dbg:
            p.dma('gpsimd', dbg_outs['o_yb'][:, :], tYB[:, :], reads=[bYB], writes=[B('o_yb')])
        for m in range(DC):
            db = m % 2
            bWf = B(f'A3_wf{db}'); bWb = B(f'A3_wb{db}'); bXm = B(f'A3_xm{db}'); bSc = B(f'A3_sc{db}')
            p.dma('sync', tWf3[db][:, :].rearrange("p (c k) -> p c k", c=DC),
                  i_pow.rearrange("(c p) mm -> p c mm", p=128)[:, :, m * 128:(m + 1) * 128],
                  writes=[bWf])
            p.do('vector', lambda e, db=db: e.tensor_copy(out=tWb3[db][:, :], in_=tWf3[db][:, :]),
                 reads=[bWf], writes=[bWb])
            p.dma('gpsimd', tXm[db][:, :], i_x[m * 128:(m + 1) * 128, HIST:TW], writes=[bXm])
            for nn in range(2):
                pb = pbank[2 + nn]
                for c in range(DC):
                    p.do('tensor', lambda e, c=c, nn=nn, db=db: e.matmul(
                        out=psF[:, 1024 + nn * 512:1024 + (nn + 1) * 512],
                        lhsT=tWb3[db][:, c * 128:(c + 1) * 128],
                        rhs=tYB[:, c * TL + nn * 512: c * TL + (nn + 1) * 512],
                        start=(c == 0), stop=(c == DC - 1)),
                        reads=[bWb, bYB], writes=[pb])
                p.do('vector', lambda e, m=m, nn=nn, db=db: e.tensor_scalar(
                    out=tSc3[db][:, nn * 512:(nn + 1) * 512], in0=psF[:, 1024 + nn * 512:1024 + (nn + 1) * 512],
                    scalar1=tPob[:, m:m + 1], scalar2=None, op0=ALU.add),
                    reads=[pb], sreads=[B('A_pob')], writes=[bSc])
                p.do('vector', lambda e, m=m, nn=nn, db=db: e.tensor_tensor(
                    out=tSc3[db][:, nn * 512:(nn + 1) * 512], in0=tSc3[db][:, nn * 512:(nn + 1) * 512],
                    in1=tXm[db][:, nn * 512:(nn + 1) * 512], op=ALU.add),
                    reads=[bXm], writes=[bSc])
                p.do('vector', lambda e, m=m, nn=nn, db=db: e.tensor_tensor(
                    out=tSc3[db][:, nn * 512:(nn + 1) * 512], in0=tSc3[db][:, nn * 512:(nn + 1) * 512],
                    in1=tLNB[:, m * TW + HIST + nn * 512: m * TW + HIST + (nn + 1) * 512], op=ALU.add),
                    reads=[bLNB], writes=[bSc])
            p.dma('sync', d_s1[m * 128:(m + 1) * 128, :], tSc3[db][:, :], reads=[bSc], writes=[B('d_s1')])

        if 'A' in dbg:
            for m in range(DC):
                p.dma('gpsimd', dbg_outs['o_s1'][m * 128:(m + 1) * 128, :], d_s1[m * 128:(m + 1) * 128, :],
                      reads=[B('d_s1')], writes=[B('o_s1')])
        p.fence()
        ar3.free()
        arL.free()

    HC = HID // 128
    if 'B' in phases:
        arBL = AutoArena(nc)
        tX1B = arBL.sbuf_tensor("B_x1b", [128, DC * TL], BF16)
        tSG = arBL.sbuf_tensor("B_sg", [128, HC], F32)
        tSU = arBL.sbuf_tensor("B_su", [128, HC], F32)
        tSD = arBL.sbuf_tensor("B_sd", [128, DC], F32)
        arBL.allocate()
        p.dma('sync', tSG[:, :].rearrange("p (c o) -> p c o", o=1), i_sg.rearrange("(c p) o -> p c o", p=128), writes=[B('B_sg')], slow=True)
        p.dma('sync', tSU[:, :].rearrange("p (c o) -> p c o", o=1), i_su.rearrange("(c p) o -> p c o", p=128), writes=[B('B_su')], slow=True)
        p.dma('sync', tSD[:, :].rearrange("p (c o) -> p c o", o=1), i_sd2.rearrange("(c p) o -> p c o", p=128), writes=[B('B_sd')], slow=True)

        arB1 = AutoArena(nc)
        tX1 = arB1.sbuf_tensor("B_x1", [128, DC * TL], F32)
        tSQb = [arB1.sbuf_tensor(f"B_sq{i}", [128, 512], F32) for i in range(2)]
        tRowb = arB1.sbuf_tensor("B_row", [1, 512 * 5], F32)
        tMBb = arB1.sbuf_tensor("B_mb", [128, 512], F32)
        tRBb = arB1.sbuf_tensor("B_rb", [128, 512], F32)
        tNrmb = arB1.sbuf_tensor("B_nrm", [128, 512], F32)
        arB1.allocate()
        p.dma('sync', tX1[:, :].rearrange("p (c t) -> p c t", c=DC),
              d_s1.rearrange("(c p) t -> p c t", p=128), reads=[B('d_s1')], writes=[B('B_x1')])
        emit_ln2(tSQb, tRowb, tMBb, tRBb, tNrmb,
                 lambda c: tX1[:, c * TL:(c + 1) * TL], TL,
                 lambda c: tX1B[:, c * TL:(c + 1) * TL], B('B_x1b'), B('B_x1'), 'Bln')
        p.fence()
        arB1.free()

        arB2 = AutoArena(nc)
        tHH = arB2.sbuf_tensor("B_hh", [128, HC * TL], BF16)
        tIW = [arB2.sbuf_tensor(f"B_iw{i}", [128, D], I32) for i in range(2)]
        tIU = [arB2.sbuf_tensor(f"B_iu{i}", [128, D], I32) for i in range(2)]
        tGW = [arB2.sbuf_tensor(f"B_gw{i}", [128, D], BF16) for i in range(2)]
        tUW = [arB2.sbuf_tensor(f"B_uw{i}", [128, D], BF16) for i in range(2)]
        tGb = arB2.sbuf_tensor("B_gb", [128, TL], BF16)
        tUb = arB2.sbuf_tensor("B_ub", [128, TL], BF16)
        arB2.allocate()
        bHH = B('B_hh')
        for hc in range(HC):
            db = hc % 2
            bIW = B(f'B_iw{db}'); bIU = B(f'B_iu{db}'); bGW = B(f'B_gw{db}'); bUW = B(f'B_uw{db}')
            p.dma('sync', tIW[db][:, :].rearrange("p (c k) -> p c k", c=DC),
                  i_wg.rearrange("(c p) mm -> p c mm", p=128)[:, :, hc * 128:(hc + 1) * 128], writes=[bIW])
            p.dma('gpsimd', tIU[db][:, :].rearrange("p (c k) -> p c k", c=DC),
                  i_wu.rearrange("(c p) mm -> p c mm", p=128)[:, :, hc * 128:(hc + 1) * 128], writes=[bIU])
            p.do('vector', lambda e, db=db: e.tensor_scalar(out=tGW[db][:, :], in0=tIW[db][:, :],
                 scalar1=-128.0, scalar2=None, op0=ALU.add), reads=[bIW], writes=[bGW])
            p.do('vector', lambda e, db=db: e.tensor_scalar(out=tUW[db][:, :], in0=tIU[db][:, :],
                 scalar1=-128.0, scalar2=None, op0=ALU.add), reads=[bIU], writes=[bUW])
            for nn in range(2):
                pbg = pbank[nn]; pbu = pbank[2 + nn]
                for c in range(DC):
                    p.do('tensor', lambda e, c=c, nn=nn, db=db: e.matmul(
                        out=psF[:, nn * 512:(nn + 1) * 512],
                        lhsT=tGW[db][:, c * 128:(c + 1) * 128],
                        rhs=tX1B[:, c * TL + nn * 512: c * TL + (nn + 1) * 512],
                        start=(c == 0), stop=(c == DC - 1)),
                        reads=[bGW, B('B_x1b')], writes=[pbg])
                    p.do('tensor', lambda e, c=c, nn=nn, db=db: e.matmul(
                        out=psF[:, 1024 + nn * 512:1024 + (nn + 1) * 512],
                        lhsT=tUW[db][:, c * 128:(c + 1) * 128],
                        rhs=tX1B[:, c * TL + nn * 512: c * TL + (nn + 1) * 512],
                        start=(c == 0), stop=(c == DC - 1)),
                        reads=[bUW, B('B_x1b')], writes=[pbu])
                p.do('scalar', lambda e, hc=hc, nn=nn: e.activation(
                    out=tGb[:, nn * 512:(nn + 1) * 512], in_=psF[:, nn * 512:(nn + 1) * 512],
                    func=ACT.Silu, scale=tSG[:, hc:hc + 1]),
                    reads=[pbg], sreads=[B('B_sg')], writes=[B('B_gb')])
                p.do('scalar', lambda e, hc=hc, nn=nn: e.activation(
                    out=tUb[:, nn * 512:(nn + 1) * 512], in_=psF[:, 1024 + nn * 512:1024 + (nn + 1) * 512],
                    func=ACT.Copy, scale=tSU[:, hc:hc + 1]),
                    reads=[pbu], sreads=[B('B_su')], writes=[B('B_ub')])
                p.do('vector', lambda e, hc=hc, nn=nn: e.tensor_tensor(
                    out=tHH[:, hc * TL + nn * 512: hc * TL + (nn + 1) * 512],
                    in0=tGb[:, nn * 512:(nn + 1) * 512], in1=tUb[:, nn * 512:(nn + 1) * 512], op=ALU.mult),
                    reads=[B('B_gb'), B('B_ub')], writes=[bHH])
        p.fence()
        arB2_weights_done = True
        arB3 = AutoArena(nc)
        tID = [arB3.sbuf_tensor(f"B_id{i}", [128, 256], I32) for i in range(2)]
        tDW = [arB3.sbuf_tensor(f"B_dw{i}", [128, 256], BF16) for i in range(2)]
        tRes = [arB3.sbuf_tensor(f"B_res{i}", [128, TL], F32) for i in range(2)]
        tSc2 = [arB3.sbuf_tensor(f"B_sc{i}", [128, TL], F32) for i in range(2)]
        arB3.allocate()
        for mg in range(8):
            for k in range(HC):
                db = k % 2
                bID = B(f'B_id{db}'); bDW = B(f'B_dw{db}')
                p.dma('sync', tID[db][:, :], i_wd[k * 128:(k + 1) * 128, mg * 256:(mg + 1) * 256], writes=[bID])
                p.do('vector', lambda e, db=db: e.tensor_scalar(out=tDW[db][:, :], in0=tID[db][:, :],
                     scalar1=-128.0, scalar2=None, op0=ALU.add), reads=[bID], writes=[bDW])
                for mm in range(2):
                    for nn in range(2):
                        pb = pbank[mm * 2 + nn]
                        p.do('tensor', lambda e, k=k, mm=mm, nn=nn, db=db: e.matmul(
                            out=psF[:, (mm * 2 + nn) * 512:(mm * 2 + nn + 1) * 512],
                            lhsT=tDW[db][:, mm * 128:(mm + 1) * 128],
                            rhs=tHH[:, k * TL + nn * 512: k * TL + (nn + 1) * 512],
                            start=(k == 0), stop=(k == HC - 1)),
                            reads=[bDW, bHH], writes=[pb])
            for mm in range(2):
                m = mg * 2 + mm
                db = mm % 2
                bRes = B(f'B_res{db}'); bSc = B(f'B_sc{db}')
                p.dma('gpsimd', tRes[db][:, :], d_s1[m * 128:(m + 1) * 128, :], reads=[B('d_s1')], writes=[bRes])
                for nn in range(2):
                    pb = pbank[mm * 2 + nn]
                    p.do('vector', lambda e, m=m, mm=mm, nn=nn, db=db: e.tensor_scalar(
                        out=tSc2[db][:, nn * 512:(nn + 1) * 512],
                        in0=psF[:, (mm * 2 + nn) * 512:(mm * 2 + nn + 1) * 512],
                        scalar1=tSD[:, m:m + 1], scalar2=None, op0=ALU.mult),
                        reads=[pb], sreads=[B('B_sd')], writes=[bSc])
                    p.do('vector', lambda e, nn=nn, db=db: e.tensor_tensor(
                        out=tSc2[db][:, nn * 512:(nn + 1) * 512], in0=tSc2[db][:, nn * 512:(nn + 1) * 512],
                        in1=tRes[db][:, nn * 512:(nn + 1) * 512], op=ALU.add),
                        reads=[bRes], writes=[bSc])
                p.dma('sync', d_s2[m * 128:(m + 1) * 128, :], tSc2[db][:, :], reads=[bSc], writes=[B('d_s2')])
        if 'B' in dbg:
            for m in range(DC):
                p.dma('gpsimd', dbg_outs['o_s2'][m * 128:(m + 1) * 128, :], d_s2[m * 128:(m + 1) * 128, :],
                      reads=[B('d_s2')], writes=[B('o_s2')])
        p.fence()
        arB3.free()
        arB2.free()
        arBL.free()

    if 'C' in phases:
        arCL = AutoArena(nc)
        tX4B = arCL.sbuf_tensor("C_x4b", [128, DC * TL], BF16)
        tCos = arCL.sbuf_tensor("C_cos", [128, TL], F32)
        tSin = arCL.sbuf_tensor("C_sin", [128, TL], F32)
        tKM = arCL.sbuf_tensor("C_km", [128, NH], F32)
        tNegM = arCL.sbuf_tensor("C_negm", [128, NH], F32)
        tRowC = arCL.sbuf_tensor("C_rowc", [1, 64], F32)
        arCL.allocate()
        p.dma('sync', tCos[:, :], i_cos[:, :], writes=[B('C_cos')])
        p.dma('sync', tSin[:, :], i_sin[:, :], writes=[B('C_sin')])

        arCx = AutoArena(nc)
        tX2LN = arCx.sbuf_tensor("C_x2ln", [128, DC * TL], F32)
        arCx.allocate()

        arC1 = AutoArena(nc)
        tX2 = arC1.sbuf_tensor("C_x2", [128, DC * TL], F32)
        tSQc = [arC1.sbuf_tensor(f"C_sq{i}", [128, 512], F32) for i in range(2)]
        tRowc = arC1.sbuf_tensor("C_row", [1, 512 * 5], F32)
        tMBc = arC1.sbuf_tensor("C_mb", [128, 512], F32)
        tRBc = arC1.sbuf_tensor("C_rb", [128, 512], F32)
        tNrmc = arC1.sbuf_tensor("C_nrm", [128, 512], F32)
        arC1.allocate()
        p.dma('sync', tX2[:, :].rearrange("p (c t) -> p c t", c=DC),
              d_s2.rearrange("(c p) t -> p c t", p=128), reads=[B('d_s2')], writes=[B('C_x2')])
        emit_ln2(tSQc, tRowc, tMBc, tRBc, tNrmc,
                 lambda c: tX2[:, c * TL:(c + 1) * TL], TL,
                 lambda c: tX2LN[:, c * TL:(c + 1) * TL], B('C_x2ln'), B('C_x2'), 'Cln')
        p.fence()
        arC1.free()

        arC1b = AutoArena(nc)
        tCM = arC1b.sbuf_tensor("C_cm", [128, TL], F32)
        tCMT = arC1b.sbuf_tensor("C_cmt", [128, 8 * 128], F32)
        tRM = arC1b.sbuf_tensor("C_rm", [128, 16], F32)
        tSrow = arC1b.sbuf_tensor("C_srow", [1, 2 * TL], F32)
        tSB4 = arC1b.sbuf_tensor("C_sb4", [128, TL], F32)
        tRSB4 = arC1b.sbuf_tensor("C_rsb4", [128, TL], F32)
        tQT = arC1b.sbuf_tensor("C_qt", [128, TL], F32)
        tQI = arC1b.sbuf_tensor("C_qi", [128, TL], I32)
        arC1b.allocate()
        bCM = B('C_cm'); bCMT = B('C_cmt'); bRM = B('C_rm'); bSrow = B('C_srow')
        bSB4 = B('C_sb4'); bRSB4 = B('C_rsb4'); bQT = B('C_qt'); bQI = B('C_qi')
        bX2LN = B('C_x2ln')
        p.do('scalar', lambda e: e.activation(out=tCM[:, :], in_=tX2LN[:, 0:TL], func=ACT.Abs),
             reads=[bX2LN], writes=[bCM])
        for c in range(1, DC):
            p.do('scalar', lambda e, c=c: e.activation(out=tQT[:, :], in_=tX2LN[:, c * TL:(c + 1) * TL], func=ACT.Abs),
                 reads=[bX2LN], writes=[bQT])
            p.do('vector', lambda e, c=c: e.tensor_tensor(out=tCM[:, :], in0=tQT[:, :], in1=tCM[:, :], op=ALU.max),
                 reads=[bQT], writes=[bCM])
        for tt in range(8):
            p.do('tensor', lambda e, tt=tt: e.transpose(
                out=psF[:, 3072:3072 + 128], in_=tCM[:, tt * 128:(tt + 1) * 128], identity=t_idf[:, :]),
                reads=[bCM, B('idf')], writes=[pbank[6]])
            p.do('vector', lambda e, tt=tt: e.tensor_copy(out=tCMT[:, tt * 128:(tt + 1) * 128], in_=psF[:, 3072:3072 + 128]),
                 reads=[pbank[6]], writes=[bCMT])
        p.do('vector', lambda e: e.tensor_reduce(out=tRM[:, 0:8], in_=tCMT[:, :].rearrange("p (t k) -> p t k", t=8),
             axis=AX.X, op=ALU.max), reads=[bCMT], writes=[bRM])
        p.pad('v', 2)
        p.do('vector', lambda e: e.tensor_scalar(out=tRM[:, 8:16], in0=tRM[:, 0:8], scalar1=EPS_LN, scalar2=None, op0=ALU.max),
             reads=[bRM], writes=[bRM])
        p.pad('v', 2)
        p.do('vector', lambda e: e.reciprocal(out=tRM[:, 0:8], in_=tRM[:, 8:16]), reads=[bRM], writes=[bRM])
        p.pad('v', 2)
        p.do('vector', lambda e: e.tensor_scalar(out=tRM[:, 8:16], in0=tRM[:, 0:8], scalar1=7.0, scalar2=None, op0=ALU.mult),
             reads=[bRM], writes=[bRM])
        p.pad('v', 2)
        p.do('vector', lambda e: e.reciprocal(out=tRM[:, 0:8], in_=tRM[:, 8:16]), reads=[bRM], writes=[bRM])
        p.pad('v', 2)
        p.dma('gpsimd', d_rm[:, :], tRM[:, :], reads=[bRM], writes=[B('d_rm')])
        p.dma('gpsimd', tSrow[0:1, :].rearrange("o (x t q) -> o x t q", x=2, t=8),
              d_rm.rearrange("(o q) (x t) -> o x t q", o=1, x=2), reads=[B('d_rm')], writes=[bSrow], slow=True)
        for (dst, bdst, off) in ((tSB4, bSB4, TL), (tRSB4, bRSB4, 0)):
            for g in range(2):
                p.do('tensor', lambda e, off=off, g=g: e.matmul(
                    out=psF[:, g * 512:(g + 1) * 512], lhsT=t_ones[0:1, 0:128],
                    rhs=tSrow[0:1, off + g * 512: off + (g + 1) * 512], start=True, stop=True),
                    reads=[bSrow, B('ones')], writes=[pbank[g]])
                p.do('vector', lambda e, dst=dst, g=g: e.tensor_copy(out=dst[:, g * 512:(g + 1) * 512], in_=psF[:, g * 512:(g + 1) * 512]),
                     reads=[pbank[g]], writes=[bdst])
        bX4B = B('C_x4b')
        for c in range(DC):
            p.do('vector', lambda e, c=c: e.tensor_tensor(out=tQT[:, :], in0=tX2LN[:, c * TL:(c + 1) * TL], in1=tSB4[:, :], op=ALU.mult),
                 reads=[bX2LN, bSB4], writes=[bQT])
            p.do('vector', lambda e: e.tensor_copy(out=tQI[:, :], in_=tQT[:, :]), reads=[bQT], writes=[bQI])
            p.do('vector', lambda e: e.tensor_copy(out=tQT[:, :], in_=tQI[:, :]), reads=[bQI], writes=[bQT])
            p.do('vector', lambda e, c=c: e.tensor_tensor(out=tX4B[:, c * TL:(c + 1) * TL], in0=tQT[:, :], in1=tRSB4[:, :], op=ALU.mult),
                 reads=[bQT, bRSB4], writes=[bX4B])
        if 'Ctaps' in dbg:
            p.dma('gpsimd', dbg_outs['o_x2ln'][:, :], tX2LN[:, :], reads=[B('C_x2ln')], writes=[B('o_x2ln')])
            p.dma('gpsimd', dbg_outs['o_cm'][:, :], tCM[:, :], reads=[bCM], writes=[B('o_cm')])
            p.dma('gpsimd', dbg_outs['o_rm'][:, :], tRM[:, :], reads=[bRM], writes=[B('o_rm')])
            p.dma('gpsimd', dbg_outs['o_srow'][:, :], tSrow[:, :], reads=[bSrow], writes=[B('o_srow')])
            p.dma('gpsimd', dbg_outs['o_sb4'][:, :], tSB4[:, :], reads=[bSB4], writes=[B('o_sb4')])
            p.dma('gpsimd', dbg_outs['o_rsb4'][:, :], tRSB4[:, :], reads=[bRSB4], writes=[B('o_rsb4')])
        p.fence()
        arC1b.free()
        arCx.free()
        arCL2 = AutoArena(nc)
        tAO = arCL2.sbuf_tensor("C_ao", [128, 8 * D], BF16)
        arCL2.allocate()
        arC2 = AutoArena(nc)
        tQB = arC2.sbuf_tensor("C_qb", [128, NH * TL], BF16)
        tWfc = [arC2.sbuf_tensor(f"C_wf{i}", [128, D], F32) for i in range(2)]
        tWbc = [arC2.sbuf_tensor(f"C_wb{i}", [128, D], BF16) for i in range(2)]
        tKf = arC2.sbuf_tensor("C_kf", [128, TL], F32)
        tRot = arC2.sbuf_tensor("C_rot", [128, TL], F32)
        tT1 = arC2.sbuf_tensor("C_t1", [128, TL], F32)
        tT2 = arC2.sbuf_tensor("C_t2", [128, TL], F32)
        tKb = arC2.sbuf_tensor("C_kb", [128, TL], BF16)
        tVb = arC2.sbuf_tensor("C_vb", [128, TL], BF16)
        tVt = arC2.sbuf_tensor("C_vt", [128, 128], BF16)
        tAbsC = arC2.sbuf_tensor("C_absc", [128, D], F32)
        tWiC = arC2.sbuf_tensor("C_wic", [128, D], I32)
        tBR3 = arC2.sbuf_tensor("C_br3", [128, 8], F32)
        arC2.allocate()
        bQB = B('C_qb'); bKf = B('C_kf'); bRot = B('C_rot'); bT1 = B('C_t1'); bT2 = B('C_t2')
        bKb = B('C_kb'); bVb = B('C_vb'); bVt = B('C_vt'); bKM = B('C_km')
        bAbsC = B('C_absc'); bWiC = B('C_wic'); bBR3 = B('C_br3'); bRowCm = B('C_rowc')
        ISQ = 0.08838834764831845

        def emit_tern_scale(iw, wfl, bwfl, brcol):
            for m in range(DC):
                db = m % 2
                bWf = bwfl[db]
                p.dma('sync', wfl[db][:, :].rearrange("p (c k) -> p c k", c=DC),
                      iw.rearrange("(c p) mm -> p c mm", p=128)[:, :, m * 128:(m + 1) * 128], writes=[bWf])
                p.do('scalar', lambda e, db=db: e.activation(out=tAbsC[:, :], in_=wfl[db][:, :], func=ACT.Abs),
                     reads=[bWf], writes=[bAbsC])
                for g in range(4):
                    p.do('tensor', lambda e, m=m, g=g: e.matmul(
                        out=psF[0:1, 3072:3072 + 512], lhsT=t_ones[:, 0:1], rhs=tAbsC[:, g * 512:(g + 1) * 512],
                        start=(m == 0 and g == 0), stop=(m == DC - 1 and g == 3)),
                        reads=[bAbsC, B('ones')], writes=[pbank[6]])
            p.do('vector', lambda e: e.tensor_reduce(out=tRowC[0:1, 40:41], in_=psF[0:1, 3072:3072 + 512],
                 axis=AX.X, op=ALU.add), reads=[pbank[6]], writes=[bRowCm])
            p.pad('v', 2)
            p.do('vector', lambda e: e.tensor_scalar(out=tRowC[0:1, 41:42], in0=tRowC[0:1, 40:41],
                 scalar1=1.0 / (D * D), scalar2=None, op0=ALU.mult), reads=[bRowCm], writes=[bRowCm])
            p.pad('v', 2)
            p.do('vector', lambda e: e.tensor_scalar(out=tRowC[0:1, 42:43], in0=tRowC[0:1, 41:42],
                 scalar1=EPS_LN, scalar2=None, op0=ALU.add), reads=[bRowCm], writes=[bRowCm])
            p.pad('v', 2)
            p.do('vector', lambda e: e.reciprocal(out=tRowC[0:1, 43:44], in_=tRowC[0:1, 42:43]),
                 reads=[bRowCm], writes=[bRowCm])
            p.pad('v', 2)
            p.do('tensor', lambda e: e.matmul(out=psF[:, 3072:3072 + 1], lhsT=t_ones[0:1, 0:128],
                 rhs=tRowC[0:1, 41:42], start=True, stop=True), reads=[bRowCm, B('ones')], writes=[pbank[6]])
            p.do('tensor', lambda e: e.matmul(out=psF[:, 3073:3073 + 1], lhsT=t_ones[0:1, 0:128],
                 rhs=tRowC[0:1, 43:44], start=True, stop=True), reads=[bRowCm, B('ones')], writes=[pbank[6]])
            p.do('vector', lambda e, brcol=brcol: e.tensor_copy(out=tBR3[:, brcol:brcol + 2], in_=psF[:, 3072:3074]),
                 reads=[pbank[6]], writes=[bBR3])
            p.pad('v', 2)

        def rope_into(dst_ap, bdst, scale):
            p.do('vector', lambda e: e.tensor_scalar(out=tRot[0:64, :], in0=tKf[64:128, :], scalar1=-1.0, scalar2=None, op0=ALU.mult),
                 reads=[bKf], writes=[bRot])
            p.do('vector', lambda e: e.tensor_scalar(out=tRot[64:128, :], in0=tKf[0:64, :], scalar1=1.0, scalar2=None, op0=ALU.mult),
                 reads=[bKf], writes=[bRot])
            p.do('vector', lambda e: e.tensor_tensor(out=tT1[:, :], in0=tKf[:, :], in1=tCos[:, :], op=ALU.mult),
                 reads=[bKf, B('C_cos')], writes=[bT1])
            p.do('vector', lambda e: e.tensor_tensor(out=tT2[:, :], in0=tRot[:, :], in1=tSin[:, :], op=ALU.mult),
                 reads=[bRot, B('C_sin')], writes=[bT2])
            p.do('vector', lambda e, dst_ap=dst_ap: e.tensor_tensor(out=dst_ap, in0=tT1[:, :], in1=tT2[:, :], op=ALU.add),
                 reads=[bT1, bT2], writes=[bdst])

        emit_tern_scale(i_wq, tWfc, [B('C_wf0'), B('C_wf1')], 0)
        emit_tern_scale(i_wk, tWfc, [B('C_wf0'), B('C_wf1')], 2)
        emit_tern_scale(i_wv, tWfc, [B('C_wf0'), B('C_wf1')], 4)
        for pj, (proj, iw) in enumerate((('q', i_wq), ('k', i_wk), ('v', i_wv))):
            for m in range(NH):
                db = m % 2
                bWf = B(f'C_wf{db}'); bWb = B(f'C_wb{db}')
                p.dma('sync', tWfc[db][:, :].rearrange("p (c k) -> p c k", c=DC),
                      iw.rearrange("(c p) mm -> p c mm", p=128)[:, :, m * 128:(m + 1) * 128], writes=[bWf])
                p.do('vector', lambda e, db=db, brcol=2 * pj: e.tensor_scalar(out=tAbsC[:, :], in0=tWfc[db][:, :],
                     scalar1=tBR3[:, brcol + 1:brcol + 2], scalar2=None, op0=ALU.mult),
                     reads=[bWf], sreads=[bBR3], writes=[bAbsC])
                p.do('vector', lambda e: e.tensor_copy(out=tWiC[:, :], in_=tAbsC[:, :]), reads=[bAbsC], writes=[bWiC])
                p.do('vector', lambda e: e.tensor_copy(out=tAbsC[:, :], in_=tWiC[:, :]), reads=[bWiC], writes=[bAbsC])
                p.do('vector', lambda e: e.tensor_scalar(out=tAbsC[:, :], in0=tAbsC[:, :],
                     scalar1=1.0, scalar2=-1.0, op0=ALU.min, op1=ALU.max), writes=[bAbsC])
                p.do('vector', lambda e, db=db, brcol=2 * pj: e.tensor_scalar(out=tWbc[db][:, :], in0=tAbsC[:, :],
                     scalar1=tBR3[:, brcol:brcol + 1], scalar2=None, op0=ALU.mult),
                     reads=[bAbsC], sreads=[bBR3], writes=[bWb])
                for nn in range(2):
                    pb = pbank[4 + nn]
                    for c in range(DC):
                        p.do('tensor', lambda e, c=c, nn=nn, db=db: e.matmul(
                            out=psF[:, 2048 + nn * 512:2048 + (nn + 1) * 512],
                            lhsT=tWbc[db][:, c * 128:(c + 1) * 128],
                            rhs=tX4B[:, c * TL + nn * 512: c * TL + (nn + 1) * 512],
                            start=(c == 0), stop=(c == DC - 1)),
                            reads=[bWb, bX4B], writes=[pb])
                if proj == 'v':
                    for nn in range(2):
                        p.do('vector', lambda e, nn=nn: e.tensor_copy(
                            out=tVb[:, nn * 512:(nn + 1) * 512], in_=psF[:, 2048 + nn * 512:2048 + (nn + 1) * 512]),
                            reads=[pbank[4 + nn]], writes=[bVb])
                    for tt in range(8):
                        p.do('tensor', lambda e, tt=tt: e.transpose(
                            out=psB[:, 0:128], in_=tVb[:, tt * 128:(tt + 1) * 128], identity=t_idb[:, :]),
                            reads=[bVb, B('idb')], writes=[pbB])
                        p.do('vector', lambda e: e.tensor_copy(out=tVt[:, :], in_=psB[:, 0:128]),
                             reads=[pbB], writes=[bVt])
                        p.dma('sync', d_vin[tt * 128:(tt + 1) * 128, m * 128:(m + 1) * 128], tVt[:, :],
                              reads=[bVt], writes=[B('d_vin')])
                else:
                    sc = ISQ if proj == 'k' else 1.0
                    for nn in range(2):
                        p.do('scalar', lambda e, nn=nn, sc=sc: e.activation(
                            out=tKf[:, nn * 512:(nn + 1) * 512], in_=psF[:, 2048 + nn * 512:2048 + (nn + 1) * 512],
                            func=ACT.Copy, scale=sc),
                            reads=[pbank[4 + nn]], writes=[bKf])
                    if proj == 'q':
                        rope_into(tQB[:, m * TL:(m + 1) * TL], bQB, 1.0)
                    else:
                        rope_into(tKb[:, :], bKb, 1.0)
                        p.dma('sync', d_kin[m * 128:(m + 1) * 128, :], tKb[:, :], reads=[bKb], writes=[B('d_kin')])
                        p.do('vector', lambda e, m=m: e.tensor_reduce(
                            out=tKM[:, m:m + 1], in_=tKb[:, :], axis=AX.X, op=ALU.max, apply_absolute_value=True),
                            reads=[bKb], writes=[bKM])
        p.dma('gpsimd', d_kmi.rearrange("(c p) o -> p c o", p=128), tKM[:, :].rearrange("p (c o) -> p c o", o=1),
              reads=[bKM], writes=[B('d_kmi')], slow=True)
        p.fence()

        p.do('gpsimd', lambda e: e.collective_compute(
            "AllGather", ALU.bypass, replica_groups=[list(range(NCORE))],
            ins=[d_kin.tensor.ap().opt() if hasattr(d_kin, 'tensor') else d_kin.opt()],
            outs=[d_kag.opt()]), reads=[B('d_kin')], writes=[B('d_kag')])
        p.do('gpsimd', lambda e: e.collective_compute(
            "AllGather", ALU.bypass, replica_groups=[list(range(NCORE))],
            ins=[d_vin.opt()], outs=[d_vag.opt()]), reads=[B('d_vin')], writes=[B('d_vag')])
        p.do('gpsimd', lambda e: e.collective_compute(
            "AllReduce", ALU.max, replica_groups=[list(range(NCORE))],
            ins=[d_kmi.opt()], outs=[d_kmo.opt()]), reads=[B('d_kmi')], writes=[B('d_kmo')])

        p.dma('sync', tKM[:, :].rearrange("p (c o) -> p c o", o=1), d_kmo.rearrange("(c p) o -> p c o", p=128),
              reads=[B('d_kmo')], writes=[bKM], slow=True)
        bNegM = B('C_negm'); bRowC = B('C_rowc')
        for m in range(NH):
            p.do('scalar', lambda e, m=m: e.activation(out=tKb[:, :], in_=tQB[:, m * TL:(m + 1) * TL],
                 func=ACT.Abs), reads=[bQB], writes=[bKb])
            p.do('vector', lambda e, m=m: e.tensor_copy(out=tVb[:, 0:1], in_=tKM[:, m:m + 1]),
                 reads=[bKM], writes=[bVb])
            p.do('tensor', lambda e, m=m: e.matmul(
                out=psF[0:1, 3072:3072 + 512], lhsT=tVb[:, 0:1], rhs=tKb[:, 0:512], start=True, stop=False),
                reads=[bVb, bKb], writes=[pbank[6]])
            p.do('tensor', lambda e, m=m: e.matmul(
                out=psF[0:1, 3072:3072 + 512], lhsT=tVb[:, 0:1], rhs=tKb[:, 512:1024], start=False, stop=True),
                reads=[bVb, bKb], writes=[pbank[6]])
            p.do('vector', lambda e, m=m: e.tensor_reduce(out=tRowC[0:1, m:m + 1], in_=psF[0:1, 3072:3072 + 512],
                 axis=AX.X, op=ALU.max), reads=[pbank[6]], writes=[bRowC])
        p.pad('v', 2)
        p.do('vector', lambda e: e.tensor_scalar(out=tRowC[0:1, 16:32], in0=tRowC[0:1, 0:16],
             scalar1=-1.0, scalar2=None, op0=ALU.mult), reads=[bRowC], writes=[bRowC])
        p.pad('v', 2)
        p.do('tensor', lambda e: e.matmul(out=psF[:, 3072:3072 + 16], lhsT=t_ones[0:1, 0:128],
             rhs=tRowC[0:1, 16:32], start=True, stop=True), reads=[bRowC, B('ones')], writes=[pbank[6]])
        p.do('vector', lambda e: e.tensor_copy(out=tNegM[:, :], in_=psF[:, 3072:3072 + 16]),
             reads=[pbank[6]], writes=[bNegM])
        p.fence()
        arC3 = AutoArena(nc)
        tKT = [arC3.sbuf_tensor(f"C_kt{i}", [128, 128], BF16) for i in range(2)]
        tVT = [arC3.sbuf_tensor(f"C_vt2{i}", [128, 129], BF16) for i in range(2)]
        tEX = [arC3.sbuf_tensor(f"C_ex{i}", [128, 512], BF16) for i in range(2)]
        tRZ = arC3.sbuf_tensor("C_rz", [128, 4], F32)
        arC3.allocate()
        p.do('gpsimd', lambda e: e.memset(tVT[0][:, 128:129], 1.0), writes=[B('C_vt20')])
        p.do('gpsimd', lambda e: e.memset(tVT[1][:, 128:129], 1.0), writes=[B('C_vt21')])
        bAO = B('C_ao'); bRZ = B('C_rz')
        for h in range(NH):
            for qsg in range(2):
                for kc in range(64):
                    db = kc % 2
                    bKT = B(f'C_kt{db}'); bVT = B(f'C_vt2{db}')
                    blk = kc // 8; lc = kc % 8
                    p.dma('sync', tKT[db][:, :],
                          d_kag[(blk * D + h * 128):(blk * D + (h + 1) * 128), lc * 128:(lc + 1) * 128],
                          reads=[B('d_kag')], writes=[bKT])
                    p.dma('gpsimd', tVT[db][:, 0:128],
                          d_vag[kc * 128:(kc + 1) * 128, h * 128:(h + 1) * 128],
                          reads=[B('d_vag')], writes=[bVT])
                    psc = pbank[4 + db]
                    p.do('tensor', lambda e, h=h, qsg=qsg, db=db: e.matmul(
                        out=psF[:, 2048 + db * 512:2048 + (db + 1) * 512],
                        lhsT=tKT[db][:, :], rhs=tQB[:, h * TL + qsg * 512: h * TL + (qsg + 1) * 512],
                        start=True, stop=True),
                        reads=[bKT, bQB], writes=[psc])
                    p.do('scalar', lambda e, h=h, db=db: e.activation(
                        out=tEX[db][:, :], in_=psF[:, 2048 + db * 512:2048 + (db + 1) * 512],
                        func=ACT.Exp, bias=tNegM[:, h:h + 1], scale=1.0),
                        reads=[psc], sreads=[bNegM], writes=[B(f'C_ex{db}')])
                    for qt in range(4):
                        p.do('tensor', lambda e, qt=qt, kc=kc, db=db: e.matmul(
                            out=psF[:, qt * 512:qt * 512 + 129],
                            lhsT=tEX[db][:, qt * 128:(qt + 1) * 128], rhs=tVT[db][:, :],
                            start=(kc == 0), stop=(kc == 63)),
                            reads=[B(f'C_ex{db}'), bVT], writes=[pbank[qt]])
                for qt in range(4):
                    p.do('vector', lambda e, qt=qt: e.reciprocal(out=tRZ[:, qt:qt + 1], in_=psF[:, qt * 512 + 128:qt * 512 + 129]),
                         reads=[pbank[qt]], writes=[bRZ])
                p.pad('v', 4)
                for qt in range(4):
                    tt = qsg * 4 + qt
                    p.do('scalar', lambda e, h=h, tt=tt, qt=qt: e.activation(
                        out=tAO[:, tt * D + h * 128: tt * D + (h + 1) * 128],
                        in_=psF[:, qt * 512:qt * 512 + 128], func=ACT.Copy, scale=tRZ[:, qt:qt + 1]),
                        reads=[pbank[qt]], sreads=[bRZ], writes=[bAO])
        if 'Ctaps' in dbg:
            p.dma('gpsimd', dbg_outs['o_ao'][:, :], tAO[:, :], reads=[bAO], writes=[B('o_ao')])
            p.dma('gpsimd', dbg_outs['o_negm'][:, :], tNegM[:, :], reads=[bNegM], writes=[B('o_negm')])
            p.dma('gpsimd', dbg_outs['o_qb'][:, :], tQB[:, :], reads=[bQB], writes=[B('o_qb')])
        p.fence()
        arC3.free()
        arC2.free()

        arCL3 = AutoArena(nc)
        tX8F = arCL3.sbuf_tensor("C_x8f", [128, DC * TL], BF16)
        arCL3.allocate()
        arC4 = AutoArena(nc)
        tAOa = arC4.sbuf_tensor("C_aoa", [128, 8 * D], BF16)
        tBS = arC4.sbuf_tensor("C_bs", [128, 64], F32)
        tJunk = arC4.sbuf_tensor("C_junk", [128, D], BF16)
        tQTo = arC4.sbuf_tensor("C_qto", [128, D], F32)
        tQIo = arC4.sbuf_tensor("C_qio", [128, D], I32)
        tMask = arC4.sbuf_tensor("C_mask", [128, D], F32)
        arC4.allocate()
        bAOa = B('C_aoa'); bBS = B('C_bs'); bJunk = B('C_junk')
        bQTo = B('C_qto'); bQIo = B('C_qio'); bMask = B('C_mask'); bX8F = B('C_x8f')
        p.do('scalar', lambda e: e.activation(out=tAOa[:, :], in_=tAO[:, :], func=ACT.Abs),
             reads=[bAO], writes=[bAOa])
        p.do('vector', lambda e: e.tensor_reduce(out=tBS[:, 56:64], in_=tAOa[:, :].rearrange("p (t k) -> p t k", t=8),
             axis=AX.X, op=ALU.max), reads=[bAOa], writes=[bBS])
        p.do('gpsimd', lambda e: e.memset(tBS[:, 0:8], 0.0), writes=[bBS])
        p.pad('v', 2)
        p.do('vector', lambda e: e.tensor_scalar(out=tBS[:, 8:16], in0=tBS[:, 56:64], scalar1=1.0000001, scalar2=None, op0=ALU.mult),
             reads=[bBS], writes=[bBS])
        p.do('vector', lambda e: e.tensor_scalar(out=tBS[:, 40:48], in0=tBS[:, 56:64], scalar1=EPS_LN, scalar2=None, op0=ALU.max),
             reads=[bBS], writes=[bBS])
        p.pad('v', 2)
        p.do('vector', lambda e: e.reciprocal(out=tBS[:, 48:56], in_=tBS[:, 40:48]), reads=[bBS], writes=[bBS])
        p.pad('v', 2)
        p.do('vector', lambda e: e.tensor_scalar(out=tBS[:, 40:48], in0=tBS[:, 48:56], scalar1=127.0, scalar2=None, op0=ALU.mult),
             reads=[bBS], writes=[bBS])
        p.pad('v', 2)
        p.do('vector', lambda e: e.reciprocal(out=tBS[:, 48:56], in_=tBS[:, 40:48]), reads=[bBS], writes=[bBS])
        NBIS = 26
        KTGT = 1126.0
        for it in range(NBIS):
            p.do('vector', lambda e: e.tensor_tensor(out=tBS[:, 16:24], in0=tBS[:, 0:8], in1=tBS[:, 8:16], op=ALU.add),
                 reads=[bBS], writes=[bBS])
            p.pad('v', 2)
            p.do('vector', lambda e: e.tensor_scalar(out=tBS[:, 16:24], in0=tBS[:, 16:24], scalar1=0.5, scalar2=None, op0=ALU.mult),
                 reads=[bBS], writes=[bBS])
            p.pad('v', 6)
            for tt in range(8):
                p.do('vector', lambda e, tt=tt: e.tensor_scalar(
                    out=tJunk[:, :], in0=tAOa[:, tt * D:(tt + 1) * D],
                    scalar1=tBS[:, 16 + tt:17 + tt], scalar2=None, op0=ALU.is_ge, op1=ALU.add,
                    accum_out=tBS[:, 24 + tt:25 + tt]),
                    reads=[bAOa], sreads=[bBS], writes=[bJunk, bBS])
            p.pad('v', 2)
            p.do('vector', lambda e: e.tensor_scalar(out=tBS[:, 32:40], in0=tBS[:, 24:32], scalar1=KTGT, scalar2=None, op0=ALU.is_ge),
                 reads=[bBS], writes=[bBS])
            p.pad('v', 2)
            p.do('vector', lambda e: e.tensor_tensor(out=tBS[:, 24:32], in0=tBS[:, 16:24], in1=tBS[:, 0:8], op=ALU.subtract),
                 reads=[bBS], writes=[bBS])
            p.pad('v', 2)
            p.do('vector', lambda e: e.tensor_tensor(out=tBS[:, 24:32], in0=tBS[:, 24:32], in1=tBS[:, 32:40], op=ALU.mult),
                 reads=[bBS], writes=[bBS])
            p.pad('v', 2)
            p.do('vector', lambda e: e.tensor_tensor(out=tBS[:, 0:8], in0=tBS[:, 0:8], in1=tBS[:, 24:32], op=ALU.add),
                 reads=[bBS], writes=[bBS])
            p.pad('v', 2)
            p.do('vector', lambda e: e.tensor_tensor(out=tBS[:, 24:32], in0=tBS[:, 8:16], in1=tBS[:, 16:24], op=ALU.subtract),
                 reads=[bBS], writes=[bBS])
            p.pad('v', 2)
            p.do('vector', lambda e: e.tensor_tensor(out=tBS[:, 24:32], in0=tBS[:, 24:32], in1=tBS[:, 32:40], op=ALU.mult),
                 reads=[bBS], writes=[bBS])
            p.pad('v', 2)
            p.do('vector', lambda e: e.tensor_tensor(out=tBS[:, 8:16], in0=tBS[:, 16:24], in1=tBS[:, 24:32], op=ALU.add),
                 reads=[bBS], writes=[bBS])
            p.pad('v', 2)
        p.pad('v', 6)
        for tt in range(8):
            p.do('vector', lambda e, tt=tt: e.tensor_scalar(out=tMask[:, :], in0=tAOa[:, tt * D:(tt + 1) * D],
                 scalar1=tBS[:, 0 + tt:1 + tt], scalar2=None, op0=ALU.is_ge), reads=[bAOa], sreads=[bBS], writes=[bMask])
            p.do('vector', lambda e, tt=tt: e.tensor_scalar(out=tQTo[:, :], in0=tAO[:, tt * D:(tt + 1) * D],
                 scalar1=tBS[:, 40 + tt:41 + tt], scalar2=None, op0=ALU.mult), reads=[bAO], sreads=[bBS], writes=[bQTo])
            p.do('vector', lambda e: e.tensor_copy(out=tQIo[:, :], in_=tQTo[:, :]), reads=[bQTo], writes=[bQIo])
            p.do('vector', lambda e: e.tensor_copy(out=tQTo[:, :], in_=tQIo[:, :]), reads=[bQIo], writes=[bQTo])
            p.do('vector', lambda e, tt=tt: e.tensor_scalar(out=tQTo[:, :], in0=tQTo[:, :],
                 scalar1=tBS[:, 48 + tt:49 + tt], scalar2=None, op0=ALU.mult), sreads=[bBS], writes=[bQTo])
            p.do('vector', lambda e: e.tensor_tensor(out=tJunk[:, :], in0=tQTo[:, :], in1=tMask[:, :], op=ALU.mult),
                 reads=[bQTo, bMask], writes=[bJunk])
            for c in range(DC):
                p.do('tensor', lambda e, c=c: e.transpose(
                    out=psB[:, 0:128], in_=tJunk[:, c * 128:(c + 1) * 128], identity=t_idb[:, :]),
                    reads=[bJunk, B('idb')], writes=[pbB])
                p.do('vector', lambda e, c=c, tt=tt: e.tensor_copy(
                    out=tX8F[:, c * TL + tt * 128:(c * TL + (tt + 1) * 128)], in_=psB[:, 0:128]),
                    reads=[pbB], writes=[bX8F])
        if 'Ctaps' in dbg:
            p.dma('gpsimd', dbg_outs['o_x4b'][:, :], tX4B[:, :], reads=[bX4B], writes=[B('o_x4b')])
            p.dma('gpsimd', dbg_outs['o_x8f'][:, :], tX8F[:, :], reads=[bX8F], writes=[B('o_x8f')])
        p.fence()
        arC4.free()

        arC5 = AutoArena(nc)
        tWoF = [arC5.sbuf_tensor(f"C_wof{i}", [128, D], F32) for i in range(2)]
        tWoB = [arC5.sbuf_tensor(f"C_wob{i}", [128, D], BF16) for i in range(2)]
        tWoI = arC5.sbuf_tensor("C_woi", [128, D], I32)
        tAbs = arC5.sbuf_tensor("C_abs", [128, D], F32)
        tBR = arC5.sbuf_tensor("C_br", [128, 2], F32)
        tResC = [arC5.sbuf_tensor(f"C_res{i}", [128, TL], F32) for i in range(2)]
        tScC = [arC5.sbuf_tensor(f"C_sc{i}", [128, TL], F32) for i in range(2)]
        arC5.allocate()
        bAbs = B('C_abs'); bBR = B('C_br'); bRowC2 = B('C_rowc')
        for m in range(DC):
            db = m % 2
            bWf = B(f'C_wof{db}')
            p.dma('sync', tWoF[db][:, :].rearrange("p (c k) -> p c k", c=DC),
                  i_wo.rearrange("(c p) mm -> p c mm", p=128)[:, :, m * 128:(m + 1) * 128], writes=[bWf])
            p.do('scalar', lambda e, db=db: e.activation(out=tAbs[:, :], in_=tWoF[db][:, :], func=ACT.Abs),
                 reads=[bWf], writes=[bAbs])
            for g in range(4):
                p.do('tensor', lambda e, m=m, g=g: e.matmul(
                    out=psF[0:1, 3072 + 0:3072 + 512], lhsT=t_ones[:, 0:1], rhs=tAbs[:, g * 512:(g + 1) * 512],
                    start=(m == 0 and g == 0), stop=(m == DC - 1 and g == 3)),
                    reads=[bAbs, B('ones')], writes=[pbank[6]])
        p.do('vector', lambda e: e.tensor_reduce(out=tRowC[0:1, 32:33], in_=psF[0:1, 3072:3072 + 512],
             axis=AX.X, op=ALU.add), reads=[pbank[6]], writes=[bRowC2])
        p.pad('v', 2)
        p.do('vector', lambda e: e.tensor_scalar(out=tRowC[0:1, 33:34], in0=tRowC[0:1, 32:33],
             scalar1=1.0 / (D * D), scalar2=None, op0=ALU.mult), reads=[bRowC2], writes=[bRowC2])
        p.pad('v', 2)
        p.do('vector', lambda e: e.tensor_scalar(out=tRowC[0:1, 34:35], in0=tRowC[0:1, 33:34],
             scalar1=EPS_LN, scalar2=None, op0=ALU.add), reads=[bRowC2], writes=[bRowC2])
        p.pad('v', 2)
        p.do('vector', lambda e: e.reciprocal(out=tRowC[0:1, 35:36], in_=tRowC[0:1, 34:35]),
             reads=[bRowC2], writes=[bRowC2])
        p.pad('v', 2)
        p.do('tensor', lambda e: e.matmul(out=psF[:, 3072:3072 + 1], lhsT=t_ones[0:1, 0:128],
             rhs=tRowC[0:1, 33:34], start=True, stop=True), reads=[bRowC2, B('ones')], writes=[pbank[6]])
        p.do('tensor', lambda e: e.matmul(out=psF[:, 3073:3073 + 1], lhsT=t_ones[0:1, 0:128],
             rhs=tRowC[0:1, 35:36], start=True, stop=True), reads=[bRowC2, B('ones')], writes=[pbank[6]])
        p.do('vector', lambda e: e.tensor_copy(out=tBR[:, :], in_=psF[:, 3072:3072 + 2]),
             reads=[pbank[6]], writes=[bBR])
        p.pad('v', 2)
        for m in range(DC):
            db = m % 2
            bWf = B(f'C_wof{db}'); bWb = B(f'C_wob{db}')
            p.dma('sync', tWoF[db][:, :].rearrange("p (c k) -> p c k", c=DC),
                  i_wo.rearrange("(c p) mm -> p c mm", p=128)[:, :, m * 128:(m + 1) * 128], writes=[bWf])
            p.do('vector', lambda e, db=db: e.tensor_scalar(out=tAbs[:, :], in0=tWoF[db][:, :],
                 scalar1=tBR[:, 1:2], scalar2=None, op0=ALU.mult), reads=[bWf], sreads=[bBR], writes=[bAbs])
            p.do('vector', lambda e: e.tensor_copy(out=tWoI[:, :], in_=tAbs[:, :]), reads=[bAbs], writes=[B('C_woi')])
            p.do('vector', lambda e: e.tensor_copy(out=tAbs[:, :], in_=tWoI[:, :]), reads=[B('C_woi')], writes=[bAbs])
            p.do('vector', lambda e: e.tensor_scalar(out=tAbs[:, :], in0=tAbs[:, :],
                 scalar1=1.0, scalar2=-1.0, op0=ALU.min, op1=ALU.max), writes=[bAbs])
            p.do('vector', lambda e, db=db: e.tensor_scalar(out=tWoB[db][:, :], in0=tAbs[:, :],
                 scalar1=tBR[:, 0:1], scalar2=None, op0=ALU.mult), reads=[bAbs], sreads=[bBR], writes=[bWb])
            bResC = B(f'C_res{db}'); bScC = B(f'C_sc{db}')
            p.dma('gpsimd', tResC[db][:, :], d_s2[m * 128:(m + 1) * 128, :], reads=[B('d_s2')], writes=[bResC])
            for nn in range(2):
                pb = pbank[2 + nn]
                for c in range(DC):
                    p.do('tensor', lambda e, c=c, nn=nn, db=db: e.matmul(
                        out=psF[:, 1024 + nn * 512:1024 + (nn + 1) * 512],
                        lhsT=tWoB[db][:, c * 128:(c + 1) * 128],
                        rhs=tX8F[:, c * TL + nn * 512: c * TL + (nn + 1) * 512],
                        start=(c == 0), stop=(c == DC - 1)),
                        reads=[bWb, bX8F], writes=[pb])
                p.do('vector', lambda e, nn=nn, db=db: e.tensor_tensor(
                    out=tScC[db][:, nn * 512:(nn + 1) * 512], in0=psF[:, 1024 + nn * 512:1024 + (nn + 1) * 512],
                    in1=tResC[db][:, nn * 512:(nn + 1) * 512], op=ALU.add),
                    reads=[pb, bResC], writes=[bScC])
            p.dma('sync', d_s3[m * 128:(m + 1) * 128, :], tScC[db][:, :], reads=[bScC], writes=[B('d_s3')])
        if 'C' in dbg:
            for m in range(DC):
                p.dma('gpsimd', dbg_outs['o_s3'][m * 128:(m + 1) * 128, :], d_s3[m * 128:(m + 1) * 128, :],
                      reads=[B('d_s3')], writes=[B('o_s3')])
        p.fence()
        arC5.free()
        arCL3.free()
        arCL2.free()
        arCL.free()

    if 'D' in phases:
        IC = II // 512
        arD0 = AutoArena(nc)
        tNW = arD0.sbuf_tensor("D_nw", [128, II], F32)
        tBM = arD0.sbuf_tensor("D_bm", [128, 512], F32)
        tS2 = arD0.sbuf_tensor("D_s2", [128, 512], F32)
        tR2 = arD0.sbuf_tensor("D_r2", [128, 512], F32)
        tPK = arD0.sbuf_tensor("D_pk", [128, 1024], F32)
        arD0.allocate()
        bNW = B('D_nw'); bBM = B('D_bm'); bS2 = B('D_s2'); bR2 = B('D_r2'); bPK = B('D_pk')

        def mx_scales(nblk):
            p.pad('v', 2)
            p.do('vector', lambda e, nblk=nblk: e.tensor_scalar(out=tBM[:, 0:nblk], in0=tBM[:, 0:nblk],
                 scalar1=1.0 / 448.0, scalar2=None, op0=ALU.mult), reads=[bBM], writes=[bBM])
            p.pad('v', 2)
            p.do('vector', lambda e, nblk=nblk: e.tensor_scalar(out=tBM[:, 0:nblk], in0=tBM[:, 0:nblk],
                 scalar1=1e-30, scalar2=None, op0=ALU.max), reads=[bBM], writes=[bBM])
            p.pad('v', 2)
            p.do('vector', lambda e, nblk=nblk: e.tensor_scalar(out=tS2[:, 0:nblk], in0=tBM[:, 0:nblk],
                 scalar1=2.0, scalar2=None, op0=ALU.mult), reads=[bBM], writes=[bS2])
            p.do('vector', lambda e, nblk=nblk: e.reciprocal(out=tR2[:, 0:nblk], in_=tBM[:, 0:nblk]),
                 reads=[bBM], writes=[bR2])
            p.pad('v', 2)
            p.do('vector', lambda e, nblk=nblk: e.tensor_scalar(out=tR2[:, 0:nblk], in0=tR2[:, 0:nblk],
                 scalar1=0.5, scalar2=None, op0=ALU.mult), reads=[bR2], writes=[bR2])
            p.pad('v', 2)

        for (inat, dsc, tag) in ((i_gwn, d_gsci, 'g'), (i_uwn, d_usci, 'u')):
            for ch in range(8):
                p.dma('sync', tNW[:, 0:D], inat[ch * 128:(ch + 1) * 128, :], writes=[bNW])
                p.do('vector', lambda e: e.tensor_reduce(out=tBM[:, 0:64],
                     in_=tNW[:, 0:D].rearrange("p (g b) -> p g b", b=32), axis=AX.X, op=ALU.max,
                     apply_absolute_value=True), reads=[bNW], writes=[bBM])
                mx_scales(64)
                p.do('vector', lambda e: e.tensor_copy(out=tPK[:, 0:64], in_=tS2[:, 0:64]), reads=[bS2], writes=[bPK])
                p.do('vector', lambda e: e.tensor_copy(out=tPK[:, 64:128], in_=tR2[:, 0:64]), reads=[bR2], writes=[bPK])
                p.dma('sync', dsc[ch * 128:(ch + 1) * 128, :], tPK[:, 0:128], reads=[bPK], writes=[B('d_' + tag + 'sci')])
        for ch in range(2):
            p.dma('sync', tNW[:, 0:II], i_dwn[ch * 128:(ch + 1) * 128, :], writes=[bNW])
            p.do('vector', lambda e: e.tensor_reduce(out=tBM[:, 0:256],
                 in_=tNW[:, 0:II].rearrange("p (g b) -> p g b", b=32), axis=AX.X, op=ALU.max,
                 apply_absolute_value=True), reads=[bNW], writes=[bBM])
            mx_scales(256)
            p.do('vector', lambda e: e.tensor_copy(out=tPK[:, 0:256], in_=tS2[:, 0:256]), reads=[bS2], writes=[bPK])
            p.do('vector', lambda e: e.tensor_copy(out=tPK[:, 256:512], in_=tR2[:, 0:256]), reads=[bR2], writes=[bPK])
            p.dma('sync', d_dsci[ch * 128:(ch + 1) * 128, :], tPK[:, 0:512], reads=[bPK], writes=[B('d_dsci')])
        p.fence()
        p.do('gpsimd', lambda e: e.collective_compute(
            "AllGather", ALU.bypass, replica_groups=[list(range(NCORE))],
            ins=[d_gsci.opt()], outs=[d_gsco.opt()]), reads=[B('d_gsci')], writes=[B('d_gsco')])
        p.do('gpsimd', lambda e: e.collective_compute(
            "AllGather", ALU.bypass, replica_groups=[list(range(NCORE))],
            ins=[d_usci.opt()], outs=[d_usco.opt()]), reads=[B('d_usci')], writes=[B('d_usco')])
        p.do('gpsimd', lambda e: e.collective_compute(
            "AllGather", ALU.bypass, replica_groups=[list(range(NCORE))],
            ins=[d_dsci.opt()], outs=[d_dsco.opt()]), reads=[B('d_dsci')], writes=[B('d_dsco')])
        p.fence()
        arD0.free()

        arDL = AutoArena(nc)
        tX3F = arDL.sbuf_tensor("D_x3f", [128, DC * TL], BF16)
        tG2P = arDL.sbuf_tensor("D_g2p", [4, 128], F32)
        arDL.allocate()
        p.dma('sync', tG2P[:, :], i_g2p[:, :], writes=[B('D_g2p')])
        arD1 = AutoArena(nc)
        tX3 = arD1.sbuf_tensor("D_x3", [128, DC * TL], F32)
        tSQd = [arD1.sbuf_tensor(f"D_sq{i}", [128, 512], F32) for i in range(2)]
        tRowd = arD1.sbuf_tensor("D_row", [1, 512 * 5], F32)
        tMBd = arD1.sbuf_tensor("D_mb", [128, 512], F32)
        tRBd = arD1.sbuf_tensor("D_rb", [128, 512], F32)
        tNrmd = arD1.sbuf_tensor("D_nrm", [128, 512], F32)
        arD1.allocate()
        p.dma('sync', tX3[:, :].rearrange("p (c t) -> p c t", c=DC),
              d_s3.rearrange("(c p) t -> p c t", p=128), reads=[B('d_s3')], writes=[B('D_x3')])
        emit_ln2(tSQd, tRowd, tMBd, tRBd, tNrmd,
                 lambda c: tX3[:, c * TL:(c + 1) * TL], TL,
                 lambda c: tX3[:, c * TL:(c + 1) * TL], B('D_x3'), B('D_x3'), 'Dln')
        p.fence()
        arD2 = AutoArena(nc)
        tXTt = arD2.sbuf_tensor("D_xtt", [128, D], F32)
        tBMx = arD2.sbuf_tensor("D_bmx", [128, 64], F32)
        tS2x = arD2.sbuf_tensor("D_s2x", [128, 64], F32)
        tR2x = arD2.sbuf_tensor("D_r2x", [128, 64], F32)
        tQ8x = arD2.sbuf_tensor("D_q8x", [128, D], FP8)
        tQBx = arD2.sbuf_tensor("D_qbx", [128, D], BF16)
        arD2.allocate()
        bXTt = B('D_xtt'); bBMx = B('D_bmx'); bS2x = B('D_s2x'); bR2x = B('D_r2x')
        bQ8x = B('D_q8x'); bQBx = B('D_qbx'); bX3F = B('D_x3f')
        for tt in range(8):
            for c in range(DC):
                p.do('tensor', lambda e, c=c, tt=tt: e.transpose(
                    out=psF[:, 3072:3072 + 128], in_=tX3[:, c * TL + tt * 128: c * TL + (tt + 1) * 128], identity=t_idf[:, :]),
                    reads=[B('D_x3'), B('idf')], writes=[pbank[6]])
                p.do('vector', lambda e, c=c: e.tensor_copy(
                    out=tXTt[:, c * 128:(c + 1) * 128], in_=psF[:, 3072:3072 + 128]),
                    reads=[pbank[6]], writes=[bXTt])
            p.do('vector', lambda e: e.tensor_reduce(out=tBMx[:, 0:64],
                 in_=tXTt[:, :].rearrange("p (g b) -> p g b", b=32), axis=AX.X, op=ALU.max,
                 apply_absolute_value=True), reads=[bXTt], writes=[bBMx])
            p.pad('v', 2)
            p.do('vector', lambda e: e.tensor_scalar(out=tBMx[:, :], in0=tBMx[:, :], scalar1=1.0 / 448.0, scalar2=None, op0=ALU.mult),
                 reads=[bBMx], writes=[bBMx])
            p.pad('v', 2)
            p.do('vector', lambda e: e.tensor_scalar(out=tBMx[:, :], in0=tBMx[:, :], scalar1=1e-30, scalar2=None, op0=ALU.max),
                 reads=[bBMx], writes=[bBMx])
            p.pad('v', 2)
            p.do('vector', lambda e: e.tensor_scalar(out=tS2x[:, :], in0=tBMx[:, :], scalar1=2.0, scalar2=None, op0=ALU.mult),
                 reads=[bBMx], writes=[bS2x])
            p.do('vector', lambda e: e.reciprocal(out=tR2x[:, :], in_=tBMx[:, :]), reads=[bBMx], writes=[bR2x])
            p.pad('v', 2)
            p.do('vector', lambda e: e.tensor_scalar(out=tR2x[:, :], in0=tR2x[:, :], scalar1=0.5, scalar2=None, op0=ALU.mult),
                 reads=[bR2x], writes=[bR2x])
            p.pad('v', 2)
            p.do('vector', lambda e: e.tensor_tensor(
                out=tQ8x[:, :].rearrange("p (g b) -> p g b", b=32),
                in0=tXTt[:, :].rearrange("p (g b) -> p g b", b=32),
                in1=tR2x[:, :].to_broadcast((128, 64, 32)), op=ALU.mult),
                reads=[bXTt, bR2x], writes=[bQ8x])
            p.do('vector', lambda e: e.tensor_tensor(
                out=tQBx[:, :].rearrange("p (g b) -> p g b", b=32),
                in0=tQ8x[:, :].rearrange("p (g b) -> p g b", b=32),
                in1=tS2x[:, :].to_broadcast((128, 64, 32)), op=ALU.mult),
                reads=[bQ8x, bS2x], writes=[bQBx])
            for c in range(DC):
                p.do('tensor', lambda e, c=c, tt=tt: e.transpose(
                    out=psB[:, 0:128], in_=tQBx[:, c * 128:(c + 1) * 128], identity=t_idb[:, :]),
                    reads=[bQBx, B('idb')], writes=[pbB])
                p.do('vector', lambda e, c=c, tt=tt: e.tensor_copy(
                    out=tX3F[:, c * TL + tt * 128: c * TL + (tt + 1) * 128], in_=psB[:, 0:128]),
                    reads=[pbB], writes=[bX3F])
        p.fence()
        arD2.free()
        arD1.free()

        arD3 = AutoArena(nc)
        tGW2 = [arD3.sbuf_tensor(f"D_gw{i}", [128, 512], F32) for i in range(2)]
        tUW2 = [arD3.sbuf_tensor(f"D_uw{i}", [128, 512], F32) for i in range(2)]
        tGQ = arD3.sbuf_tensor("D_gq", [128, DC * 512], BF16)
        tUQ = arD3.sbuf_tensor("D_uq", [128, DC * 512], BF16)
        tQ8w = arD3.sbuf_tensor("D_q8w", [128, 512], FP8)
        tSC4 = arD3.sbuf_tensor("D_sc4", [4, 512], F32)
        tSbw = arD3.sbuf_tensor("D_sbw", [128, 512], F32)
        tRbw = arD3.sbuf_tensor("D_rbw", [128, 512], F32)
        tGf = arD3.sbuf_tensor("D_gf", [128, 512], F32)
        tUf = arD3.sbuf_tensor("D_uf", [128, 512], F32)
        tHf = arD3.sbuf_tensor("D_hf", [128, 8 * 512], F32)
        tBMh = arD3.sbuf_tensor("D_bmh", [128, 128], F32)
        tS2h = arD3.sbuf_tensor("D_s2h", [128, 128], F32)
        tR2h = arD3.sbuf_tensor("D_r2h", [128, 128], F32)
        tQ8h = arD3.sbuf_tensor("D_q8h", [128, 8 * 512], FP8)
        tHQ = arD3.sbuf_tensor("D_hq", [128, 8 * 512], BF16)
        tHT = arD3.sbuf_tensor("D_ht", [128, 128], BF16)
        arD3.allocate()
        bGQ = B('D_gq'); bUQ = B('D_uq'); bQ8w = B('D_q8w'); bSC4 = B('D_sc4')
        bSbw = B('D_sbw'); bRbw = B('D_rbw'); bGf = B('D_gf'); bUf = B('D_uf')
        bHf = B('D_hf'); bBMh = B('D_bmh'); bS2h = B('D_s2h'); bR2h = B('D_r2h')
        bQ8h = B('D_q8h'); bHQ = B('D_hq'); bHT = B('D_ht')

        def quant_wslab(wf, bwf, dsc, bdsc, c, ig, out_tile, bout, nwid):
            for (colbase, dstT, bdst) in ((0, tSbw, bSbw), (64, tRbw, bRbw)):
                p.dma('gpsimd', tSC4[:, 0:nwid],
                      dsc[ig * nwid:(ig + 1) * nwid, colbase + c * 4: colbase + c * 4 + 4].rearrange("i g -> g i"),
                      reads=[bdsc], writes=[bSC4], slow=True)
                p.do('tensor', lambda e, nwid=nwid: e.matmul(out=psF[:, 2048:2048 + nwid], lhsT=tG2P[:, :],
                     rhs=tSC4[:, 0:nwid], start=True, stop=True), reads=[bSC4, B('D_g2p')], writes=[pbank[4]])
                p.do('vector', lambda e, dstT=dstT, nwid=nwid: e.tensor_copy(out=dstT[:, 0:nwid], in_=psF[:, 2048:2048 + nwid]),
                     reads=[pbank[4]], writes=[bdst])
            p.do('vector', lambda e, nwid=nwid: e.tensor_tensor(out=tQ8w[:, 0:nwid], in0=wf[:, 0:nwid], in1=tRbw[:, 0:nwid], op=ALU.mult),
                 reads=[bwf, bRbw], writes=[bQ8w])
            p.do('vector', lambda e, out_tile=out_tile, nwid=nwid: e.tensor_tensor(out=out_tile, in0=tQ8w[:, 0:nwid], in1=tSbw[:, 0:nwid], op=ALU.mult),
                 reads=[bQ8w, bSbw], writes=[bout])

        for ig in range(IC):
            for c in range(DC):
                db = c % 2
                bGW = B(f'D_gw{db}'); bUW = B(f'D_uw{db}')
                p.dma('sync', tGW2[db][:, :], i_gw[c * 128:(c + 1) * 128, ig * 512:(ig + 1) * 512], writes=[bGW])
                p.dma('sync', tUW2[db][:, :], i_uw[c * 128:(c + 1) * 128, ig * 512:(ig + 1) * 512], writes=[bUW])
                quant_wslab(tGW2[db], bGW, d_gsco, B('d_gsco'), c, ig, tGQ[:, c * 512:(c + 1) * 512], bGQ, 512)
                quant_wslab(tUW2[db], bUW, d_usco, B('d_usco'), c, ig, tUQ[:, c * 512:(c + 1) * 512], bUQ, 512)
            for tt in range(8):
                for c in range(DC):
                    p.do('tensor', lambda e, tt=tt, c=c: e.matmul(
                        out=psF[:, 0:512], lhsT=tX3F[:, c * TL + tt * 128: c * TL + (tt + 1) * 128],
                        rhs=tGQ[:, c * 512:(c + 1) * 512], start=(c == 0), stop=(c == DC - 1)),
                        reads=[bX3F, bGQ], writes=[pbank[0]])
                    p.do('tensor', lambda e, tt=tt, c=c: e.matmul(
                        out=psF[:, 512:1024], lhsT=tX3F[:, c * TL + tt * 128: c * TL + (tt + 1) * 128],
                        rhs=tUQ[:, c * 512:(c + 1) * 512], start=(c == 0), stop=(c == DC - 1)),
                        reads=[bX3F, bUQ], writes=[pbank[1]])
                p.do('scalar', lambda e: e.activation(out=tGf[:, :], in_=psF[:, 0:512], func=ACT.Silu),
                     reads=[pbank[0]], writes=[bGf])
                p.do('scalar', lambda e: e.activation(out=tUf[:, :], in_=psF[:, 512:1024], func=ACT.Copy),
                     reads=[pbank[1]], writes=[bUf])
                p.do('vector', lambda e, tt=tt: e.tensor_tensor(out=tHf[:, tt * 512:(tt + 1) * 512], in0=tGf[:, :], in1=tUf[:, :], op=ALU.mult),
                     reads=[bGf, bUf], writes=[bHf])
            p.do('vector', lambda e: e.tensor_reduce(out=tBMh[:, 0:128],
                 in_=tHf[:, :].rearrange("p (g b) -> p g b", b=32), axis=AX.X, op=ALU.max,
                 apply_absolute_value=True), reads=[bHf], writes=[bBMh])
            p.pad('v', 2)
            p.do('vector', lambda e: e.tensor_scalar(out=tBMh[:, :], in0=tBMh[:, :], scalar1=1.0 / 448.0, scalar2=None, op0=ALU.mult),
                 reads=[bBMh], writes=[bBMh])
            p.pad('v', 2)
            p.do('vector', lambda e: e.tensor_scalar(out=tBMh[:, :], in0=tBMh[:, :], scalar1=1e-30, scalar2=None, op0=ALU.max),
                 reads=[bBMh], writes=[bBMh])
            p.pad('v', 2)
            p.do('vector', lambda e: e.tensor_scalar(out=tS2h[:, :], in0=tBMh[:, :], scalar1=2.0, scalar2=None, op0=ALU.mult),
                 reads=[bBMh], writes=[bS2h])
            p.do('vector', lambda e: e.reciprocal(out=tR2h[:, :], in_=tBMh[:, :]), reads=[bBMh], writes=[bR2h])
            p.pad('v', 2)
            p.do('vector', lambda e: e.tensor_scalar(out=tR2h[:, :], in0=tR2h[:, :], scalar1=0.5, scalar2=None, op0=ALU.mult),
                 reads=[bR2h], writes=[bR2h])
            p.pad('v', 2)
            p.do('vector', lambda e: e.tensor_tensor(
                out=tQ8h[:, :].rearrange("p (g b) -> p g b", b=32),
                in0=tHf[:, :].rearrange("p (g b) -> p g b", b=32),
                in1=tR2h[:, :].to_broadcast((128, 128, 32)), op=ALU.mult),
                reads=[bHf, bR2h], writes=[bQ8h])
            p.do('vector', lambda e: e.tensor_tensor(
                out=tHQ[:, :].rearrange("p (g b) -> p g b", b=32),
                in0=tQ8h[:, :].rearrange("p (g b) -> p g b", b=32),
                in1=tS2h[:, :].to_broadcast((128, 128, 32)), op=ALU.mult),
                reads=[bQ8h, bS2h], writes=[bHQ])
            for tt in range(8):
                for sub in range(4):
                    p.do('tensor', lambda e, tt=tt, sub=sub: e.transpose(
                        out=psB[:, 0:128], in_=tHQ[:, tt * 512 + sub * 128: tt * 512 + (sub + 1) * 128], identity=t_idb[:, :]),
                        reads=[bHQ, B('idb')], writes=[pbB])
                    p.do('vector', lambda e: e.tensor_copy(out=tHT[:, :], in_=psB[:, 0:128]),
                         reads=[pbB], writes=[bHT])
                    p.dma('sync', d_hq[ig * 512 + sub * 128: ig * 512 + (sub + 1) * 128, tt * 128:(tt + 1) * 128],
                          tHT[:, :], reads=[bHT], writes=[B('d_hq')])
        p.fence()
        arD3.free()

        arD4 = AutoArena(nc)
        tDWf = [arD4.sbuf_tensor(f"D_dwf{i}", [128, 256], F32) for i in range(2)]
        tDWQ = [arD4.sbuf_tensor(f"D_dwq{i}", [128, 256], BF16) for i in range(2)]
        tHTm = [arD4.sbuf_tensor(f"D_htm{i}", [128, TL], BF16) for i in range(2)]
        tQ8d = arD4.sbuf_tensor("D_q8d", [128, 256], FP8)
        tSC4d = arD4.sbuf_tensor("D_sc4d", [4, 256], F32)
        tSbd = arD4.sbuf_tensor("D_sbd", [128, 256], F32)
        tRbd = arD4.sbuf_tensor("D_rbd", [128, 256], F32)
        tResD = [arD4.sbuf_tensor(f"D_res{i}", [128, TL], F32) for i in range(2)]
        tScD = [arD4.sbuf_tensor(f"D_scd{i}", [128, TL], F32) for i in range(2)]
        arD4.allocate()
        bQ8d = B('D_q8d'); bSC4d = B('D_sc4d'); bSbd = B('D_sbd'); bRbd = B('D_rbd')
        for mg in range(8):
            for k in range(II // 128):
                db = k % 2
                bDWf = B(f'D_dwf{db}'); bDWQ = B(f'D_dwq{db}'); bHTm = B(f'D_htm{db}')
                p.dma('sync', tDWf[db][:, :], i_dw[k * 128:(k + 1) * 128, mg * 256:(mg + 1) * 256], writes=[bDWf])
                p.dma('gpsimd', tHTm[db][:, :], d_hq[k * 128:(k + 1) * 128, :], reads=[B('d_hq')], writes=[bHTm])
                for (colbase, dstT, bdst) in ((0, tSbd, bSbd), (256, tRbd, bRbd)):
                    p.dma('gpsimd', tSC4d[:, 0:256],
                          d_dsco[mg * 256:(mg + 1) * 256, colbase + k * 4: colbase + k * 4 + 4].rearrange("i g -> g i"),
                          reads=[B('d_dsco')], writes=[bSC4d], slow=True)
                    p.do('tensor', lambda e: e.matmul(out=psF[:, 2048:2048 + 256], lhsT=tG2P[:, :],
                         rhs=tSC4d[:, 0:256], start=True, stop=True), reads=[bSC4d, B('D_g2p')], writes=[pbank[4]])
                    p.do('vector', lambda e, dstT=dstT: e.tensor_copy(out=dstT[:, :], in_=psF[:, 2048:2048 + 256]),
                         reads=[pbank[4]], writes=[bdst])
                p.do('vector', lambda e, db=db: e.tensor_tensor(out=tQ8d[:, :], in0=tDWf[db][:, :], in1=tRbd[:, :], op=ALU.mult),
                     reads=[bDWf, bRbd], writes=[bQ8d])
                p.do('vector', lambda e, db=db: e.tensor_tensor(out=tDWQ[db][:, :], in0=tQ8d[:, :], in1=tSbd[:, :], op=ALU.mult),
                     reads=[bQ8d, bSbd], writes=[bDWQ])
                for mm in range(2):
                    for nn in range(2):
                        pb = pbank[mm * 2 + nn]
                        p.do('tensor', lambda e, k=k, mm=mm, nn=nn, db=db: e.matmul(
                            out=psF[:, (mm * 2 + nn) * 512:(mm * 2 + nn + 1) * 512],
                            lhsT=tDWQ[db][:, mm * 128:(mm + 1) * 128],
                            rhs=tHTm[db][:, nn * 512:(nn + 1) * 512],
                            start=(k == 0), stop=(k == II // 128 - 1)),
                            reads=[bDWQ, bHTm], writes=[pb])
            for mm in range(2):
                m = mg * 2 + mm
                db = mm % 2
                bResD = B(f'D_res{db}'); bScD = B(f'D_scd{db}')
                p.dma('gpsimd', tResD[db][:, :], d_s3[m * 128:(m + 1) * 128, :], reads=[B('d_s3')], writes=[bResD])
                for nn in range(2):
                    pb = pbank[mm * 2 + nn]
                    p.do('vector', lambda e, mm=mm, nn=nn, db=db: e.tensor_tensor(
                        out=tScD[db][:, nn * 512:(nn + 1) * 512],
                        in0=psF[:, (mm * 2 + nn) * 512:(mm * 2 + nn + 1) * 512],
                        in1=tResD[db][:, nn * 512:(nn + 1) * 512], op=ALU.add),
                        reads=[pb, bResD], writes=[bScD])
                p.dma('sync', final_out[m * 128:(m + 1) * 128, :], tScD[db][:, :], reads=[bScD], writes=[B('out')])
        p.fence()
        arD4.free()
        arDL.free()

    if phases[-1] == 'A':
        for m in range(DC):
            p.dma('sync', final_out[m * 128:(m + 1) * 128, :], d_s1[m * 128:(m + 1) * 128, :],
                  reads=[B('d_s1')], writes=[B('out')])
    if phases[-1] == 'B':
        for m in range(DC):
            p.dma('sync', final_out[m * 128:(m + 1) * 128, :], d_s2[m * 128:(m + 1) * 128, :],
                  reads=[B('d_s2')], writes=[B('out')])
    if phases[-1] == 'C':
        for m in range(DC):
            p.dma('sync', final_out[m * 128:(m + 1) * 128, :], d_s3[m * 128:(m + 1) * 128, :],
                  reads=[B('d_s3')], writes=[B('out')])

    finals = p.barrier_tok()
    with nc.Block() as block:
        for eng in ENGS:
            its = p.items[eng]
            if not its:
                continue
            getattr(block, eng)(p.mk_emitter(its, sems, finals))

    stack.close()
    return nc, out_names


def _prep_inputs(inputs):
    x = np.asarray(inputs['L0_attn_x'])[0]
    xT = np.ascontiguousarray(x.T)
    z64 = np.zeros((D, HIST), np.float32)
    piw = np.ascontiguousarray(np.asarray(inputs['L0_attn_proj_in_w']).T)
    pow_ = np.ascontiguousarray(np.asarray(inputs['L0_attn_proj_out_w']).T)
    pib = np.ascontiguousarray(np.asarray(inputs['L0_attn_proj_in_b']).reshape(D, 1))
    pob = np.ascontiguousarray(np.asarray(inputs['L0_attn_proj_out_b']).reshape(D, 1))
    A = np.ascontiguousarray(np.asarray(inputs['L0_attn_A']))
    BwT = np.ascontiguousarray(np.asarray(inputs['L0_attn_B_w']).T)
    C2 = np.ascontiguousarray(np.tile(np.asarray(inputs['L0_attn_C_w']).T, (LF, 1)))
    Dp = np.ascontiguousarray(np.asarray(inputs['L0_attn_D_param']).reshape(D, 1))
    idf = np.eye(128, dtype=np.float32)
    idb = np.eye(128, dtype=ml_dtypes.bfloat16)
    consts = np.zeros((128, 8), np.float32)
    consts[:, 0] = EPS_LN
    rep16 = (np.arange(128)[None, :] % 16 == np.arange(16)[:, None]).astype(np.float32)
    onesm = np.ones((128, 128), np.float32)
    wg = np.ascontiguousarray(np.asarray(inputs['L0_ffn_wq_gate']).T)
    wu = np.ascontiguousarray(np.asarray(inputs['L0_ffn_wq_up']).T)
    wd = np.ascontiguousarray(np.asarray(inputs['L0_ffn_wq_down']).T)
    sg = np.ascontiguousarray(np.asarray(inputs['L0_ffn_s_gate']).reshape(HID, 1))
    su = np.ascontiguousarray(np.asarray(inputs['L0_ffn_s_up']).reshape(HID, 1))
    sd2 = np.ascontiguousarray(np.asarray(inputs['L0_ffn_s_down']).reshape(D, 1))
    wqT = np.ascontiguousarray(np.asarray(inputs['L1_attn_wq']).T)
    wkT = np.ascontiguousarray(np.asarray(inputs['L1_attn_wk']).T)
    wvT = np.ascontiguousarray(np.asarray(inputs['L1_attn_wv']).T)
    woT = np.ascontiguousarray(np.asarray(inputs['L1_attn_wo']).T)
    iarr = (np.arange(0, HD, 2).astype(np.float32) / np.float32(HD)).astype(np.float32)
    inv32 = (np.float32(1.0) / np.power(np.float32(10000.0), iarr, dtype=np.float32)).astype(np.float32)
    tpos = np.arange(T, dtype=np.float32)
    freqs = (tpos[:, None] * inv32[None, :]).astype(np.float32)
    emb = np.concatenate([freqs, freqs], axis=1)
    cosT = np.ascontiguousarray(np.cos(emb).astype(np.float32).T)
    sinT = np.ascontiguousarray(np.sin(emb).astype(np.float32).T)
    gwN = np.ascontiguousarray(np.asarray(inputs['L1_ffn_gate_w']))
    uwN = np.ascontiguousarray(np.asarray(inputs['L1_ffn_up_w']))
    dwN = np.ascontiguousarray(np.asarray(inputs['L1_ffn_down_w']))
    gwT = np.ascontiguousarray(gwN.T); uwT = np.ascontiguousarray(uwN.T); dwT = np.ascontiguousarray(dwN.T)
    g2p = (np.arange(128)[None, :] // 32 == np.arange(4)[:, None]).astype(np.float32)
    maps = []
    for c in range(NCORE):
        lo = c * TL
        if c == 0:
            xloc = np.ascontiguousarray(np.concatenate([z64, xT[:, 0:TL]], axis=1))
            hm = np.zeros((S, HIST), np.float32)
        else:
            xloc = np.ascontiguousarray(xT[:, lo - HIST: lo + TL])
            hm = np.ones((S, HIST), np.float32)
        maps.append({
            'i_x': xloc, 'i_piw': piw, 'i_pib': pib, 'i_pow': pow_, 'i_pob': pob,
            'i_A': A, 'i_BwT': BwT, 'i_C2': C2, 'i_Dp': Dp, 'i_hm': hm,
            'i_idf': idf, 'i_idb': idb, 'i_const': consts, 'i_rep16': rep16, 'i_ones': onesm,
            'i_wg': wg, 'i_wu': wu, 'i_wd': wd, 'i_sg': sg, 'i_su': su, 'i_sd2': sd2,
            'i_wq': wqT, 'i_wk': wkT, 'i_wv': wvT, 'i_wo': woT,
            'i_cos': cosT[:, lo:lo + TL].copy(), 'i_sin': sinT[:, lo:lo + TL].copy(),
            'i_gw': gwT, 'i_uw': uwT, 'i_dw': dwT,
            'i_gwn': gwN[lo:lo + TL].copy(), 'i_uwn': uwN[lo:lo + TL].copy(),
            'i_dwn': dwN[c * (D // NCORE):(c + 1) * (D // NCORE)].copy(), 'i_g2p': g2p,
        })
    return maps


_PHASES = ('A', 'B', 'C', 'D')

def kernel(**inputs):
    maps = _prep_inputs(inputs)
    nc, out_names = build_program(phases=_PHASES)
    res = run_bass_kernel_spmd(nc, maps, core_ids=list(range(NCORE)))
    shards = [np.asarray(r['out_shard']) for r in res.results]
    out = np.concatenate([s.T for s in shards], axis=0).reshape(1, T, D).astype(np.float32)
    return out


# version 03
# speedup vs baseline: 1.0028x; 1.0028x over previous
import contextlib
import numpy as np
import ml_dtypes
import concourse.bass as bass
import concourse.mybir as mybir
from concourse.bass import AutoArena
from concourse.bass_utils import run_bass_kernel_spmd

F32 = mybir.dt.float32
BF16 = mybir.dt.bfloat16
I32 = mybir.dt.int32
FP8 = mybir.dt.float8e4
ACT = mybir.ActivationFunctionType
ALU = mybir.AluOpType
AX = mybir.AxisListType

NCORE = 8
T = 8192
D = 2048
TL = T // NCORE
HIST = 64
TW = TL + HIST
LF = 64
S = 16
HID = 5632
II = 8192
HD = 128
NH = 16
DC = D // 128
NTT = (TW + 127) // 128
EPS_LN = 1e-5

ENGS = ('sync', 'vector', 'scalar', 'tensor', 'gpsimd')
DMA_Q = {'sync': 'qs', 'gpsimd': 'qg', 'vector': 'qv', 'scalar': 'qc'}
SEM_NAMES = ('qs', 'qg', 'qv', 'qc', 'v', 's', 't', 'g')
ENG_SEM = {'vector': 'v', 'scalar': 's', 'tensor': 't', 'gpsimd': 'g', 'sync': 'qs'}
COMPUTE_SEMS = ('v', 's', 't', 'g')


class Buf:
    __slots__ = ('wtok', 'rtoks', 'name')
    def __init__(self, name=''):
        self.wtok = None
        self.rtoks = []
        self.name = name


class P:
    def __init__(self, nc):
        self.nc = nc
        self.items = {e: [] for e in ENGS}
        self.cnt = {sm: 0 for sm in SEM_NAMES}
        self.fence_deps = {}

    def do(self, eng, fn, reads=(), writes=(), inc=1, sem=None, sreads=()):
        sem = sem or ENG_SEM[eng]
        for b in sreads:
            if b.wtok is not None and b.wtok[0] in ('v', 's', 'g') and b.wtok[1] >= self.cnt[b.wtok[0]]:
                self._settle(b.wtok[0])
        deps = dict(self.fence_deps)
        def need(tok):
            if tok is None:
                return
            sm, val = tok
            if sm == sem and sm in COMPUTE_SEMS:
                return
            if deps.get(sm, 0) < val:
                deps[sm] = val
        def need_s(tok):
            if tok is None:
                return
            sm, val = tok
            if sm == sem and sm in COMPUTE_SEMS:
                return
            if deps.get(sm, 0) < val:
                deps[sm] = val
        for b in reads:
            need(b.wtok)
        for b in sreads:
            need_s(b.wtok)
        for b in writes:
            need(b.wtok)
            for t_ in b.rtoks:
                need(t_)
        if sem in deps and sem in COMPUTE_SEMS:
            del deps[sem]
        for sm in ('v', 's', 'g'):
            if sm in deps and sm != sem:
                if deps[sm] >= self.cnt[sm]:
                    self._settle(sm)
                deps[sm] = min(deps[sm] + 1, self.cnt[sm])
        self.items[eng].append((deps, fn, sem, inc))
        self.cnt[sem] += inc
        tok = (sem, self.cnt[sem])
        for b in reads:
            b.rtoks.append(tok)
        for b in sreads:
            b.rtoks.append(tok)
        for b in writes:
            b.wtok = tok
            b.rtoks = []
        return tok

    def _settle(self, sm):
        eng = {'v': 'vector', 's': 'scalar', 'g': 'gpsimd'}[sm]
        st = self.settle_tile
        if sm == 'v':
            fn = lambda e: e.tensor_copy(out=st[0:1, 0:1], in_=st[0:1, 1:2])
        elif sm == 's':
            fn = lambda e: e.activation(out=st[0:1, 0:1], in_=st[0:1, 1:2], func=ACT.Copy)
        else:
            fn = lambda e: e.memset(st[0:1, 0:1], 0.0)
        self.items[eng].append(({}, fn, sm, 1))
        self.cnt[sm] += 1

    def dma(self, eng, out, in_, reads=(), writes=(), slow=False):
        q = DMA_Q[eng]
        if slow:
            fn = lambda e, o=out, i=in_: e.dma_start(out=o, in_=i, allow_slow_non_contiguous=True)
        else:
            fn = lambda e, o=out, i=in_: e.dma_start(out=o, in_=i)
        return self.do(eng, fn, reads=reads, writes=writes, inc=16, sem=q)

    def pad(self, sm, n):
        for _ in range(n):
            self._settle(sm)

    def fence(self):
        self.fence_deps = {sm: self.cnt[sm] for sm in SEM_NAMES if self.cnt[sm] > 0}

    def barrier_tok(self):
        return [(sm, self.cnt[sm]) for sm in SEM_NAMES if self.cnt[sm] > 0]

    def mk_emitter(self, its, sems, finals):
        def g(e):
            cur = {sm: 0 for sm in SEM_NAMES}
            for deps, fn, sem, inc in its:
                for sm in sorted(deps):
                    val = deps[sm]
                    if val > cur[sm]:
                        e.wait_ge(sems[sm], val)
                        cur[sm] = val
                fn(e).then_inc(sems[sem], inc)
            for sm, val in finals:
                if val > cur[sm]:
                    e.wait_ge(sems[sm], val)
                    cur[sm] = val
        return g


def build_program(phases=('A', 'B', 'C', 'D'), dbg=()):
    nc = bass.Bass("TRN2", target_bir_lowering=False)
    p = P(nc)

    def DI(name, shape, dt=F32):
        return nc.dram_tensor(name, shape, dt, kind="ExternalInput").ap()
    out_names = []
    def DO(name, shape, dt=F32):
        out_names.append(name)
        return nc.dram_tensor(name, shape, dt, kind="ExternalOutput").ap()

    i_x = DI("i_x", [D, TW])
    i_piw = DI("i_piw", [D, D]); i_pib = DI("i_pib", [D, 1])
    i_pow = DI("i_pow", [D, D]); i_pob = DI("i_pob", [D, 1])
    i_A = DI("i_A", [D, S]); i_BwT = DI("i_BwT", [D, S])
    i_C2 = DI("i_C2", [LF * S, D]); i_Dp = DI("i_Dp", [D, 1])
    i_hm = DI("i_hm", [S, HIST])
    i_idf = DI("i_idf", [128, 128], F32)
    i_idb = DI("i_idb", [128, 128], BF16)
    i_const = DI("i_const", [128, 8], F32)
    i_rep16 = DI("i_rep16", [16, 128], F32)
    i_ones = DI("i_ones", [128, 128], F32)
    i_wg = DI("i_wg", [D, HID], I32)
    i_wu = DI("i_wu", [D, HID], I32)
    i_wd = DI("i_wd", [HID, D], I32)
    i_sg = DI("i_sg", [HID, 1]); i_su = DI("i_su", [HID, 1]); i_sd2 = DI("i_sd2", [D, 1])
    i_wq = DI("i_wq", [D, D]); i_wk = DI("i_wk", [D, D]); i_wv = DI("i_wv", [D, D]); i_wo = DI("i_wo", [D, D])
    i_cos = DI("i_cos", [128, TL]); i_sin = DI("i_sin", [128, TL])
    i_gw = DI("i_gw", [D, II]); i_uw = DI("i_uw", [D, II]); i_dw = DI("i_dw", [II, D])
    i_gwn = DI("i_gwn", [TL, D]); i_uwn = DI("i_uwn", [TL, D]); i_dwn = DI("i_dwn", [D // NCORE, II])
    i_g2p = DI("i_g2p", [4, 128])

    d_xp = nc.dram_tensor("d_xp", [D, TW], F32).ap()
    d_s1 = nc.dram_tensor("d_s1", [D, TL], F32).ap()
    d_s2 = nc.dram_tensor("d_s2", [D, TL], F32).ap()
    d_s3 = nc.dram_tensor("d_s3", [D, TL], F32).ap()
    d_kin = nc.dram_tensor("d_kin", [D, TL], BF16).ap()
    d_kag = nc.dram_tensor("d_kag", [NCORE * D, TL], BF16).ap()
    d_vin = nc.dram_tensor("d_vin", [NH * TL, HD], BF16).ap()
    d_vag = nc.dram_tensor("d_vag", [NCORE * NH * TL, HD], BF16).ap()
    d_kmi = nc.dram_tensor("d_kmi", [D, 1], F32).ap()
    d_rm = nc.dram_tensor("d_rm", [128, 16], F32).ap()
    d_gsci = nc.dram_tensor("d_gsci", [TL, 128], F32).ap()
    d_gsco = nc.dram_tensor("d_gsco", [II, 128], F32).ap()
    d_usci = nc.dram_tensor("d_usci", [TL, 128], F32).ap()
    d_usco = nc.dram_tensor("d_usco", [II, 128], F32).ap()
    d_dsci = nc.dram_tensor("d_dsci", [D // NCORE, 512], F32).ap()
    d_dsco = nc.dram_tensor("d_dsco", [D, 512], F32).ap()
    d_hq = nc.dram_tensor("d_hq", [II, TL], BF16).ap()
    d_kmo = nc.dram_tensor("d_kmo", [D, 1], F32).ap()

    dbg_outs = {}
    if 'A' in dbg:
        dbg_outs['o_s1'] = DO("o_s1", [D, TL])
    if 'B' in dbg:
        dbg_outs['o_s2'] = DO("o_s2", [D, TL])
    if 'C' in dbg:
        dbg_outs['o_s3'] = DO("o_s3", [D, TL])
    if 'Ctaps' in dbg:
        dbg_outs['o_ao'] = DO("o_ao", [128, 8 * D], BF16)
        dbg_outs['o_negm'] = DO("o_negm", [128, NH])
        dbg_outs['o_x4b'] = DO("o_x4b", [128, DC * TL], BF16)
        dbg_outs['o_qb'] = DO("o_qb", [128, NH * TL], BF16)
        dbg_outs['o_x8f'] = DO("o_x8f", [128, DC * TL], BF16)
        dbg_outs['o_x2ln'] = DO("o_x2ln", [128, DC * TL])
        dbg_outs['o_cm'] = DO("o_cm", [128, TL])
        dbg_outs['o_rm'] = DO("o_rm", [128, 16])
        dbg_outs['o_srow'] = DO("o_srow", [1, 2 * TL])
        dbg_outs['o_sb4'] = DO("o_sb4", [128, TL])
        dbg_outs['o_rsb4'] = DO("o_rsb4", [128, TL])
    if 'Ataps' in dbg:
        dbg_outs['o_x'] = DO("o_x", [128, DC * TW])
        dbg_outs['o_ln'] = DO("o_ln", [128, DC * TW], BF16)
        dbg_outs['o_xp'] = DO("o_xp", [D, TW])
        dbg_outs['o_dlt'] = DO("o_dlt", [16, TW])
        dbg_outs['o_esk'] = DO("o_esk", [128, 8 * TL], BF16)
        dbg_outs['o_yb'] = DO("o_yb", [128, DC * TL], BF16)
    final_out = DO("out_shard", [D, TL])

    stack = contextlib.ExitStack()
    sems = {sm: stack.enter_context(nc.semaphore("sem_" + sm)) for sm in SEM_NAMES}
    psF = stack.enter_context(nc.psum_tensor("psF", [128, 3584], F32))
    psB = stack.enter_context(nc.psum_tensor("psB", [128, 1024], BF16))

    pers = AutoArena(nc)
    t_settle = pers.sbuf_tensor("t_settle", [128, 4], F32)
    p.settle_tile = t_settle
    t_idf = pers.sbuf_tensor("t_idf", [128, 128], F32)
    t_idb = pers.sbuf_tensor("t_idb", [128, 128], BF16)
    t_const = pers.sbuf_tensor("t_const", [128, 8], F32)
    t_rep16 = pers.sbuf_tensor("t_rep16", [16, 128], F32)
    t_ones = pers.sbuf_tensor("t_ones", [128, 128], F32)
    pers.allocate()

    bufs = {}
    def B(name):
        if name not in bufs:
            bufs[name] = Buf(name)
        return bufs[name]
    pbank = [B(f"psF{i}") for i in range(7)]
    pbB = B("psB")

    def emit_ln2(tSQ, tRow, tMB, tRB, tNrm, xc, w_total, lnb, blnb, bx, tag):
        bSQ = [B(tag + '_sq0'), B(tag + '_sq1')]
        bRow = B(tag + '_row'); bMB = B(tag + '_mb'); bRB = B(tag + '_rb'); bNrm = B(tag + '_nrm')
        nbl = [(n0, min(512, w_total - n0)) for n0 in range(0, w_total, 512)]
        for (n0, nw) in nbl:
            pbx = pbank[4]; pbq = pbank[5]
            for c in range(DC):
                db = c % 2
                p.do('scalar', lambda e, c=c, n0=n0, nw=nw, db=db: e.activation(
                    out=tSQ[db][:, 0:nw], in_=xc(c)[:, n0:n0 + nw], func=ACT.Square),
                    reads=[bx], writes=[bSQ[db]])
                p.do('tensor', lambda e, c=c, n0=n0, nw=nw: e.matmul(
                    out=psF[0:1, 2048:2048 + nw], lhsT=t_ones[:, 0:1], rhs=xc(c)[:, n0:n0 + nw],
                    start=(c == 0), stop=(c == DC - 1)),
                    reads=[bx, B('ones')], writes=[pbx])
                p.do('tensor', lambda e, c=c, nw=nw, db=db: e.matmul(
                    out=psF[0:1, 2560:2560 + nw], lhsT=t_ones[:, 0:1], rhs=tSQ[db][:, 0:nw],
                    start=(c == 0), stop=(c == DC - 1)),
                    reads=[bSQ[db], B('ones')], writes=[pbq])
            p.do('vector', lambda e, nw=nw: e.tensor_scalar(out=tRow[0:1, 0:nw], in0=psF[0:1, 2048:2048 + nw],
                 scalar1=1.0 / D, scalar2=None, op0=ALU.mult), reads=[pbx], writes=[bRow])
            p.do('vector', lambda e, nw=nw: e.tensor_scalar(out=tRow[0:1, 512:512 + nw], in0=psF[0:1, 2560:2560 + nw],
                 scalar1=1.0 / D, scalar2=None, op0=ALU.mult), reads=[pbq], writes=[bRow])
            p.do('vector', lambda e, nw=nw: e.tensor_tensor(out=tRow[0:1, 1024:1024 + nw], in0=tRow[0:1, 0:nw],
                 in1=tRow[0:1, 0:nw], op=ALU.mult), reads=[bRow], writes=[bRow])
            p.do('vector', lambda e, nw=nw: e.tensor_tensor(out=tRow[0:1, 1536:1536 + nw], in0=tRow[0:1, 512:512 + nw],
                 in1=tRow[0:1, 1024:1024 + nw], op=ALU.subtract), reads=[bRow], writes=[bRow])
            p.do('scalar', lambda e, nw=nw: e.activation(out=tRow[0:1, 2048:2048 + nw], in_=tRow[0:1, 1536:1536 + nw],
                 func=ACT.Sqrt, bias=t_const[0:1, 0:1], scale=1.0), reads=[bRow], sreads=[B('const')], writes=[bRow])
            p.do('vector', lambda e, nw=nw: e.reciprocal(out=tRow[0:1, 1536:1536 + nw], in_=tRow[0:1, 2048:2048 + nw]),
                 reads=[bRow], writes=[bRow])
            p.do('tensor', lambda e, nw=nw: e.matmul(
                out=psF[:, 0:nw], lhsT=t_ones[0:1, 0:128], rhs=tRow[0:1, 0:nw], start=True, stop=True),
                reads=[bRow, B('ones')], writes=[pbank[0]])
            p.do('vector', lambda e, nw=nw: e.tensor_copy(out=tMB[:, 0:nw], in_=psF[:, 0:nw]),
                 reads=[pbank[0]], writes=[bMB])
            p.do('tensor', lambda e, nw=nw: e.matmul(
                out=psF[:, 512:512 + nw], lhsT=t_ones[0:1, 0:128], rhs=tRow[0:1, 1536:1536 + nw], start=True, stop=True),
                reads=[bRow, B('ones')], writes=[pbank[1]])
            p.do('vector', lambda e, nw=nw: e.tensor_copy(out=tRB[:, 0:nw], in_=psF[:, 512:512 + nw]),
                 reads=[pbank[1]], writes=[bRB])
            for c in range(DC):
                p.do('vector', lambda e, c=c, n0=n0, nw=nw: e.tensor_tensor(
                    out=tNrm[:, 0:nw], in0=xc(c)[:, n0:n0 + nw], in1=tMB[:, 0:nw], op=ALU.subtract),
                    reads=[bx, bMB], writes=[bNrm])
                p.do('vector', lambda e, c=c, n0=n0, nw=nw: e.tensor_tensor(
                    out=lnb(c)[:, n0:n0 + nw], in0=tNrm[:, 0:nw], in1=tRB[:, 0:nw], op=ALU.mult),
                    reads=[bNrm, bRB], writes=[blnb])


    p.dma('sync', t_idf[:, :], i_idf[:, :], writes=[B('idf')])
    p.dma('sync', t_idb[:, :], i_idb[:, :], writes=[B('idb')])
    p.dma('sync', t_const[:, :], i_const[:, :], writes=[B('const')])
    p.dma('sync', t_rep16[:, :], i_rep16[:, :], writes=[B('rep16')])
    p.dma('sync', t_ones[:, :], i_ones[:, :], writes=[B('ones')])

    if 'A' in phases:
        arL = AutoArena(nc)
        tLNB = arL.sbuf_tensor("A_LNB", [128, DC * TW], BF16)
        tEskB = arL.sbuf_tensor("A_eskB", [128, 8 * TL], BF16)
        tPib = arL.sbuf_tensor("A_pib", [128, DC], F32)
        tPob = arL.sbuf_tensor("A_pob", [128, DC], F32)
        tDp = arL.sbuf_tensor("A_dp", [128, DC], F32)
        tHm = arL.sbuf_tensor("A_hm", [16, HIST], F32)
        tBwf = arL.sbuf_tensor("A_bwf", [128, DC * S], F32)
        tAf = arL.sbuf_tensor("A_af", [128, DC * S], F32)
        arL.allocate()

        p.dma('sync', tPib[:, :].rearrange("p (c o) -> p c o", o=1), i_pib.rearrange("(c p) o -> p c o", p=128), writes=[B('A_pib')], slow=True)
        p.dma('sync', tPob[:, :].rearrange("p (c o) -> p c o", o=1), i_pob.rearrange("(c p) o -> p c o", p=128), writes=[B('A_pob')], slow=True)
        p.dma('sync', tDp[:, :].rearrange("p (c o) -> p c o", o=1), i_Dp.rearrange("(c p) o -> p c o", p=128), writes=[B('A_dp')], slow=True)
        p.dma('sync', tHm[:, :], i_hm[:, :], writes=[B('A_hm')])
        p.dma('sync', tAf[:, :].rearrange("p (c s) -> p c s", c=DC), i_A.rearrange("(c p) s -> p c s", p=128), writes=[B('A_af')], slow=True)
        p.dma('sync', tBwf[:, :].rearrange("p (c s) -> p c s", c=DC), i_BwT.rearrange("(c p) s -> p c s", p=128), writes=[B('A_bwf')], slow=True)

        ar1 = AutoArena(nc)
        tX = ar1.sbuf_tensor("A_X", [128, DC * TW], F32)
        tSQ = [ar1.sbuf_tensor(f"A_sq{i}", [128, 512], F32) for i in range(2)]
        tRow = ar1.sbuf_tensor("A_row", [1, 512 * 5], F32)
        tMB = ar1.sbuf_tensor("A_mb", [128, 512], F32)
        tRB = ar1.sbuf_tensor("A_rb", [128, 512], F32)
        tNrm = ar1.sbuf_tensor("A_nrm", [128, 512], F32)
        ar1.allocate()

        p.dma('sync', tX[:, :].rearrange("p (c t) -> p c t", c=DC),
              i_x.rearrange("(c p) t -> p c t", p=128), writes=[B('A_X')])
        bLNB = B('A_LNB')

        emit_ln2(tSQ, tRow, tMB, tRB, tNrm,
                 lambda c: tX[:, c * TW:(c + 1) * TW], TW,
                 lambda c: tLNB[:, c * TW:(c + 1) * TW], bLNB, B('A_X'), 'Aln')
        p.fence()
        ar1.free()

        ar2 = AutoArena(nc)
        tXPB = ar2.sbuf_tensor("A_XPB", [128, DC * TW], BF16)
        tP = ar2.sbuf_tensor("A_P", [128, DC * 8 * 128], BF16)
        tWf = [ar2.sbuf_tensor(f"A_wf{i}", [128, D], F32) for i in range(2)]
        tWb = [ar2.sbuf_tensor(f"A_wb{i}", [128, D], BF16) for i in range(2)]
        tSc = [ar2.sbuf_tensor(f"A_sc{i}", [128, TW], F32) for i in range(2)]
        tBwb = ar2.sbuf_tensor("A_bwb", [128, DC * S], BF16)
        tDlt = ar2.sbuf_tensor("A_dlt", [16, TW], F32)
        tDla = ar2.sbuf_tensor("A_dla", [16, TW], F32)
        tDrep = ar2.sbuf_tensor("A_drep", [128, TW], F32)
        tSig = ar2.sbuf_tensor("A_sig", [128, DC * S], F32)
        tPrev = ar2.sbuf_tensor("A_prev", [128, DC * S], F32)
        tEt = [ar2.sbuf_tensor(f"A_et{i}", [128, TW], F32) for i in range(2)]
        tEsk = ar2.sbuf_tensor("A_esk", [128, TL], F32)
        ar2.allocate()

        NBL = [(0, 512), (512, 512), (1024, TW - 1024)]
        bXPB = B('A_XPB')
        for m in range(DC):
            db = m % 2
            bWf = B(f'A_wf{db}'); bWb = B(f'A_wb{db}'); bSc = B(f'A_sc{db}')
            p.dma('sync', tWf[db][:, :].rearrange("p (c k) -> p c k", c=DC),
                  i_piw.rearrange("(c p) mm -> p c mm", p=128)[:, :, m * 128:(m + 1) * 128],
                  writes=[bWf])
            p.do('vector', lambda e, db=db: e.tensor_copy(out=tWb[db][:, :], in_=tWf[db][:, :]),
                 reads=[bWf], writes=[bWb])
            for (n0, nw) in NBL:
                pb = pbank[2 + n0 // 512]
                for c in range(DC):
                    p.do('tensor', lambda e, c=c, n0=n0, nw=nw, db=db: e.matmul(
                        out=psF[:, 1024 + n0: 1024 + n0 + nw],
                        lhsT=tWb[db][:, c * 128:(c + 1) * 128],
                        rhs=tLNB[:, c * TW + n0: c * TW + n0 + nw],
                        start=(c == 0), stop=(c == DC - 1)),
                        reads=[bWb, bLNB], writes=[pb])
                p.do('vector', lambda e, m=m, n0=n0, nw=nw, db=db: e.tensor_scalar(
                    out=tSc[db][:, n0:n0 + nw], in0=psF[:, 1024 + n0:1024 + n0 + nw],
                    scalar1=tPib[:, m:m + 1], scalar2=None, op0=ALU.add),
                    reads=[pb], sreads=[B('A_pib')], writes=[bSc])
            p.do('vector', lambda e, m=m, db=db: e.tensor_copy(
                out=tXPB[:, m * TW:(m + 1) * TW], in_=tSc[db][:, :]),
                reads=[bSc], writes=[bXPB])
            p.dma('sync', d_xp[m * 128:(m + 1) * 128, :], tSc[db][:, :], reads=[bSc], writes=[B('d_xp')])

        if 'Ataps' in dbg:
            for m in range(DC):
                p.dma('gpsimd', dbg_outs['o_xp'][m * 128:(m + 1) * 128, :], d_xp[m * 128:(m + 1) * 128, :],
                      reads=[B('d_xp')], writes=[B('o_xp')])
        bBwb = B('A_bwb')
        p.do('vector', lambda e: e.tensor_copy(out=tBwb[:, :], in_=tBwf[:, :]), reads=[B('A_bwf')], writes=[bBwb])
        bDlt = B('A_dlt'); bDla = B('A_dla')
        for (n0, nw) in NBL:
            pb = pbank[n0 // 512]
            for c in range(DC):
                p.do('tensor', lambda e, c=c, n0=n0, nw=nw: e.matmul(
                    out=psF[0:16, n0:n0 + nw],
                    lhsT=tBwb[:, c * S:(c + 1) * S],
                    rhs=tXPB[:, c * TW + n0: c * TW + n0 + nw],
                    start=(c == 0), stop=(c == DC - 1)),
                    reads=[bBwb, bXPB], writes=[pb])
            p.do('scalar', lambda e, n0=n0, nw=nw: e.activation(out=tDla[:, n0:n0 + nw], in_=psF[0:16, n0:n0 + nw], func=ACT.Abs),
                 reads=[pb], writes=[bDla])
            p.do('scalar', lambda e, n0=n0, nw=nw: e.activation(out=tDla[:, n0:n0 + nw], in_=tDla[:, n0:n0 + nw], func=ACT.Exp, scale=-1.0),
                 reads=[bDla], writes=[bDla])
            p.do('scalar', lambda e, n0=n0, nw=nw: e.activation(out=tDla[:, n0:n0 + nw], in_=tDla[:, n0:n0 + nw], func=ACT.Ln, bias=1.0, scale=1.0),
                 reads=[bDla], writes=[bDla])
            p.do('scalar', lambda e, n0=n0, nw=nw: e.activation(out=tDlt[:, n0:n0 + nw], in_=psF[0:16, n0:n0 + nw], func=ACT.Relu),
                 reads=[pb], writes=[bDlt])
            p.do('vector', lambda e, n0=n0, nw=nw: e.tensor_tensor(out=tDlt[:, n0:n0 + nw], in0=tDlt[:, n0:n0 + nw], in1=tDla[:, n0:n0 + nw], op=ALU.add),
                 reads=[bDla], writes=[bDlt])
        p.do('vector', lambda e: e.tensor_tensor(out=tDlt[:, 0:HIST], in0=tDlt[:, 0:HIST], in1=tHm[:, :], op=ALU.mult),
             reads=[B('A_hm')], writes=[bDlt])
        bDrep = B('A_drep')
        for (n0, nw) in NBL:
            pb = pbank[3 + n0 // 512]
            p.do('tensor', lambda e, n0=n0, nw=nw: e.matmul(
                out=psF[:, 1536 + n0:1536 + n0 + nw], lhsT=t_rep16[:, :], rhs=tDlt[:, n0:n0 + nw],
                start=True, stop=True),
                reads=[B('rep16'), bDlt], writes=[pb])
            p.do('vector', lambda e, n0=n0, nw=nw: e.tensor_copy(out=tDrep[:, n0:n0 + nw], in_=psF[:, 1536 + n0:1536 + n0 + nw]),
                 reads=[pb], writes=[bDrep])

        if 'Ataps' in dbg:
            p.dma('gpsimd', dbg_outs['o_dlt'][:, :], tDlt[:, :], reads=[bDlt], writes=[B('o_dlt')])
        bSig = B('A_sig'); bPrev = B('A_prev'); bP = B('A_P')
        p.do('scalar', lambda e: e.activation(out=tSig[:, :], in_=tAf[:, :], func=ACT.Sigmoid), reads=[B('A_af')], writes=[bSig])
        p.do('gpsimd', lambda e: e.memset(tPrev[:, :], 1.0), writes=[bPrev])
        for l in range(LF):
            j, jj = l // 8, l % 8
            p.do('vector', lambda e, j=j, jj=jj: e.tensor_copy(
                out=tP[:, :].rearrange("p (c q) -> p c q", q=8 * 128)[:, :, j * 128 + jj * 16: j * 128 + jj * 16 + 16],
                in_=tPrev[:, :].rearrange("p (c s) -> p c s", c=DC)),
                reads=[bPrev], writes=[bP])
            if l < LF - 1:
                p.do('vector', lambda e: e.tensor_tensor(out=tPrev[:, :], in0=tPrev[:, :], in1=tSig[:, :], op=ALU.mult),
                     reads=[bSig], writes=[bPrev])

        bEskB = B('A_eskB')
        for j in range(8):
            db = j % 2
            bEt = B(f'A_et{db}'); bEsk = B('A_esk')
            for (n0, nw) in NBL:
                pb = pbank[3 + n0 // 512]
                for c in range(DC):
                    p.do('tensor', lambda e, c=c, j=j, n0=n0, nw=nw: e.matmul(
                        out=psF[:, 1536 + n0:1536 + n0 + nw],
                        lhsT=tP[:, (c * 8 + j) * 128:(c * 8 + j + 1) * 128],
                        rhs=tXPB[:, c * TW + n0:c * TW + n0 + nw],
                        start=(c == 0), stop=(c == DC - 1)),
                        reads=[bP, bXPB], writes=[pb])
                p.do('vector', lambda e, n0=n0, nw=nw, db=db: e.tensor_tensor(
                    out=tEt[db][:, n0:n0 + nw], in0=psF[:, 1536 + n0:1536 + n0 + nw],
                    in1=tDrep[:, n0:n0 + nw], op=ALU.mult),
                    reads=[pb, bDrep], writes=[bEt])
            for jj in range(8):
                l = 8 * j + jj
                p.dma('gpsimd', tEsk[16 * jj:16 * (jj + 1), :],
                      tEt[db][16 * jj:16 * (jj + 1), HIST - l: HIST - l + TL],
                      reads=[bEt], writes=[bEsk])
            p.do('vector', lambda e, j=j: e.tensor_copy(out=tEskB[:, j * TL:(j + 1) * TL], in_=tEsk[:, :]),
                 reads=[bEsk], writes=[bEskB])
        if 'Ataps' in dbg:
            p.dma('gpsimd', dbg_outs['o_esk'][:, :], tEskB[:, :], reads=[bEskB], writes=[B('o_esk')])
        p.fence()
        ar2.free()

        ar3 = AutoArena(nc)
        tC2f = ar3.sbuf_tensor("A_c2f", [128, D], F32)
        tC2b = ar3.sbuf_tensor("A_c2b", [128, 8 * D], BF16)
        tYB = ar3.sbuf_tensor("A_yb", [128, DC * TL], BF16)
        tWf3 = [ar3.sbuf_tensor(f"A3_wf{i}", [128, D], F32) for i in range(2)]
        tWb3 = [ar3.sbuf_tensor(f"A3_wb{i}", [128, D], BF16) for i in range(2)]
        tSc3 = [ar3.sbuf_tensor(f"A3_sc{i}", [128, TL], F32) for i in range(2)]
        tXPm = [ar3.sbuf_tensor(f"A3_xpm{i}", [128, TL], F32) for i in range(2)]
        tXm = [ar3.sbuf_tensor(f"A3_xm{i}", [128, TL], F32) for i in range(2)]
        ar3.allocate()

        bC2b = B('A_c2b')
        for k in range(8):
            bc2f = B('A_c2f')
            p.dma('sync', tC2f[:, :], i_C2[k * 128:(k + 1) * 128, :], writes=[bc2f])
            p.do('vector', lambda e, k=k: e.tensor_copy(out=tC2b[:, k * D:(k + 1) * D], in_=tC2f[:, :]),
                 reads=[bc2f], writes=[bC2b])
        bYB = B('A_yb')
        for m in range(DC):
            db = m % 2
            bXPm = B(f'A3_xpm{db}'); bSc = B(f'A3_sc{db}')
            p.dma('sync', tXPm[db][:, :], d_xp[m * 128:(m + 1) * 128, HIST:TW], reads=[B('d_xp')], writes=[bXPm])
            p.do('vector', lambda e, m=m, db=db: e.tensor_scalar(
                out=tXPm[db][:, :], in0=tXPm[db][:, :], scalar1=tDp[:, m:m + 1], scalar2=None, op0=ALU.mult),
                sreads=[B('A_dp')], writes=[bXPm])
            for nn in range(2):
                pb = pbank[nn]
                for k in range(8):
                    p.do('tensor', lambda e, k=k, m=m, nn=nn: e.matmul(
                        out=psF[:, nn * 512:(nn + 1) * 512],
                        lhsT=tC2b[:, k * D + m * 128: k * D + (m + 1) * 128],
                        rhs=tEskB[:, k * TL + nn * 512: k * TL + (nn + 1) * 512],
                        start=(k == 0), stop=(k == 7)),
                        reads=[bC2b, bEskB], writes=[pb])
                p.do('vector', lambda e, m=m, nn=nn, db=db: e.tensor_tensor(
                    out=tSc3[db][:, nn * 512:(nn + 1) * 512], in0=psF[:, nn * 512:(nn + 1) * 512],
                    in1=tXPm[db][:, nn * 512:(nn + 1) * 512], op=ALU.add),
                    reads=[pbank[nn], bXPm], writes=[bSc])
            p.do('vector', lambda e, m=m, db=db: e.tensor_copy(
                out=tYB[:, m * TL:(m + 1) * TL], in_=tSc3[db][:, :]),
                reads=[bSc], writes=[bYB])

        if 'Ataps' in dbg:
            p.dma('gpsimd', dbg_outs['o_yb'][:, :], tYB[:, :], reads=[bYB], writes=[B('o_yb')])
        for m in range(DC):
            db = m % 2
            bWf = B(f'A3_wf{db}'); bWb = B(f'A3_wb{db}'); bXm = B(f'A3_xm{db}'); bSc = B(f'A3_sc{db}')
            p.dma('sync', tWf3[db][:, :].rearrange("p (c k) -> p c k", c=DC),
                  i_pow.rearrange("(c p) mm -> p c mm", p=128)[:, :, m * 128:(m + 1) * 128],
                  writes=[bWf])
            p.do('vector', lambda e, db=db: e.tensor_copy(out=tWb3[db][:, :], in_=tWf3[db][:, :]),
                 reads=[bWf], writes=[bWb])
            p.dma('gpsimd', tXm[db][:, :], i_x[m * 128:(m + 1) * 128, HIST:TW], writes=[bXm])
            for nn in range(2):
                pb = pbank[2 + nn]
                for c in range(DC):
                    p.do('tensor', lambda e, c=c, nn=nn, db=db: e.matmul(
                        out=psF[:, 1024 + nn * 512:1024 + (nn + 1) * 512],
                        lhsT=tWb3[db][:, c * 128:(c + 1) * 128],
                        rhs=tYB[:, c * TL + nn * 512: c * TL + (nn + 1) * 512],
                        start=(c == 0), stop=(c == DC - 1)),
                        reads=[bWb, bYB], writes=[pb])
                p.do('vector', lambda e, m=m, nn=nn, db=db: e.tensor_scalar(
                    out=tSc3[db][:, nn * 512:(nn + 1) * 512], in0=psF[:, 1024 + nn * 512:1024 + (nn + 1) * 512],
                    scalar1=tPob[:, m:m + 1], scalar2=None, op0=ALU.add),
                    reads=[pb], sreads=[B('A_pob')], writes=[bSc])
                p.do('vector', lambda e, m=m, nn=nn, db=db: e.tensor_tensor(
                    out=tSc3[db][:, nn * 512:(nn + 1) * 512], in0=tSc3[db][:, nn * 512:(nn + 1) * 512],
                    in1=tXm[db][:, nn * 512:(nn + 1) * 512], op=ALU.add),
                    reads=[bXm], writes=[bSc])
                p.do('vector', lambda e, m=m, nn=nn, db=db: e.tensor_tensor(
                    out=tSc3[db][:, nn * 512:(nn + 1) * 512], in0=tSc3[db][:, nn * 512:(nn + 1) * 512],
                    in1=tLNB[:, m * TW + HIST + nn * 512: m * TW + HIST + (nn + 1) * 512], op=ALU.add),
                    reads=[bLNB], writes=[bSc])
            p.dma('sync', d_s1[m * 128:(m + 1) * 128, :], tSc3[db][:, :], reads=[bSc], writes=[B('d_s1')])

        if 'A' in dbg:
            for m in range(DC):
                p.dma('gpsimd', dbg_outs['o_s1'][m * 128:(m + 1) * 128, :], d_s1[m * 128:(m + 1) * 128, :],
                      reads=[B('d_s1')], writes=[B('o_s1')])
        p.fence()
        ar3.free()
        arL.free()

    HC = HID // 128
    if 'B' in phases:
        arBL = AutoArena(nc)
        tX1B = arBL.sbuf_tensor("B_x1b", [128, DC * TL], BF16)
        tSG = arBL.sbuf_tensor("B_sg", [128, HC], F32)
        tSU = arBL.sbuf_tensor("B_su", [128, HC], F32)
        tSD = arBL.sbuf_tensor("B_sd", [128, DC], F32)
        arBL.allocate()
        p.dma('sync', tSG[:, :].rearrange("p (c o) -> p c o", o=1), i_sg.rearrange("(c p) o -> p c o", p=128), writes=[B('B_sg')], slow=True)
        p.dma('sync', tSU[:, :].rearrange("p (c o) -> p c o", o=1), i_su.rearrange("(c p) o -> p c o", p=128), writes=[B('B_su')], slow=True)
        p.dma('sync', tSD[:, :].rearrange("p (c o) -> p c o", o=1), i_sd2.rearrange("(c p) o -> p c o", p=128), writes=[B('B_sd')], slow=True)

        arB1 = AutoArena(nc)
        tX1 = arB1.sbuf_tensor("B_x1", [128, DC * TL], F32)
        tSQb = [arB1.sbuf_tensor(f"B_sq{i}", [128, 512], F32) for i in range(2)]
        tRowb = arB1.sbuf_tensor("B_row", [1, 512 * 5], F32)
        tMBb = arB1.sbuf_tensor("B_mb", [128, 512], F32)
        tRBb = arB1.sbuf_tensor("B_rb", [128, 512], F32)
        tNrmb = arB1.sbuf_tensor("B_nrm", [128, 512], F32)
        arB1.allocate()
        p.dma('sync', tX1[:, :].rearrange("p (c t) -> p c t", c=DC),
              d_s1.rearrange("(c p) t -> p c t", p=128), reads=[B('d_s1')], writes=[B('B_x1')])
        emit_ln2(tSQb, tRowb, tMBb, tRBb, tNrmb,
                 lambda c: tX1[:, c * TL:(c + 1) * TL], TL,
                 lambda c: tX1B[:, c * TL:(c + 1) * TL], B('B_x1b'), B('B_x1'), 'Bln')
        p.fence()
        arB1.free()

        arB2 = AutoArena(nc)
        tHH = arB2.sbuf_tensor("B_hh", [128, HC * TL], BF16)
        tIW = [arB2.sbuf_tensor(f"B_iw{i}", [128, D], I32) for i in range(2)]
        tIU = [arB2.sbuf_tensor(f"B_iu{i}", [128, D], I32) for i in range(2)]
        tGW = [arB2.sbuf_tensor(f"B_gw{i}", [128, D], BF16) for i in range(2)]
        tUW = [arB2.sbuf_tensor(f"B_uw{i}", [128, D], BF16) for i in range(2)]
        tGb = arB2.sbuf_tensor("B_gb", [128, TL], BF16)
        tUb = arB2.sbuf_tensor("B_ub", [128, TL], BF16)
        arB2.allocate()
        bHH = B('B_hh')
        for hc in range(HC):
            db = hc % 2
            bIW = B(f'B_iw{db}'); bIU = B(f'B_iu{db}'); bGW = B(f'B_gw{db}'); bUW = B(f'B_uw{db}')
            p.dma('sync', tIW[db][:, :].rearrange("p (c k) -> p c k", c=DC),
                  i_wg.rearrange("(c p) mm -> p c mm", p=128)[:, :, hc * 128:(hc + 1) * 128], writes=[bIW])
            p.dma('gpsimd', tIU[db][:, :].rearrange("p (c k) -> p c k", c=DC),
                  i_wu.rearrange("(c p) mm -> p c mm", p=128)[:, :, hc * 128:(hc + 1) * 128], writes=[bIU])
            p.do('vector', lambda e, db=db: e.tensor_scalar(out=tGW[db][:, :], in0=tIW[db][:, :],
                 scalar1=-128.0, scalar2=None, op0=ALU.add), reads=[bIW], writes=[bGW])
            p.do('vector', lambda e, db=db: e.tensor_scalar(out=tUW[db][:, :], in0=tIU[db][:, :],
                 scalar1=-128.0, scalar2=None, op0=ALU.add), reads=[bIU], writes=[bUW])
            for nn in range(2):
                pbg = pbank[nn]; pbu = pbank[2 + nn]
                for c in range(DC):
                    p.do('tensor', lambda e, c=c, nn=nn, db=db: e.matmul(
                        out=psF[:, nn * 512:(nn + 1) * 512],
                        lhsT=tGW[db][:, c * 128:(c + 1) * 128],
                        rhs=tX1B[:, c * TL + nn * 512: c * TL + (nn + 1) * 512],
                        start=(c == 0), stop=(c == DC - 1)),
                        reads=[bGW, B('B_x1b')], writes=[pbg])
                    p.do('tensor', lambda e, c=c, nn=nn, db=db: e.matmul(
                        out=psF[:, 1024 + nn * 512:1024 + (nn + 1) * 512],
                        lhsT=tUW[db][:, c * 128:(c + 1) * 128],
                        rhs=tX1B[:, c * TL + nn * 512: c * TL + (nn + 1) * 512],
                        start=(c == 0), stop=(c == DC - 1)),
                        reads=[bUW, B('B_x1b')], writes=[pbu])
                p.do('scalar', lambda e, hc=hc, nn=nn: e.activation(
                    out=tGb[:, nn * 512:(nn + 1) * 512], in_=psF[:, nn * 512:(nn + 1) * 512],
                    func=ACT.Silu, scale=tSG[:, hc:hc + 1]),
                    reads=[pbg], sreads=[B('B_sg')], writes=[B('B_gb')])
                p.do('scalar', lambda e, hc=hc, nn=nn: e.activation(
                    out=tUb[:, nn * 512:(nn + 1) * 512], in_=psF[:, 1024 + nn * 512:1024 + (nn + 1) * 512],
                    func=ACT.Copy, scale=tSU[:, hc:hc + 1]),
                    reads=[pbu], sreads=[B('B_su')], writes=[B('B_ub')])
                p.do('vector', lambda e, hc=hc, nn=nn: e.tensor_tensor(
                    out=tHH[:, hc * TL + nn * 512: hc * TL + (nn + 1) * 512],
                    in0=tGb[:, nn * 512:(nn + 1) * 512], in1=tUb[:, nn * 512:(nn + 1) * 512], op=ALU.mult),
                    reads=[B('B_gb'), B('B_ub')], writes=[bHH])
        p.fence()
        arB2_weights_done = True
        arB3 = AutoArena(nc)
        tID = [arB3.sbuf_tensor(f"B_id{i}", [128, 256], I32) for i in range(2)]
        tDW = [arB3.sbuf_tensor(f"B_dw{i}", [128, 256], BF16) for i in range(2)]
        tRes = [arB3.sbuf_tensor(f"B_res{i}", [128, TL], F32) for i in range(2)]
        tSc2 = [arB3.sbuf_tensor(f"B_sc{i}", [128, TL], F32) for i in range(2)]
        arB3.allocate()
        for mg in range(8):
            for k in range(HC):
                db = k % 2
                bID = B(f'B_id{db}'); bDW = B(f'B_dw{db}')
                p.dma('sync', tID[db][:, :], i_wd[k * 128:(k + 1) * 128, mg * 256:(mg + 1) * 256], writes=[bID])
                p.do('vector', lambda e, db=db: e.tensor_scalar(out=tDW[db][:, :], in0=tID[db][:, :],
                     scalar1=-128.0, scalar2=None, op0=ALU.add), reads=[bID], writes=[bDW])
                for mm in range(2):
                    for nn in range(2):
                        pb = pbank[mm * 2 + nn]
                        p.do('tensor', lambda e, k=k, mm=mm, nn=nn, db=db: e.matmul(
                            out=psF[:, (mm * 2 + nn) * 512:(mm * 2 + nn + 1) * 512],
                            lhsT=tDW[db][:, mm * 128:(mm + 1) * 128],
                            rhs=tHH[:, k * TL + nn * 512: k * TL + (nn + 1) * 512],
                            start=(k == 0), stop=(k == HC - 1)),
                            reads=[bDW, bHH], writes=[pb])
            for mm in range(2):
                m = mg * 2 + mm
                db = mm % 2
                bRes = B(f'B_res{db}'); bSc = B(f'B_sc{db}')
                p.dma('gpsimd', tRes[db][:, :], d_s1[m * 128:(m + 1) * 128, :], reads=[B('d_s1')], writes=[bRes])
                for nn in range(2):
                    pb = pbank[mm * 2 + nn]
                    p.do('vector', lambda e, m=m, mm=mm, nn=nn, db=db: e.tensor_scalar(
                        out=tSc2[db][:, nn * 512:(nn + 1) * 512],
                        in0=psF[:, (mm * 2 + nn) * 512:(mm * 2 + nn + 1) * 512],
                        scalar1=tSD[:, m:m + 1], scalar2=None, op0=ALU.mult),
                        reads=[pb], sreads=[B('B_sd')], writes=[bSc])
                    p.do('vector', lambda e, nn=nn, db=db: e.tensor_tensor(
                        out=tSc2[db][:, nn * 512:(nn + 1) * 512], in0=tSc2[db][:, nn * 512:(nn + 1) * 512],
                        in1=tRes[db][:, nn * 512:(nn + 1) * 512], op=ALU.add),
                        reads=[bRes], writes=[bSc])
                p.dma('sync', d_s2[m * 128:(m + 1) * 128, :], tSc2[db][:, :], reads=[bSc], writes=[B('d_s2')])
        if 'B' in dbg:
            for m in range(DC):
                p.dma('gpsimd', dbg_outs['o_s2'][m * 128:(m + 1) * 128, :], d_s2[m * 128:(m + 1) * 128, :],
                      reads=[B('d_s2')], writes=[B('o_s2')])
        p.fence()
        arB3.free()
        arB2.free()
        arBL.free()

    if 'C' in phases:
        arCL = AutoArena(nc)
        tX4B = arCL.sbuf_tensor("C_x4b", [128, DC * TL], BF16)
        tCos = arCL.sbuf_tensor("C_cos", [128, TL], F32)
        tSin = arCL.sbuf_tensor("C_sin", [128, TL], F32)
        tKM = arCL.sbuf_tensor("C_km", [128, NH], F32)
        tNegM = arCL.sbuf_tensor("C_negm", [128, NH], F32)
        tRowC = arCL.sbuf_tensor("C_rowc", [1, 64], F32)
        arCL.allocate()
        p.dma('sync', tCos[:, :], i_cos[:, :], writes=[B('C_cos')])
        p.dma('sync', tSin[:, :], i_sin[:, :], writes=[B('C_sin')])

        arCx = AutoArena(nc)
        tX2LN = arCx.sbuf_tensor("C_x2ln", [128, DC * TL], F32)
        arCx.allocate()

        arC1 = AutoArena(nc)
        tX2 = arC1.sbuf_tensor("C_x2", [128, DC * TL], F32)
        tSQc = [arC1.sbuf_tensor(f"C_sq{i}", [128, 512], F32) for i in range(2)]
        tRowc = arC1.sbuf_tensor("C_row", [1, 512 * 5], F32)
        tMBc = arC1.sbuf_tensor("C_mb", [128, 512], F32)
        tRBc = arC1.sbuf_tensor("C_rb", [128, 512], F32)
        tNrmc = arC1.sbuf_tensor("C_nrm", [128, 512], F32)
        arC1.allocate()
        p.dma('sync', tX2[:, :].rearrange("p (c t) -> p c t", c=DC),
              d_s2.rearrange("(c p) t -> p c t", p=128), reads=[B('d_s2')], writes=[B('C_x2')])
        emit_ln2(tSQc, tRowc, tMBc, tRBc, tNrmc,
                 lambda c: tX2[:, c * TL:(c + 1) * TL], TL,
                 lambda c: tX2LN[:, c * TL:(c + 1) * TL], B('C_x2ln'), B('C_x2'), 'Cln')
        p.fence()
        arC1.free()

        arC1b = AutoArena(nc)
        tCM = arC1b.sbuf_tensor("C_cm", [128, TL], F32)
        tCMT = arC1b.sbuf_tensor("C_cmt", [128, 8 * 128], F32)
        tRM = arC1b.sbuf_tensor("C_rm", [128, 16], F32)
        tSrow = arC1b.sbuf_tensor("C_srow", [1, 2 * TL], F32)
        tSB4 = arC1b.sbuf_tensor("C_sb4", [128, TL], F32)
        tRSB4 = arC1b.sbuf_tensor("C_rsb4", [128, TL], F32)
        tQT = arC1b.sbuf_tensor("C_qt", [128, TL], F32)
        tQI = arC1b.sbuf_tensor("C_qi", [128, TL], I32)
        arC1b.allocate()
        bCM = B('C_cm'); bCMT = B('C_cmt'); bRM = B('C_rm'); bSrow = B('C_srow')
        bSB4 = B('C_sb4'); bRSB4 = B('C_rsb4'); bQT = B('C_qt'); bQI = B('C_qi')
        bX2LN = B('C_x2ln')
        p.do('scalar', lambda e: e.activation(out=tCM[:, :], in_=tX2LN[:, 0:TL], func=ACT.Abs),
             reads=[bX2LN], writes=[bCM])
        for c in range(1, DC):
            p.do('scalar', lambda e, c=c: e.activation(out=tQT[:, :], in_=tX2LN[:, c * TL:(c + 1) * TL], func=ACT.Abs),
                 reads=[bX2LN], writes=[bQT])
            p.do('vector', lambda e, c=c: e.tensor_tensor(out=tCM[:, :], in0=tQT[:, :], in1=tCM[:, :], op=ALU.max),
                 reads=[bQT], writes=[bCM])
        for tt in range(8):
            p.do('tensor', lambda e, tt=tt: e.transpose(
                out=psF[:, 3072:3072 + 128], in_=tCM[:, tt * 128:(tt + 1) * 128], identity=t_idf[:, :]),
                reads=[bCM, B('idf')], writes=[pbank[6]])
            p.do('vector', lambda e, tt=tt: e.tensor_copy(out=tCMT[:, tt * 128:(tt + 1) * 128], in_=psF[:, 3072:3072 + 128]),
                 reads=[pbank[6]], writes=[bCMT])
        p.do('vector', lambda e: e.tensor_reduce(out=tRM[:, 0:8], in_=tCMT[:, :].rearrange("p (t k) -> p t k", t=8),
             axis=AX.X, op=ALU.max), reads=[bCMT], writes=[bRM])
        p.pad('v', 2)
        p.do('vector', lambda e: e.tensor_scalar(out=tRM[:, 8:16], in0=tRM[:, 0:8], scalar1=EPS_LN, scalar2=None, op0=ALU.max),
             reads=[bRM], writes=[bRM])
        p.pad('v', 2)
        p.do('vector', lambda e: e.reciprocal(out=tRM[:, 0:8], in_=tRM[:, 8:16]), reads=[bRM], writes=[bRM])
        p.pad('v', 2)
        p.do('vector', lambda e: e.tensor_scalar(out=tRM[:, 8:16], in0=tRM[:, 0:8], scalar1=7.0, scalar2=None, op0=ALU.mult),
             reads=[bRM], writes=[bRM])
        p.pad('v', 2)
        p.do('vector', lambda e: e.reciprocal(out=tRM[:, 0:8], in_=tRM[:, 8:16]), reads=[bRM], writes=[bRM])
        p.pad('v', 2)
        p.dma('gpsimd', d_rm[:, :], tRM[:, :], reads=[bRM], writes=[B('d_rm')])
        p.dma('gpsimd', tSrow[0:1, :].rearrange("o (x t q) -> o x t q", x=2, t=8),
              d_rm.rearrange("(o q) (x t) -> o x t q", o=1, x=2), reads=[B('d_rm')], writes=[bSrow], slow=True)
        for (dst, bdst, off) in ((tSB4, bSB4, TL), (tRSB4, bRSB4, 0)):
            for g in range(2):
                p.do('tensor', lambda e, off=off, g=g: e.matmul(
                    out=psF[:, g * 512:(g + 1) * 512], lhsT=t_ones[0:1, 0:128],
                    rhs=tSrow[0:1, off + g * 512: off + (g + 1) * 512], start=True, stop=True),
                    reads=[bSrow, B('ones')], writes=[pbank[g]])
                p.do('vector', lambda e, dst=dst, g=g: e.tensor_copy(out=dst[:, g * 512:(g + 1) * 512], in_=psF[:, g * 512:(g + 1) * 512]),
                     reads=[pbank[g]], writes=[bdst])
        bX4B = B('C_x4b')
        for c in range(DC):
            p.do('vector', lambda e, c=c: e.tensor_tensor(out=tQT[:, :], in0=tX2LN[:, c * TL:(c + 1) * TL], in1=tSB4[:, :], op=ALU.mult),
                 reads=[bX2LN, bSB4], writes=[bQT])
            p.do('vector', lambda e: e.tensor_copy(out=tQI[:, :], in_=tQT[:, :]), reads=[bQT], writes=[bQI])
            p.do('vector', lambda e: e.tensor_copy(out=tQT[:, :], in_=tQI[:, :]), reads=[bQI], writes=[bQT])
            p.do('vector', lambda e, c=c: e.tensor_tensor(out=tX4B[:, c * TL:(c + 1) * TL], in0=tQT[:, :], in1=tRSB4[:, :], op=ALU.mult),
                 reads=[bQT, bRSB4], writes=[bX4B])
        if 'Ctaps' in dbg:
            p.dma('gpsimd', dbg_outs['o_x2ln'][:, :], tX2LN[:, :], reads=[B('C_x2ln')], writes=[B('o_x2ln')])
            p.dma('gpsimd', dbg_outs['o_cm'][:, :], tCM[:, :], reads=[bCM], writes=[B('o_cm')])
            p.dma('gpsimd', dbg_outs['o_rm'][:, :], tRM[:, :], reads=[bRM], writes=[B('o_rm')])
            p.dma('gpsimd', dbg_outs['o_srow'][:, :], tSrow[:, :], reads=[bSrow], writes=[B('o_srow')])
            p.dma('gpsimd', dbg_outs['o_sb4'][:, :], tSB4[:, :], reads=[bSB4], writes=[B('o_sb4')])
            p.dma('gpsimd', dbg_outs['o_rsb4'][:, :], tRSB4[:, :], reads=[bRSB4], writes=[B('o_rsb4')])
        p.fence()
        arC1b.free()
        arCx.free()
        arCL2 = AutoArena(nc)
        tAO = arCL2.sbuf_tensor("C_ao", [128, 8 * D], BF16)
        arCL2.allocate()
        arC2a = AutoArena(nc)
        tQB = arC2a.sbuf_tensor("C_qb", [128, NH * TL], BF16)
        arC2a.allocate()
        arC2 = AutoArena(nc)
        tWfc = [arC2.sbuf_tensor(f"C_wf{i}", [128, D], F32) for i in range(2)]
        tWbc = [arC2.sbuf_tensor(f"C_wb{i}", [128, D], BF16) for i in range(2)]
        tKf = arC2.sbuf_tensor("C_kf", [128, TL], F32)
        tRot = arC2.sbuf_tensor("C_rot", [128, TL], F32)
        tT1 = arC2.sbuf_tensor("C_t1", [128, TL], F32)
        tT2 = arC2.sbuf_tensor("C_t2", [128, TL], F32)
        tKb = arC2.sbuf_tensor("C_kb", [128, TL], BF16)
        tVb = arC2.sbuf_tensor("C_vb", [128, TL], BF16)
        tVt = arC2.sbuf_tensor("C_vt", [128, 128], BF16)
        tAbsC = arC2.sbuf_tensor("C_absc", [128, D], F32)
        tWiC = arC2.sbuf_tensor("C_wic", [128, D], I32)
        tBR3 = arC2.sbuf_tensor("C_br3", [128, 8], F32)
        arC2.allocate()
        bQB = B('C_qb'); bKf = B('C_kf'); bRot = B('C_rot'); bT1 = B('C_t1'); bT2 = B('C_t2')
        bKb = B('C_kb'); bVb = B('C_vb'); bVt = B('C_vt'); bKM = B('C_km')
        bAbsC = B('C_absc'); bWiC = B('C_wic'); bBR3 = B('C_br3'); bRowCm = B('C_rowc')
        ISQ = 0.08838834764831845

        def emit_tern_scale(iw, wfl, bwfl, brcol):
            for m in range(DC):
                db = m % 2
                bWf = bwfl[db]
                p.dma('sync', wfl[db][:, :].rearrange("p (c k) -> p c k", c=DC),
                      iw.rearrange("(c p) mm -> p c mm", p=128)[:, :, m * 128:(m + 1) * 128], writes=[bWf])
                p.do('scalar', lambda e, db=db: e.activation(out=tAbsC[:, :], in_=wfl[db][:, :], func=ACT.Abs),
                     reads=[bWf], writes=[bAbsC])
                for g in range(4):
                    p.do('tensor', lambda e, m=m, g=g: e.matmul(
                        out=psF[0:1, 3072:3072 + 512], lhsT=t_ones[:, 0:1], rhs=tAbsC[:, g * 512:(g + 1) * 512],
                        start=(m == 0 and g == 0), stop=(m == DC - 1 and g == 3)),
                        reads=[bAbsC, B('ones')], writes=[pbank[6]])
            p.do('vector', lambda e: e.tensor_reduce(out=tRowC[0:1, 40:41], in_=psF[0:1, 3072:3072 + 512],
                 axis=AX.X, op=ALU.add), reads=[pbank[6]], writes=[bRowCm])
            p.pad('v', 2)
            p.do('vector', lambda e: e.tensor_scalar(out=tRowC[0:1, 41:42], in0=tRowC[0:1, 40:41],
                 scalar1=1.0 / (D * D), scalar2=None, op0=ALU.mult), reads=[bRowCm], writes=[bRowCm])
            p.pad('v', 2)
            p.do('vector', lambda e: e.tensor_scalar(out=tRowC[0:1, 42:43], in0=tRowC[0:1, 41:42],
                 scalar1=EPS_LN, scalar2=None, op0=ALU.add), reads=[bRowCm], writes=[bRowCm])
            p.pad('v', 2)
            p.do('vector', lambda e: e.reciprocal(out=tRowC[0:1, 43:44], in_=tRowC[0:1, 42:43]),
                 reads=[bRowCm], writes=[bRowCm])
            p.pad('v', 2)
            p.do('tensor', lambda e: e.matmul(out=psF[:, 3072:3072 + 1], lhsT=t_ones[0:1, 0:128],
                 rhs=tRowC[0:1, 41:42], start=True, stop=True), reads=[bRowCm, B('ones')], writes=[pbank[6]])
            p.do('tensor', lambda e: e.matmul(out=psF[:, 3073:3073 + 1], lhsT=t_ones[0:1, 0:128],
                 rhs=tRowC[0:1, 43:44], start=True, stop=True), reads=[bRowCm, B('ones')], writes=[pbank[6]])
            p.do('vector', lambda e, brcol=brcol: e.tensor_copy(out=tBR3[:, brcol:brcol + 2], in_=psF[:, 3072:3074]),
                 reads=[pbank[6]], writes=[bBR3])
            p.pad('v', 2)

        def rope_into(dst_ap, bdst, scale):
            p.do('vector', lambda e: e.tensor_scalar(out=tRot[0:64, :], in0=tKf[64:128, :], scalar1=-1.0, scalar2=None, op0=ALU.mult),
                 reads=[bKf], writes=[bRot])
            p.do('vector', lambda e: e.tensor_scalar(out=tRot[64:128, :], in0=tKf[0:64, :], scalar1=1.0, scalar2=None, op0=ALU.mult),
                 reads=[bKf], writes=[bRot])
            p.do('vector', lambda e: e.tensor_tensor(out=tT1[:, :], in0=tKf[:, :], in1=tCos[:, :], op=ALU.mult),
                 reads=[bKf, B('C_cos')], writes=[bT1])
            p.do('vector', lambda e: e.tensor_tensor(out=tT2[:, :], in0=tRot[:, :], in1=tSin[:, :], op=ALU.mult),
                 reads=[bRot, B('C_sin')], writes=[bT2])
            p.do('vector', lambda e, dst_ap=dst_ap: e.tensor_tensor(out=dst_ap, in0=tT1[:, :], in1=tT2[:, :], op=ALU.add),
                 reads=[bT1, bT2], writes=[bdst])

        emit_tern_scale(i_wq, tWfc, [B('C_wf0'), B('C_wf1')], 0)
        emit_tern_scale(i_wk, tWfc, [B('C_wf0'), B('C_wf1')], 2)
        emit_tern_scale(i_wv, tWfc, [B('C_wf0'), B('C_wf1')], 4)
        for pj, (proj, iw) in enumerate((('q', i_wq), ('k', i_wk), ('v', i_wv))):
            for m in range(NH):
                db = m % 2
                bWf = B(f'C_wf{db}'); bWb = B(f'C_wb{db}')
                p.dma('sync', tWfc[db][:, :].rearrange("p (c k) -> p c k", c=DC),
                      iw.rearrange("(c p) mm -> p c mm", p=128)[:, :, m * 128:(m + 1) * 128], writes=[bWf])
                p.do('vector', lambda e, db=db, brcol=2 * pj: e.tensor_scalar(out=tAbsC[:, :], in0=tWfc[db][:, :],
                     scalar1=tBR3[:, brcol + 1:brcol + 2], scalar2=None, op0=ALU.mult),
                     reads=[bWf], sreads=[bBR3], writes=[bAbsC])
                p.do('vector', lambda e: e.tensor_copy(out=tWiC[:, :], in_=tAbsC[:, :]), reads=[bAbsC], writes=[bWiC])
                p.do('vector', lambda e: e.tensor_copy(out=tAbsC[:, :], in_=tWiC[:, :]), reads=[bWiC], writes=[bAbsC])
                p.do('vector', lambda e: e.tensor_scalar(out=tAbsC[:, :], in0=tAbsC[:, :],
                     scalar1=1.0, scalar2=-1.0, op0=ALU.min, op1=ALU.max), writes=[bAbsC])
                p.do('vector', lambda e, db=db, brcol=2 * pj: e.tensor_scalar(out=tWbc[db][:, :], in0=tAbsC[:, :],
                     scalar1=tBR3[:, brcol:brcol + 1], scalar2=None, op0=ALU.mult),
                     reads=[bAbsC], sreads=[bBR3], writes=[bWb])
                for nn in range(2):
                    pb = pbank[4 + nn]
                    for c in range(DC):
                        p.do('tensor', lambda e, c=c, nn=nn, db=db: e.matmul(
                            out=psF[:, 2048 + nn * 512:2048 + (nn + 1) * 512],
                            lhsT=tWbc[db][:, c * 128:(c + 1) * 128],
                            rhs=tX4B[:, c * TL + nn * 512: c * TL + (nn + 1) * 512],
                            start=(c == 0), stop=(c == DC - 1)),
                            reads=[bWb, bX4B], writes=[pb])
                if proj == 'v':
                    for nn in range(2):
                        p.do('vector', lambda e, nn=nn: e.tensor_copy(
                            out=tVb[:, nn * 512:(nn + 1) * 512], in_=psF[:, 2048 + nn * 512:2048 + (nn + 1) * 512]),
                            reads=[pbank[4 + nn]], writes=[bVb])
                    for tt in range(8):
                        p.do('tensor', lambda e, tt=tt: e.transpose(
                            out=psB[:, 0:128], in_=tVb[:, tt * 128:(tt + 1) * 128], identity=t_idb[:, :]),
                            reads=[bVb, B('idb')], writes=[pbB])
                        p.do('vector', lambda e: e.tensor_copy(out=tVt[:, :], in_=psB[:, 0:128]),
                             reads=[pbB], writes=[bVt])
                        p.dma('sync', d_vin[m * TL + tt * 128: m * TL + (tt + 1) * 128, :], tVt[:, :],
                              reads=[bVt], writes=[B('d_vin')])
                else:
                    sc = ISQ if proj == 'k' else 1.0
                    for nn in range(2):
                        p.do('scalar', lambda e, nn=nn, sc=sc: e.activation(
                            out=tKf[:, nn * 512:(nn + 1) * 512], in_=psF[:, 2048 + nn * 512:2048 + (nn + 1) * 512],
                            func=ACT.Copy, scale=sc),
                            reads=[pbank[4 + nn]], writes=[bKf])
                    if proj == 'q':
                        rope_into(tQB[:, m * TL:(m + 1) * TL], bQB, 1.0)
                    else:
                        rope_into(tKb[:, :], bKb, 1.0)
                        p.dma('sync', d_kin[m * 128:(m + 1) * 128, :], tKb[:, :], reads=[bKb], writes=[B('d_kin')])
                        p.do('vector', lambda e, m=m: e.tensor_reduce(
                            out=tKM[:, m:m + 1], in_=tKb[:, :], axis=AX.X, op=ALU.max, apply_absolute_value=True),
                            reads=[bKb], writes=[bKM])
        p.dma('gpsimd', d_kmi.rearrange("(c p) o -> p c o", p=128), tKM[:, :].rearrange("p (c o) -> p c o", o=1),
              reads=[bKM], writes=[B('d_kmi')], slow=True)
        p.fence()

        p.do('gpsimd', lambda e: e.collective_compute(
            "AllGather", ALU.bypass, replica_groups=[list(range(NCORE))],
            ins=[d_kin.tensor.ap().opt() if hasattr(d_kin, 'tensor') else d_kin.opt()],
            outs=[d_kag.opt()]), reads=[B('d_kin')], writes=[B('d_kag')])
        p.do('gpsimd', lambda e: e.collective_compute(
            "AllGather", ALU.bypass, replica_groups=[list(range(NCORE))],
            ins=[d_vin.opt()], outs=[d_vag.opt()]), reads=[B('d_vin')], writes=[B('d_vag')])
        p.do('gpsimd', lambda e: e.collective_compute(
            "AllReduce", ALU.max, replica_groups=[list(range(NCORE))],
            ins=[d_kmi.opt()], outs=[d_kmo.opt()]), reads=[B('d_kmi')], writes=[B('d_kmo')])

        p.dma('sync', tKM[:, :].rearrange("p (c o) -> p c o", o=1), d_kmo.rearrange("(c p) o -> p c o", p=128),
              reads=[B('d_kmo')], writes=[bKM], slow=True)
        bNegM = B('C_negm'); bRowC = B('C_rowc')
        for m in range(NH):
            p.do('scalar', lambda e, m=m: e.activation(out=tKb[:, :], in_=tQB[:, m * TL:(m + 1) * TL],
                 func=ACT.Abs), reads=[bQB], writes=[bKb])
            p.do('vector', lambda e, m=m: e.tensor_copy(out=tVb[:, 0:1], in_=tKM[:, m:m + 1]),
                 reads=[bKM], writes=[bVb])
            p.do('tensor', lambda e, m=m: e.matmul(
                out=psF[0:1, 3072:3072 + 512], lhsT=tVb[:, 0:1], rhs=tKb[:, 0:512], start=True, stop=False),
                reads=[bVb, bKb], writes=[pbank[6]])
            p.do('tensor', lambda e, m=m: e.matmul(
                out=psF[0:1, 3072:3072 + 512], lhsT=tVb[:, 0:1], rhs=tKb[:, 512:1024], start=False, stop=True),
                reads=[bVb, bKb], writes=[pbank[6]])
            p.do('vector', lambda e, m=m: e.tensor_reduce(out=tRowC[0:1, m:m + 1], in_=psF[0:1, 3072:3072 + 512],
                 axis=AX.X, op=ALU.max), reads=[pbank[6]], writes=[bRowC])
        p.pad('v', 2)
        p.do('vector', lambda e: e.tensor_scalar(out=tRowC[0:1, 16:32], in0=tRowC[0:1, 0:16],
             scalar1=-1.0, scalar2=None, op0=ALU.mult), reads=[bRowC], writes=[bRowC])
        p.pad('v', 2)
        p.do('tensor', lambda e: e.matmul(out=psF[:, 3072:3072 + 16], lhsT=t_ones[0:1, 0:128],
             rhs=tRowC[0:1, 16:32], start=True, stop=True), reads=[bRowC, B('ones')], writes=[pbank[6]])
        p.do('vector', lambda e: e.tensor_copy(out=tNegM[:, :], in_=psF[:, 3072:3072 + 16]),
             reads=[pbank[6]], writes=[bNegM])
        p.fence()
        p.fence()
        arC2.free()
        arC3 = AutoArena(nc)
        tKH = [arC3.sbuf_tensor(f"C_kh{i}", [128, T], BF16) for i in range(2)]
        tVH = [arC3.sbuf_tensor(f"C_vh{i}", [128, 64 * 129], BF16) for i in range(2)]
        tEX = [arC3.sbuf_tensor(f"C_ex{i}", [128, 512], BF16) for i in range(2)]
        tRZ = arC3.sbuf_tensor("C_rz", [128, 8], F32)
        arC3.allocate()
        for i in range(2):
            p.do('gpsimd', lambda e, i=i: e.memset(
                tVH[i][:, :].rearrange("p (kc s) -> p kc s", s=129)[:, :, 128:129], 1.0),
                writes=[B(f'C_vh{i}')])
        bAO = B('C_ao'); bRZ = B('C_rz')
        for h in range(NH):
            hb = h % 2
            bKH = B(f'C_kh{hb}'); bVH = B(f'C_vh{hb}')
            for b in range(NCORE):
                p.dma('sync', tKH[hb][:, b * TL:(b + 1) * TL],
                      d_kag[(b * D + h * 128):(b * D + (h + 1) * 128), :],
                      reads=[B('d_kag')], writes=[bKH])
                p.dma('scalar', tVH[hb][:, :].rearrange("p (kc s) -> p kc s", s=129)[:, b * 8:(b + 1) * 8, 0:128],
                      d_vag[(b * NH + h) * TL: (b * NH + h + 1) * TL, :].rearrange("(kc p) hd -> p kc hd", p=128),
                      reads=[B('d_vag')], writes=[bVH])
            for qsg in range(2):
                pend = None
                for kc in range(64):
                    db = kc % 2
                    psc = pbank[4 + db]
                    p.do('tensor', lambda e, h=h, qsg=qsg, kc=kc, db=db, hb=hb: e.matmul(
                        out=psF[:, 2048 + db * 512:2048 + (db + 1) * 512],
                        lhsT=tKH[hb][:, kc * 128:(kc + 1) * 128],
                        rhs=tQB[:, h * TL + qsg * 512: h * TL + (qsg + 1) * 512],
                        start=True, stop=True),
                        reads=[bKH, bQB], writes=[psc])
                    p.do('scalar', lambda e, h=h, db=db: e.activation(
                        out=tEX[db][:, :], in_=psF[:, 2048 + db * 512:2048 + (db + 1) * 512],
                        func=ACT.Exp, bias=tNegM[:, h:h + 1], scale=1.0),
                        reads=[psc], sreads=[bNegM], writes=[B(f'C_ex{db}')])
                    if pend is not None:
                        pkc, pdb = pend
                        for qt in range(4):
                            p.do('tensor', lambda e, qt=qt, pkc=pkc, pdb=pdb, hb=hb: e.matmul(
                                out=psF[:, qt * 512:qt * 512 + 129],
                                lhsT=tEX[pdb][:, qt * 128:(qt + 1) * 128],
                                rhs=tVH[hb][:, pkc * 129:(pkc + 1) * 129],
                                start=(pkc == 0), stop=(pkc == 63)),
                                reads=[B(f'C_ex{pdb}'), bVH], writes=[pbank[qt]])
                    pend = (kc, db)
                pkc, pdb = pend
                for qt in range(4):
                    p.do('tensor', lambda e, qt=qt, pkc=pkc, pdb=pdb, hb=hb: e.matmul(
                        out=psF[:, qt * 512:qt * 512 + 129],
                        lhsT=tEX[pdb][:, qt * 128:(qt + 1) * 128],
                        rhs=tVH[hb][:, pkc * 129:(pkc + 1) * 129],
                        start=(pkc == 0), stop=(pkc == 63)),
                        reads=[B(f'C_ex{pdb}'), bVH], writes=[pbank[qt]])
                for qt in range(4):
                    p.do('vector', lambda e, qt=qt: e.reciprocal(out=tRZ[:, qt:qt + 1], in_=psF[:, qt * 512 + 128:qt * 512 + 129]),
                         reads=[pbank[qt]], writes=[bRZ])
                p.pad('v', 4)
                for qt in range(4):
                    tt = qsg * 4 + qt
                    p.do('scalar', lambda e, h=h, tt=tt, qt=qt: e.activation(
                        out=tAO[:, tt * D + h * 128: tt * D + (h + 1) * 128],
                        in_=psF[:, qt * 512:qt * 512 + 128], func=ACT.Copy, scale=tRZ[:, qt:qt + 1]),
                        reads=[pbank[qt]], sreads=[bRZ], writes=[bAO])
        p.fence()
        arC3.free()
        arC2a.free()

        arCL3 = AutoArena(nc)
        tX8F = arCL3.sbuf_tensor("C_x8f", [128, DC * TL], BF16)
        arCL3.allocate()
        arC4 = AutoArena(nc)
        tAOa = arC4.sbuf_tensor("C_aoa", [128, 8 * D], BF16)
        tBS = arC4.sbuf_tensor("C_bs", [128, 64], F32)
        tJunk = arC4.sbuf_tensor("C_junk", [128, D], BF16)
        tQTo = arC4.sbuf_tensor("C_qto", [128, D], F32)
        tQIo = arC4.sbuf_tensor("C_qio", [128, D], I32)
        tMask = arC4.sbuf_tensor("C_mask", [128, D], F32)
        arC4.allocate()
        bAOa = B('C_aoa'); bBS = B('C_bs'); bJunk = B('C_junk')
        bQTo = B('C_qto'); bQIo = B('C_qio'); bMask = B('C_mask'); bX8F = B('C_x8f')
        p.do('scalar', lambda e: e.activation(out=tAOa[:, :], in_=tAO[:, :], func=ACT.Abs),
             reads=[bAO], writes=[bAOa])
        p.do('vector', lambda e: e.tensor_reduce(out=tBS[:, 56:64], in_=tAOa[:, :].rearrange("p (t k) -> p t k", t=8),
             axis=AX.X, op=ALU.max), reads=[bAOa], writes=[bBS])
        p.do('gpsimd', lambda e: e.memset(tBS[:, 0:8], 0.0), writes=[bBS])
        p.pad('v', 2)
        p.do('vector', lambda e: e.tensor_scalar(out=tBS[:, 8:16], in0=tBS[:, 56:64], scalar1=1.0000001, scalar2=None, op0=ALU.mult),
             reads=[bBS], writes=[bBS])
        p.do('vector', lambda e: e.tensor_scalar(out=tBS[:, 40:48], in0=tBS[:, 56:64], scalar1=EPS_LN, scalar2=None, op0=ALU.max),
             reads=[bBS], writes=[bBS])
        p.pad('v', 2)
        p.do('vector', lambda e: e.reciprocal(out=tBS[:, 48:56], in_=tBS[:, 40:48]), reads=[bBS], writes=[bBS])
        p.pad('v', 2)
        p.do('vector', lambda e: e.tensor_scalar(out=tBS[:, 40:48], in0=tBS[:, 48:56], scalar1=127.0, scalar2=None, op0=ALU.mult),
             reads=[bBS], writes=[bBS])
        p.pad('v', 2)
        p.do('vector', lambda e: e.reciprocal(out=tBS[:, 48:56], in_=tBS[:, 40:48]), reads=[bBS], writes=[bBS])
        NBIS = 26
        KTGT = 1126.0
        for it in range(NBIS):
            p.do('vector', lambda e: e.tensor_tensor(out=tBS[:, 16:24], in0=tBS[:, 0:8], in1=tBS[:, 8:16], op=ALU.add),
                 reads=[bBS], writes=[bBS])
            p.pad('v', 2)
            p.do('vector', lambda e: e.tensor_scalar(out=tBS[:, 16:24], in0=tBS[:, 16:24], scalar1=0.5, scalar2=None, op0=ALU.mult),
                 reads=[bBS], writes=[bBS])
            p.pad('v', 6)
            for tt in range(8):
                p.do('vector', lambda e, tt=tt: e.tensor_scalar(
                    out=tJunk[:, :], in0=tAOa[:, tt * D:(tt + 1) * D],
                    scalar1=tBS[:, 16 + tt:17 + tt], scalar2=None, op0=ALU.is_ge, op1=ALU.add,
                    accum_out=tBS[:, 24 + tt:25 + tt]),
                    reads=[bAOa], sreads=[bBS], writes=[bJunk, bBS])
            p.pad('v', 2)
            p.do('vector', lambda e: e.tensor_scalar(out=tBS[:, 32:40], in0=tBS[:, 24:32], scalar1=KTGT, scalar2=None, op0=ALU.is_ge),
                 reads=[bBS], writes=[bBS])
            p.pad('v', 2)
            p.do('vector', lambda e: e.tensor_tensor(out=tBS[:, 24:32], in0=tBS[:, 16:24], in1=tBS[:, 0:8], op=ALU.subtract),
                 reads=[bBS], writes=[bBS])
            p.pad('v', 2)
            p.do('vector', lambda e: e.tensor_tensor(out=tBS[:, 24:32], in0=tBS[:, 24:32], in1=tBS[:, 32:40], op=ALU.mult),
                 reads=[bBS], writes=[bBS])
            p.pad('v', 2)
            p.do('vector', lambda e: e.tensor_tensor(out=tBS[:, 0:8], in0=tBS[:, 0:8], in1=tBS[:, 24:32], op=ALU.add),
                 reads=[bBS], writes=[bBS])
            p.pad('v', 2)
            p.do('vector', lambda e: e.tensor_tensor(out=tBS[:, 24:32], in0=tBS[:, 8:16], in1=tBS[:, 16:24], op=ALU.subtract),
                 reads=[bBS], writes=[bBS])
            p.pad('v', 2)
            p.do('vector', lambda e: e.tensor_tensor(out=tBS[:, 24:32], in0=tBS[:, 24:32], in1=tBS[:, 32:40], op=ALU.mult),
                 reads=[bBS], writes=[bBS])
            p.pad('v', 2)
            p.do('vector', lambda e: e.tensor_tensor(out=tBS[:, 8:16], in0=tBS[:, 16:24], in1=tBS[:, 24:32], op=ALU.add),
                 reads=[bBS], writes=[bBS])
            p.pad('v', 2)
        p.pad('v', 6)
        for tt in range(8):
            p.do('vector', lambda e, tt=tt: e.tensor_scalar(out=tMask[:, :], in0=tAOa[:, tt * D:(tt + 1) * D],
                 scalar1=tBS[:, 0 + tt:1 + tt], scalar2=None, op0=ALU.is_ge), reads=[bAOa], sreads=[bBS], writes=[bMask])
            p.do('vector', lambda e, tt=tt: e.tensor_scalar(out=tQTo[:, :], in0=tAO[:, tt * D:(tt + 1) * D],
                 scalar1=tBS[:, 40 + tt:41 + tt], scalar2=None, op0=ALU.mult), reads=[bAO], sreads=[bBS], writes=[bQTo])
            p.do('vector', lambda e: e.tensor_copy(out=tQIo[:, :], in_=tQTo[:, :]), reads=[bQTo], writes=[bQIo])
            p.do('vector', lambda e: e.tensor_copy(out=tQTo[:, :], in_=tQIo[:, :]), reads=[bQIo], writes=[bQTo])
            p.do('vector', lambda e, tt=tt: e.tensor_scalar(out=tQTo[:, :], in0=tQTo[:, :],
                 scalar1=tBS[:, 48 + tt:49 + tt], scalar2=None, op0=ALU.mult), sreads=[bBS], writes=[bQTo])
            p.do('vector', lambda e: e.tensor_tensor(out=tJunk[:, :], in0=tQTo[:, :], in1=tMask[:, :], op=ALU.mult),
                 reads=[bQTo, bMask], writes=[bJunk])
            for c in range(DC):
                p.do('tensor', lambda e, c=c: e.transpose(
                    out=psB[:, 0:128], in_=tJunk[:, c * 128:(c + 1) * 128], identity=t_idb[:, :]),
                    reads=[bJunk, B('idb')], writes=[pbB])
                p.do('vector', lambda e, c=c, tt=tt: e.tensor_copy(
                    out=tX8F[:, c * TL + tt * 128:(c * TL + (tt + 1) * 128)], in_=psB[:, 0:128]),
                    reads=[pbB], writes=[bX8F])
        if 'Ctaps' in dbg:
            p.dma('gpsimd', dbg_outs['o_x4b'][:, :], tX4B[:, :], reads=[bX4B], writes=[B('o_x4b')])
            p.dma('gpsimd', dbg_outs['o_x8f'][:, :], tX8F[:, :], reads=[bX8F], writes=[B('o_x8f')])
        p.fence()
        arC4.free()

        arC5 = AutoArena(nc)
        tWoF = [arC5.sbuf_tensor(f"C_wof{i}", [128, D], F32) for i in range(2)]
        tWoB = [arC5.sbuf_tensor(f"C_wob{i}", [128, D], BF16) for i in range(2)]
        tWoI = arC5.sbuf_tensor("C_woi", [128, D], I32)
        tAbs = arC5.sbuf_tensor("C_abs", [128, D], F32)
        tBR = arC5.sbuf_tensor("C_br", [128, 2], F32)
        tResC = [arC5.sbuf_tensor(f"C_res{i}", [128, TL], F32) for i in range(2)]
        tScC = [arC5.sbuf_tensor(f"C_sc{i}", [128, TL], F32) for i in range(2)]
        arC5.allocate()
        bAbs = B('C_abs'); bBR = B('C_br'); bRowC2 = B('C_rowc')
        for m in range(DC):
            db = m % 2
            bWf = B(f'C_wof{db}')
            p.dma('sync', tWoF[db][:, :].rearrange("p (c k) -> p c k", c=DC),
                  i_wo.rearrange("(c p) mm -> p c mm", p=128)[:, :, m * 128:(m + 1) * 128], writes=[bWf])
            p.do('scalar', lambda e, db=db: e.activation(out=tAbs[:, :], in_=tWoF[db][:, :], func=ACT.Abs),
                 reads=[bWf], writes=[bAbs])
            for g in range(4):
                p.do('tensor', lambda e, m=m, g=g: e.matmul(
                    out=psF[0:1, 3072 + 0:3072 + 512], lhsT=t_ones[:, 0:1], rhs=tAbs[:, g * 512:(g + 1) * 512],
                    start=(m == 0 and g == 0), stop=(m == DC - 1 and g == 3)),
                    reads=[bAbs, B('ones')], writes=[pbank[6]])
        p.do('vector', lambda e: e.tensor_reduce(out=tRowC[0:1, 32:33], in_=psF[0:1, 3072:3072 + 512],
             axis=AX.X, op=ALU.add), reads=[pbank[6]], writes=[bRowC2])
        p.pad('v', 2)
        p.do('vector', lambda e: e.tensor_scalar(out=tRowC[0:1, 33:34], in0=tRowC[0:1, 32:33],
             scalar1=1.0 / (D * D), scalar2=None, op0=ALU.mult), reads=[bRowC2], writes=[bRowC2])
        p.pad('v', 2)
        p.do('vector', lambda e: e.tensor_scalar(out=tRowC[0:1, 34:35], in0=tRowC[0:1, 33:34],
             scalar1=EPS_LN, scalar2=None, op0=ALU.add), reads=[bRowC2], writes=[bRowC2])
        p.pad('v', 2)
        p.do('vector', lambda e: e.reciprocal(out=tRowC[0:1, 35:36], in_=tRowC[0:1, 34:35]),
             reads=[bRowC2], writes=[bRowC2])
        p.pad('v', 2)
        p.do('tensor', lambda e: e.matmul(out=psF[:, 3072:3072 + 1], lhsT=t_ones[0:1, 0:128],
             rhs=tRowC[0:1, 33:34], start=True, stop=True), reads=[bRowC2, B('ones')], writes=[pbank[6]])
        p.do('tensor', lambda e: e.matmul(out=psF[:, 3073:3073 + 1], lhsT=t_ones[0:1, 0:128],
             rhs=tRowC[0:1, 35:36], start=True, stop=True), reads=[bRowC2, B('ones')], writes=[pbank[6]])
        p.do('vector', lambda e: e.tensor_copy(out=tBR[:, :], in_=psF[:, 3072:3072 + 2]),
             reads=[pbank[6]], writes=[bBR])
        p.pad('v', 2)
        for m in range(DC):
            db = m % 2
            bWf = B(f'C_wof{db}'); bWb = B(f'C_wob{db}')
            p.dma('sync', tWoF[db][:, :].rearrange("p (c k) -> p c k", c=DC),
                  i_wo.rearrange("(c p) mm -> p c mm", p=128)[:, :, m * 128:(m + 1) * 128], writes=[bWf])
            p.do('vector', lambda e, db=db: e.tensor_scalar(out=tAbs[:, :], in0=tWoF[db][:, :],
                 scalar1=tBR[:, 1:2], scalar2=None, op0=ALU.mult), reads=[bWf], sreads=[bBR], writes=[bAbs])
            p.do('vector', lambda e: e.tensor_copy(out=tWoI[:, :], in_=tAbs[:, :]), reads=[bAbs], writes=[B('C_woi')])
            p.do('vector', lambda e: e.tensor_copy(out=tAbs[:, :], in_=tWoI[:, :]), reads=[B('C_woi')], writes=[bAbs])
            p.do('vector', lambda e: e.tensor_scalar(out=tAbs[:, :], in0=tAbs[:, :],
                 scalar1=1.0, scalar2=-1.0, op0=ALU.min, op1=ALU.max), writes=[bAbs])
            p.do('vector', lambda e, db=db: e.tensor_scalar(out=tWoB[db][:, :], in0=tAbs[:, :],
                 scalar1=tBR[:, 0:1], scalar2=None, op0=ALU.mult), reads=[bAbs], sreads=[bBR], writes=[bWb])
            bResC = B(f'C_res{db}'); bScC = B(f'C_sc{db}')
            p.dma('gpsimd', tResC[db][:, :], d_s2[m * 128:(m + 1) * 128, :], reads=[B('d_s2')], writes=[bResC])
            for nn in range(2):
                pb = pbank[2 + nn]
                for c in range(DC):
                    p.do('tensor', lambda e, c=c, nn=nn, db=db: e.matmul(
                        out=psF[:, 1024 + nn * 512:1024 + (nn + 1) * 512],
                        lhsT=tWoB[db][:, c * 128:(c + 1) * 128],
                        rhs=tX8F[:, c * TL + nn * 512: c * TL + (nn + 1) * 512],
                        start=(c == 0), stop=(c == DC - 1)),
                        reads=[bWb, bX8F], writes=[pb])
                p.do('vector', lambda e, nn=nn, db=db: e.tensor_tensor(
                    out=tScC[db][:, nn * 512:(nn + 1) * 512], in0=psF[:, 1024 + nn * 512:1024 + (nn + 1) * 512],
                    in1=tResC[db][:, nn * 512:(nn + 1) * 512], op=ALU.add),
                    reads=[pb, bResC], writes=[bScC])
            p.dma('sync', d_s3[m * 128:(m + 1) * 128, :], tScC[db][:, :], reads=[bScC], writes=[B('d_s3')])
        if 'C' in dbg:
            for m in range(DC):
                p.dma('gpsimd', dbg_outs['o_s3'][m * 128:(m + 1) * 128, :], d_s3[m * 128:(m + 1) * 128, :],
                      reads=[B('d_s3')], writes=[B('o_s3')])
        p.fence()
        arC5.free()
        arCL3.free()
        arCL2.free()
        arCL.free()

    if 'D' in phases:
        IC = II // 512
        arD0 = AutoArena(nc)
        tNW = arD0.sbuf_tensor("D_nw", [128, II], F32)
        tBM = arD0.sbuf_tensor("D_bm", [128, 512], F32)
        tS2 = arD0.sbuf_tensor("D_s2", [128, 512], F32)
        tR2 = arD0.sbuf_tensor("D_r2", [128, 512], F32)
        tPK = arD0.sbuf_tensor("D_pk", [128, 1024], F32)
        arD0.allocate()
        bNW = B('D_nw'); bBM = B('D_bm'); bS2 = B('D_s2'); bR2 = B('D_r2'); bPK = B('D_pk')

        def mx_scales(nblk):
            p.pad('v', 2)
            p.do('vector', lambda e, nblk=nblk: e.tensor_scalar(out=tBM[:, 0:nblk], in0=tBM[:, 0:nblk],
                 scalar1=1.0 / 448.0, scalar2=None, op0=ALU.mult), reads=[bBM], writes=[bBM])
            p.pad('v', 2)
            p.do('vector', lambda e, nblk=nblk: e.tensor_scalar(out=tBM[:, 0:nblk], in0=tBM[:, 0:nblk],
                 scalar1=1e-30, scalar2=None, op0=ALU.max), reads=[bBM], writes=[bBM])
            p.pad('v', 2)
            p.do('vector', lambda e, nblk=nblk: e.tensor_scalar(out=tS2[:, 0:nblk], in0=tBM[:, 0:nblk],
                 scalar1=2.0, scalar2=None, op0=ALU.mult), reads=[bBM], writes=[bS2])
            p.do('vector', lambda e, nblk=nblk: e.reciprocal(out=tR2[:, 0:nblk], in_=tBM[:, 0:nblk]),
                 reads=[bBM], writes=[bR2])
            p.pad('v', 2)
            p.do('vector', lambda e, nblk=nblk: e.tensor_scalar(out=tR2[:, 0:nblk], in0=tR2[:, 0:nblk],
                 scalar1=0.5, scalar2=None, op0=ALU.mult), reads=[bR2], writes=[bR2])
            p.pad('v', 2)

        for (inat, dsc, tag) in ((i_gwn, d_gsci, 'g'), (i_uwn, d_usci, 'u')):
            for ch in range(8):
                p.dma('sync', tNW[:, 0:D], inat[ch * 128:(ch + 1) * 128, :], writes=[bNW])
                p.do('vector', lambda e: e.tensor_reduce(out=tBM[:, 0:64],
                     in_=tNW[:, 0:D].rearrange("p (g b) -> p g b", b=32), axis=AX.X, op=ALU.max,
                     apply_absolute_value=True), reads=[bNW], writes=[bBM])
                mx_scales(64)
                p.do('vector', lambda e: e.tensor_copy(out=tPK[:, 0:64], in_=tS2[:, 0:64]), reads=[bS2], writes=[bPK])
                p.do('vector', lambda e: e.tensor_copy(out=tPK[:, 64:128], in_=tR2[:, 0:64]), reads=[bR2], writes=[bPK])
                p.dma('sync', dsc[ch * 128:(ch + 1) * 128, :], tPK[:, 0:128], reads=[bPK], writes=[B('d_' + tag + 'sci')])
        for ch in range(2):
            p.dma('sync', tNW[:, 0:II], i_dwn[ch * 128:(ch + 1) * 128, :], writes=[bNW])
            p.do('vector', lambda e: e.tensor_reduce(out=tBM[:, 0:256],
                 in_=tNW[:, 0:II].rearrange("p (g b) -> p g b", b=32), axis=AX.X, op=ALU.max,
                 apply_absolute_value=True), reads=[bNW], writes=[bBM])
            mx_scales(256)
            p.do('vector', lambda e: e.tensor_copy(out=tPK[:, 0:256], in_=tS2[:, 0:256]), reads=[bS2], writes=[bPK])
            p.do('vector', lambda e: e.tensor_copy(out=tPK[:, 256:512], in_=tR2[:, 0:256]), reads=[bR2], writes=[bPK])
            p.dma('sync', d_dsci[ch * 128:(ch + 1) * 128, :], tPK[:, 0:512], reads=[bPK], writes=[B('d_dsci')])
        p.fence()
        p.do('gpsimd', lambda e: e.collective_compute(
            "AllGather", ALU.bypass, replica_groups=[list(range(NCORE))],
            ins=[d_gsci.opt()], outs=[d_gsco.opt()]), reads=[B('d_gsci')], writes=[B('d_gsco')])
        p.do('gpsimd', lambda e: e.collective_compute(
            "AllGather", ALU.bypass, replica_groups=[list(range(NCORE))],
            ins=[d_usci.opt()], outs=[d_usco.opt()]), reads=[B('d_usci')], writes=[B('d_usco')])
        p.do('gpsimd', lambda e: e.collective_compute(
            "AllGather", ALU.bypass, replica_groups=[list(range(NCORE))],
            ins=[d_dsci.opt()], outs=[d_dsco.opt()]), reads=[B('d_dsci')], writes=[B('d_dsco')])
        p.fence()
        arD0.free()

        arDL = AutoArena(nc)
        tX3F = arDL.sbuf_tensor("D_x3f", [128, DC * TL], BF16)
        tG2P = arDL.sbuf_tensor("D_g2p", [4, 128], F32)
        arDL.allocate()
        p.dma('sync', tG2P[:, :], i_g2p[:, :], writes=[B('D_g2p')])
        arD1 = AutoArena(nc)
        tX3 = arD1.sbuf_tensor("D_x3", [128, DC * TL], F32)
        tSQd = [arD1.sbuf_tensor(f"D_sq{i}", [128, 512], F32) for i in range(2)]
        tRowd = arD1.sbuf_tensor("D_row", [1, 512 * 5], F32)
        tMBd = arD1.sbuf_tensor("D_mb", [128, 512], F32)
        tRBd = arD1.sbuf_tensor("D_rb", [128, 512], F32)
        tNrmd = arD1.sbuf_tensor("D_nrm", [128, 512], F32)
        arD1.allocate()
        p.dma('sync', tX3[:, :].rearrange("p (c t) -> p c t", c=DC),
              d_s3.rearrange("(c p) t -> p c t", p=128), reads=[B('d_s3')], writes=[B('D_x3')])
        emit_ln2(tSQd, tRowd, tMBd, tRBd, tNrmd,
                 lambda c: tX3[:, c * TL:(c + 1) * TL], TL,
                 lambda c: tX3[:, c * TL:(c + 1) * TL], B('D_x3'), B('D_x3'), 'Dln')
        p.fence()
        arD2 = AutoArena(nc)
        tXTt = arD2.sbuf_tensor("D_xtt", [128, D], F32)
        tBMx = arD2.sbuf_tensor("D_bmx", [128, 64], F32)
        tS2x = arD2.sbuf_tensor("D_s2x", [128, 64], F32)
        tR2x = arD2.sbuf_tensor("D_r2x", [128, 64], F32)
        tQ8x = arD2.sbuf_tensor("D_q8x", [128, D], FP8)
        tQBx = arD2.sbuf_tensor("D_qbx", [128, D], BF16)
        arD2.allocate()
        bXTt = B('D_xtt'); bBMx = B('D_bmx'); bS2x = B('D_s2x'); bR2x = B('D_r2x')
        bQ8x = B('D_q8x'); bQBx = B('D_qbx'); bX3F = B('D_x3f')
        for tt in range(8):
            for c in range(DC):
                p.do('tensor', lambda e, c=c, tt=tt: e.transpose(
                    out=psF[:, 3072:3072 + 128], in_=tX3[:, c * TL + tt * 128: c * TL + (tt + 1) * 128], identity=t_idf[:, :]),
                    reads=[B('D_x3'), B('idf')], writes=[pbank[6]])
                p.do('vector', lambda e, c=c: e.tensor_copy(
                    out=tXTt[:, c * 128:(c + 1) * 128], in_=psF[:, 3072:3072 + 128]),
                    reads=[pbank[6]], writes=[bXTt])
            p.do('vector', lambda e: e.tensor_reduce(out=tBMx[:, 0:64],
                 in_=tXTt[:, :].rearrange("p (g b) -> p g b", b=32), axis=AX.X, op=ALU.max,
                 apply_absolute_value=True), reads=[bXTt], writes=[bBMx])
            p.pad('v', 2)
            p.do('vector', lambda e: e.tensor_scalar(out=tBMx[:, :], in0=tBMx[:, :], scalar1=1.0 / 448.0, scalar2=None, op0=ALU.mult),
                 reads=[bBMx], writes=[bBMx])
            p.pad('v', 2)
            p.do('vector', lambda e: e.tensor_scalar(out=tBMx[:, :], in0=tBMx[:, :], scalar1=1e-30, scalar2=None, op0=ALU.max),
                 reads=[bBMx], writes=[bBMx])
            p.pad('v', 2)
            p.do('vector', lambda e: e.tensor_scalar(out=tS2x[:, :], in0=tBMx[:, :], scalar1=2.0, scalar2=None, op0=ALU.mult),
                 reads=[bBMx], writes=[bS2x])
            p.do('vector', lambda e: e.reciprocal(out=tR2x[:, :], in_=tBMx[:, :]), reads=[bBMx], writes=[bR2x])
            p.pad('v', 2)
            p.do('vector', lambda e: e.tensor_scalar(out=tR2x[:, :], in0=tR2x[:, :], scalar1=0.5, scalar2=None, op0=ALU.mult),
                 reads=[bR2x], writes=[bR2x])
            p.pad('v', 2)
            p.do('vector', lambda e: e.tensor_tensor(
                out=tQ8x[:, :].rearrange("p (g b) -> p g b", b=32),
                in0=tXTt[:, :].rearrange("p (g b) -> p g b", b=32),
                in1=tR2x[:, :].to_broadcast((128, 64, 32)), op=ALU.mult),
                reads=[bXTt, bR2x], writes=[bQ8x])
            p.do('vector', lambda e: e.tensor_tensor(
                out=tQBx[:, :].rearrange("p (g b) -> p g b", b=32),
                in0=tQ8x[:, :].rearrange("p (g b) -> p g b", b=32),
                in1=tS2x[:, :].to_broadcast((128, 64, 32)), op=ALU.mult),
                reads=[bQ8x, bS2x], writes=[bQBx])
            for c in range(DC):
                p.do('tensor', lambda e, c=c, tt=tt: e.transpose(
                    out=psB[:, 0:128], in_=tQBx[:, c * 128:(c + 1) * 128], identity=t_idb[:, :]),
                    reads=[bQBx, B('idb')], writes=[pbB])
                p.do('vector', lambda e, c=c, tt=tt: e.tensor_copy(
                    out=tX3F[:, c * TL + tt * 128: c * TL + (tt + 1) * 128], in_=psB[:, 0:128]),
                    reads=[pbB], writes=[bX3F])
        p.fence()
        arD2.free()
        arD1.free()

        arD3 = AutoArena(nc)
        tGW2 = [arD3.sbuf_tensor(f"D_gw{i}", [128, 512], F32) for i in range(2)]
        tUW2 = [arD3.sbuf_tensor(f"D_uw{i}", [128, 512], F32) for i in range(2)]
        tGQ = arD3.sbuf_tensor("D_gq", [128, DC * 512], BF16)
        tUQ = arD3.sbuf_tensor("D_uq", [128, DC * 512], BF16)
        tQ8w = arD3.sbuf_tensor("D_q8w", [128, 512], FP8)
        tSC4 = arD3.sbuf_tensor("D_sc4", [4, 512], F32)
        tSbw = arD3.sbuf_tensor("D_sbw", [128, 512], F32)
        tRbw = arD3.sbuf_tensor("D_rbw", [128, 512], F32)
        tGf = arD3.sbuf_tensor("D_gf", [128, 512], F32)
        tUf = arD3.sbuf_tensor("D_uf", [128, 512], F32)
        tHf = arD3.sbuf_tensor("D_hf", [128, 8 * 512], F32)
        tBMh = arD3.sbuf_tensor("D_bmh", [128, 128], F32)
        tS2h = arD3.sbuf_tensor("D_s2h", [128, 128], F32)
        tR2h = arD3.sbuf_tensor("D_r2h", [128, 128], F32)
        tQ8h = arD3.sbuf_tensor("D_q8h", [128, 8 * 512], FP8)
        tHQ = arD3.sbuf_tensor("D_hq", [128, 8 * 512], BF16)
        tHT = arD3.sbuf_tensor("D_ht", [128, 128], BF16)
        arD3.allocate()
        bGQ = B('D_gq'); bUQ = B('D_uq'); bQ8w = B('D_q8w'); bSC4 = B('D_sc4')
        bSbw = B('D_sbw'); bRbw = B('D_rbw'); bGf = B('D_gf'); bUf = B('D_uf')
        bHf = B('D_hf'); bBMh = B('D_bmh'); bS2h = B('D_s2h'); bR2h = B('D_r2h')
        bQ8h = B('D_q8h'); bHQ = B('D_hq'); bHT = B('D_ht')

        def quant_wslab(wf, bwf, dsc, bdsc, c, ig, out_tile, bout, nwid):
            for (colbase, dstT, bdst) in ((0, tSbw, bSbw), (64, tRbw, bRbw)):
                p.dma('gpsimd', tSC4[:, 0:nwid],
                      dsc[ig * nwid:(ig + 1) * nwid, colbase + c * 4: colbase + c * 4 + 4].rearrange("i g -> g i"),
                      reads=[bdsc], writes=[bSC4], slow=True)
                p.do('tensor', lambda e, nwid=nwid: e.matmul(out=psF[:, 2048:2048 + nwid], lhsT=tG2P[:, :],
                     rhs=tSC4[:, 0:nwid], start=True, stop=True), reads=[bSC4, B('D_g2p')], writes=[pbank[4]])
                p.do('vector', lambda e, dstT=dstT, nwid=nwid: e.tensor_copy(out=dstT[:, 0:nwid], in_=psF[:, 2048:2048 + nwid]),
                     reads=[pbank[4]], writes=[bdst])
            p.do('vector', lambda e, nwid=nwid: e.tensor_tensor(out=tQ8w[:, 0:nwid], in0=wf[:, 0:nwid], in1=tRbw[:, 0:nwid], op=ALU.mult),
                 reads=[bwf, bRbw], writes=[bQ8w])
            p.do('vector', lambda e, out_tile=out_tile, nwid=nwid: e.tensor_tensor(out=out_tile, in0=tQ8w[:, 0:nwid], in1=tSbw[:, 0:nwid], op=ALU.mult),
                 reads=[bQ8w, bSbw], writes=[bout])

        for ig in range(IC):
            for c in range(DC):
                db = c % 2
                bGW = B(f'D_gw{db}'); bUW = B(f'D_uw{db}')
                p.dma('sync', tGW2[db][:, :], i_gw[c * 128:(c + 1) * 128, ig * 512:(ig + 1) * 512], writes=[bGW])
                p.dma('sync', tUW2[db][:, :], i_uw[c * 128:(c + 1) * 128, ig * 512:(ig + 1) * 512], writes=[bUW])
                quant_wslab(tGW2[db], bGW, d_gsco, B('d_gsco'), c, ig, tGQ[:, c * 512:(c + 1) * 512], bGQ, 512)
                quant_wslab(tUW2[db], bUW, d_usco, B('d_usco'), c, ig, tUQ[:, c * 512:(c + 1) * 512], bUQ, 512)
            for tt in range(8):
                for c in range(DC):
                    p.do('tensor', lambda e, tt=tt, c=c: e.matmul(
                        out=psF[:, 0:512], lhsT=tX3F[:, c * TL + tt * 128: c * TL + (tt + 1) * 128],
                        rhs=tGQ[:, c * 512:(c + 1) * 512], start=(c == 0), stop=(c == DC - 1)),
                        reads=[bX3F, bGQ], writes=[pbank[0]])
                    p.do('tensor', lambda e, tt=tt, c=c: e.matmul(
                        out=psF[:, 512:1024], lhsT=tX3F[:, c * TL + tt * 128: c * TL + (tt + 1) * 128],
                        rhs=tUQ[:, c * 512:(c + 1) * 512], start=(c == 0), stop=(c == DC - 1)),
                        reads=[bX3F, bUQ], writes=[pbank[1]])
                p.do('scalar', lambda e: e.activation(out=tGf[:, :], in_=psF[:, 0:512], func=ACT.Silu),
                     reads=[pbank[0]], writes=[bGf])
                p.do('scalar', lambda e: e.activation(out=tUf[:, :], in_=psF[:, 512:1024], func=ACT.Copy),
                     reads=[pbank[1]], writes=[bUf])
                p.do('vector', lambda e, tt=tt: e.tensor_tensor(out=tHf[:, tt * 512:(tt + 1) * 512], in0=tGf[:, :], in1=tUf[:, :], op=ALU.mult),
                     reads=[bGf, bUf], writes=[bHf])
            p.do('vector', lambda e: e.tensor_reduce(out=tBMh[:, 0:128],
                 in_=tHf[:, :].rearrange("p (g b) -> p g b", b=32), axis=AX.X, op=ALU.max,
                 apply_absolute_value=True), reads=[bHf], writes=[bBMh])
            p.pad('v', 2)
            p.do('vector', lambda e: e.tensor_scalar(out=tBMh[:, :], in0=tBMh[:, :], scalar1=1.0 / 448.0, scalar2=None, op0=ALU.mult),
                 reads=[bBMh], writes=[bBMh])
            p.pad('v', 2)
            p.do('vector', lambda e: e.tensor_scalar(out=tBMh[:, :], in0=tBMh[:, :], scalar1=1e-30, scalar2=None, op0=ALU.max),
                 reads=[bBMh], writes=[bBMh])
            p.pad('v', 2)
            p.do('vector', lambda e: e.tensor_scalar(out=tS2h[:, :], in0=tBMh[:, :], scalar1=2.0, scalar2=None, op0=ALU.mult),
                 reads=[bBMh], writes=[bS2h])
            p.do('vector', lambda e: e.reciprocal(out=tR2h[:, :], in_=tBMh[:, :]), reads=[bBMh], writes=[bR2h])
            p.pad('v', 2)
            p.do('vector', lambda e: e.tensor_scalar(out=tR2h[:, :], in0=tR2h[:, :], scalar1=0.5, scalar2=None, op0=ALU.mult),
                 reads=[bR2h], writes=[bR2h])
            p.pad('v', 2)
            p.do('vector', lambda e: e.tensor_tensor(
                out=tQ8h[:, :].rearrange("p (g b) -> p g b", b=32),
                in0=tHf[:, :].rearrange("p (g b) -> p g b", b=32),
                in1=tR2h[:, :].to_broadcast((128, 128, 32)), op=ALU.mult),
                reads=[bHf, bR2h], writes=[bQ8h])
            p.do('vector', lambda e: e.tensor_tensor(
                out=tHQ[:, :].rearrange("p (g b) -> p g b", b=32),
                in0=tQ8h[:, :].rearrange("p (g b) -> p g b", b=32),
                in1=tS2h[:, :].to_broadcast((128, 128, 32)), op=ALU.mult),
                reads=[bQ8h, bS2h], writes=[bHQ])
            for tt in range(8):
                for sub in range(4):
                    p.do('tensor', lambda e, tt=tt, sub=sub: e.transpose(
                        out=psB[:, 0:128], in_=tHQ[:, tt * 512 + sub * 128: tt * 512 + (sub + 1) * 128], identity=t_idb[:, :]),
                        reads=[bHQ, B('idb')], writes=[pbB])
                    p.do('vector', lambda e: e.tensor_copy(out=tHT[:, :], in_=psB[:, 0:128]),
                         reads=[pbB], writes=[bHT])
                    p.dma('sync', d_hq[ig * 512 + sub * 128: ig * 512 + (sub + 1) * 128, tt * 128:(tt + 1) * 128],
                          tHT[:, :], reads=[bHT], writes=[B('d_hq')])
        p.fence()
        arD3.free()

        arD4 = AutoArena(nc)
        tDWf = [arD4.sbuf_tensor(f"D_dwf{i}", [128, 256], F32) for i in range(2)]
        tDWQ = [arD4.sbuf_tensor(f"D_dwq{i}", [128, 256], BF16) for i in range(2)]
        tHTm = [arD4.sbuf_tensor(f"D_htm{i}", [128, TL], BF16) for i in range(2)]
        tQ8d = arD4.sbuf_tensor("D_q8d", [128, 256], FP8)
        tSC4d = arD4.sbuf_tensor("D_sc4d", [4, 256], F32)
        tSbd = arD4.sbuf_tensor("D_sbd", [128, 256], F32)
        tRbd = arD4.sbuf_tensor("D_rbd", [128, 256], F32)
        tResD = [arD4.sbuf_tensor(f"D_res{i}", [128, TL], F32) for i in range(2)]
        tScD = [arD4.sbuf_tensor(f"D_scd{i}", [128, TL], F32) for i in range(2)]
        arD4.allocate()
        bQ8d = B('D_q8d'); bSC4d = B('D_sc4d'); bSbd = B('D_sbd'); bRbd = B('D_rbd')
        for mg in range(8):
            for k in range(II // 128):
                db = k % 2
                bDWf = B(f'D_dwf{db}'); bDWQ = B(f'D_dwq{db}'); bHTm = B(f'D_htm{db}')
                p.dma('sync', tDWf[db][:, :], i_dw[k * 128:(k + 1) * 128, mg * 256:(mg + 1) * 256], writes=[bDWf])
                p.dma('gpsimd', tHTm[db][:, :], d_hq[k * 128:(k + 1) * 128, :], reads=[B('d_hq')], writes=[bHTm])
                for (colbase, dstT, bdst) in ((0, tSbd, bSbd), (256, tRbd, bRbd)):
                    p.dma('gpsimd', tSC4d[:, 0:256],
                          d_dsco[mg * 256:(mg + 1) * 256, colbase + k * 4: colbase + k * 4 + 4].rearrange("i g -> g i"),
                          reads=[B('d_dsco')], writes=[bSC4d], slow=True)
                    p.do('tensor', lambda e: e.matmul(out=psF[:, 2048:2048 + 256], lhsT=tG2P[:, :],
                         rhs=tSC4d[:, 0:256], start=True, stop=True), reads=[bSC4d, B('D_g2p')], writes=[pbank[4]])
                    p.do('vector', lambda e, dstT=dstT: e.tensor_copy(out=dstT[:, :], in_=psF[:, 2048:2048 + 256]),
                         reads=[pbank[4]], writes=[bdst])
                p.do('vector', lambda e, db=db: e.tensor_tensor(out=tQ8d[:, :], in0=tDWf[db][:, :], in1=tRbd[:, :], op=ALU.mult),
                     reads=[bDWf, bRbd], writes=[bQ8d])
                p.do('vector', lambda e, db=db: e.tensor_tensor(out=tDWQ[db][:, :], in0=tQ8d[:, :], in1=tSbd[:, :], op=ALU.mult),
                     reads=[bQ8d, bSbd], writes=[bDWQ])
                for mm in range(2):
                    for nn in range(2):
                        pb = pbank[mm * 2 + nn]
                        p.do('tensor', lambda e, k=k, mm=mm, nn=nn, db=db: e.matmul(
                            out=psF[:, (mm * 2 + nn) * 512:(mm * 2 + nn + 1) * 512],
                            lhsT=tDWQ[db][:, mm * 128:(mm + 1) * 128],
                            rhs=tHTm[db][:, nn * 512:(nn + 1) * 512],
                            start=(k == 0), stop=(k == II // 128 - 1)),
                            reads=[bDWQ, bHTm], writes=[pb])
            for mm in range(2):
                m = mg * 2 + mm
                db = mm % 2
                bResD = B(f'D_res{db}'); bScD = B(f'D_scd{db}')
                p.dma('gpsimd', tResD[db][:, :], d_s3[m * 128:(m + 1) * 128, :], reads=[B('d_s3')], writes=[bResD])
                for nn in range(2):
                    pb = pbank[mm * 2 + nn]
                    p.do('vector', lambda e, mm=mm, nn=nn, db=db: e.tensor_tensor(
                        out=tScD[db][:, nn * 512:(nn + 1) * 512],
                        in0=psF[:, (mm * 2 + nn) * 512:(mm * 2 + nn + 1) * 512],
                        in1=tResD[db][:, nn * 512:(nn + 1) * 512], op=ALU.add),
                        reads=[pb, bResD], writes=[bScD])
                p.dma('sync', final_out[m * 128:(m + 1) * 128, :], tScD[db][:, :], reads=[bScD], writes=[B('out')])
        p.fence()
        arD4.free()
        arDL.free()

    if phases[-1] == 'A':
        for m in range(DC):
            p.dma('sync', final_out[m * 128:(m + 1) * 128, :], d_s1[m * 128:(m + 1) * 128, :],
                  reads=[B('d_s1')], writes=[B('out')])
    if phases[-1] == 'B':
        for m in range(DC):
            p.dma('sync', final_out[m * 128:(m + 1) * 128, :], d_s2[m * 128:(m + 1) * 128, :],
                  reads=[B('d_s2')], writes=[B('out')])
    if phases[-1] == 'C':
        for m in range(DC):
            p.dma('sync', final_out[m * 128:(m + 1) * 128, :], d_s3[m * 128:(m + 1) * 128, :],
                  reads=[B('d_s3')], writes=[B('out')])

    finals = p.barrier_tok()
    with nc.Block() as block:
        for eng in ENGS:
            its = p.items[eng]
            if not its:
                continue
            getattr(block, eng)(p.mk_emitter(its, sems, finals))

    stack.close()
    return nc, out_names


def _prep_inputs(inputs):
    x = np.asarray(inputs['L0_attn_x'])[0]
    xT = np.ascontiguousarray(x.T)
    z64 = np.zeros((D, HIST), np.float32)
    piw = np.ascontiguousarray(np.asarray(inputs['L0_attn_proj_in_w']).T)
    pow_ = np.ascontiguousarray(np.asarray(inputs['L0_attn_proj_out_w']).T)
    pib = np.ascontiguousarray(np.asarray(inputs['L0_attn_proj_in_b']).reshape(D, 1))
    pob = np.ascontiguousarray(np.asarray(inputs['L0_attn_proj_out_b']).reshape(D, 1))
    A = np.ascontiguousarray(np.asarray(inputs['L0_attn_A']))
    BwT = np.ascontiguousarray(np.asarray(inputs['L0_attn_B_w']).T)
    C2 = np.ascontiguousarray(np.tile(np.asarray(inputs['L0_attn_C_w']).T, (LF, 1)))
    Dp = np.ascontiguousarray(np.asarray(inputs['L0_attn_D_param']).reshape(D, 1))
    idf = np.eye(128, dtype=np.float32)
    idb = np.eye(128, dtype=ml_dtypes.bfloat16)
    consts = np.zeros((128, 8), np.float32)
    consts[:, 0] = EPS_LN
    rep16 = (np.arange(128)[None, :] % 16 == np.arange(16)[:, None]).astype(np.float32)
    onesm = np.ones((128, 128), np.float32)
    wg = np.ascontiguousarray(np.asarray(inputs['L0_ffn_wq_gate']).T)
    wu = np.ascontiguousarray(np.asarray(inputs['L0_ffn_wq_up']).T)
    wd = np.ascontiguousarray(np.asarray(inputs['L0_ffn_wq_down']).T)
    sg = np.ascontiguousarray(np.asarray(inputs['L0_ffn_s_gate']).reshape(HID, 1))
    su = np.ascontiguousarray(np.asarray(inputs['L0_ffn_s_up']).reshape(HID, 1))
    sd2 = np.ascontiguousarray(np.asarray(inputs['L0_ffn_s_down']).reshape(D, 1))
    wqT = np.ascontiguousarray(np.asarray(inputs['L1_attn_wq']).T)
    wkT = np.ascontiguousarray(np.asarray(inputs['L1_attn_wk']).T)
    wvT = np.ascontiguousarray(np.asarray(inputs['L1_attn_wv']).T)
    woT = np.ascontiguousarray(np.asarray(inputs['L1_attn_wo']).T)
    iarr = (np.arange(0, HD, 2).astype(np.float32) / np.float32(HD)).astype(np.float32)
    inv32 = (np.float32(1.0) / np.power(np.float32(10000.0), iarr, dtype=np.float32)).astype(np.float32)
    tpos = np.arange(T, dtype=np.float32)
    freqs = (tpos[:, None] * inv32[None, :]).astype(np.float32)
    emb = np.concatenate([freqs, freqs], axis=1)
    cosT = np.ascontiguousarray(np.cos(emb).astype(np.float32).T)
    sinT = np.ascontiguousarray(np.sin(emb).astype(np.float32).T)
    gwN = np.ascontiguousarray(np.asarray(inputs['L1_ffn_gate_w']))
    uwN = np.ascontiguousarray(np.asarray(inputs['L1_ffn_up_w']))
    dwN = np.ascontiguousarray(np.asarray(inputs['L1_ffn_down_w']))
    gwT = np.ascontiguousarray(gwN.T); uwT = np.ascontiguousarray(uwN.T); dwT = np.ascontiguousarray(dwN.T)
    g2p = (np.arange(128)[None, :] // 32 == np.arange(4)[:, None]).astype(np.float32)
    maps = []
    for c in range(NCORE):
        lo = c * TL
        if c == 0:
            xloc = np.ascontiguousarray(np.concatenate([z64, xT[:, 0:TL]], axis=1))
            hm = np.zeros((S, HIST), np.float32)
        else:
            xloc = np.ascontiguousarray(xT[:, lo - HIST: lo + TL])
            hm = np.ones((S, HIST), np.float32)
        maps.append({
            'i_x': xloc, 'i_piw': piw, 'i_pib': pib, 'i_pow': pow_, 'i_pob': pob,
            'i_A': A, 'i_BwT': BwT, 'i_C2': C2, 'i_Dp': Dp, 'i_hm': hm,
            'i_idf': idf, 'i_idb': idb, 'i_const': consts, 'i_rep16': rep16, 'i_ones': onesm,
            'i_wg': wg, 'i_wu': wu, 'i_wd': wd, 'i_sg': sg, 'i_su': su, 'i_sd2': sd2,
            'i_wq': wqT, 'i_wk': wkT, 'i_wv': wvT, 'i_wo': woT,
            'i_cos': cosT[:, lo:lo + TL].copy(), 'i_sin': sinT[:, lo:lo + TL].copy(),
            'i_gw': gwT, 'i_uw': uwT, 'i_dw': dwT,
            'i_gwn': gwN[lo:lo + TL].copy(), 'i_uwn': uwN[lo:lo + TL].copy(),
            'i_dwn': dwN[c * (D // NCORE):(c + 1) * (D // NCORE)].copy(), 'i_g2p': g2p,
        })
    return maps


_PHASES = ('A', 'B', 'C', 'D')

def kernel(**inputs):
    maps = _prep_inputs(inputs)
    nc, out_names = build_program(phases=_PHASES)
    res = run_bass_kernel_spmd(nc, maps, core_ids=list(range(NCORE)))
    shards = [np.asarray(r['out_shard']) for r in res.results]
    out = np.concatenate([s.T for s in shards], axis=0).reshape(1, T, D).astype(np.float32)
    return out


# version 05
# speedup vs baseline: 1.2206x; 1.2171x over previous
import contextlib
import numpy as np
import ml_dtypes
import concourse.bass as bass
import concourse.mybir as mybir
from concourse.bass import AutoArena
from concourse.bass_utils import run_bass_kernel_spmd

F32 = mybir.dt.float32
BF16 = mybir.dt.bfloat16
I32 = mybir.dt.int32
FP8 = mybir.dt.float8e4
ACT = mybir.ActivationFunctionType
ALU = mybir.AluOpType
AX = mybir.AxisListType

NCORE = 8
T = 8192
D = 2048
TL = T // NCORE
HIST = 64
TW = TL + HIST
LF = 64
S = 16
HID = 5632
II = 8192
HD = 128
NH = 16
DC = D // 128
NTT = (TW + 127) // 128
EPS_LN = 1e-5

ENGS = ('sync', 'vector', 'scalar', 'tensor', 'gpsimd')
DMA_Q = {'sync': 'qs', 'gpsimd': 'qg', 'vector': 'qv', 'scalar': 'qc'}
SEM_NAMES = ('qs', 'qg', 'qv', 'qc', 'v', 's', 't', 'g')
ENG_SEM = {'vector': 'v', 'scalar': 's', 'tensor': 't', 'gpsimd': 'g', 'sync': 'qs'}
COMPUTE_SEMS = ('v', 's', 't', 'g')


class Buf:
    __slots__ = ('wtok', 'rtoks', 'name')
    def __init__(self, name=''):
        self.wtok = None
        self.rtoks = []
        self.name = name


class P:
    def __init__(self, nc):
        self.nc = nc
        self.items = {e: [] for e in ENGS}
        self.cnt = {sm: 0 for sm in SEM_NAMES}
        self.fence_deps = {}

    def do(self, eng, fn, reads=(), writes=(), inc=1, sem=None, sreads=()):
        sem = sem or ENG_SEM[eng]
        for b in sreads:
            if b.wtok is not None and b.wtok[0] in ('v', 's', 'g') and b.wtok[1] >= self.cnt[b.wtok[0]]:
                self._settle(b.wtok[0])
        deps = dict(self.fence_deps)
        def need(tok):
            if tok is None:
                return
            sm, val = tok
            if sm == sem and sm in COMPUTE_SEMS:
                return
            if deps.get(sm, 0) < val:
                deps[sm] = val
        def need_s(tok):
            if tok is None:
                return
            sm, val = tok
            if sm == sem and sm in COMPUTE_SEMS:
                return
            if deps.get(sm, 0) < val:
                deps[sm] = val
        for b in reads:
            need(b.wtok)
        for b in sreads:
            need_s(b.wtok)
        for b in writes:
            need(b.wtok)
            for t_ in b.rtoks:
                need(t_)
        if sem in deps and sem in COMPUTE_SEMS:
            del deps[sem]
        for sm in ('v', 's', 'g'):
            if sm in deps and sm != sem:
                if deps[sm] >= self.cnt[sm]:
                    self._settle(sm)
                deps[sm] = min(deps[sm] + 1, self.cnt[sm])
        self.items[eng].append((deps, fn, sem, inc))
        self.cnt[sem] += inc
        tok = (sem, self.cnt[sem])
        for b in reads:
            b.rtoks.append(tok)
        for b in sreads:
            b.rtoks.append(tok)
        for b in writes:
            b.wtok = tok
            b.rtoks = []
        return tok

    def _settle(self, sm):
        eng = {'v': 'vector', 's': 'scalar', 'g': 'gpsimd'}[sm]
        st = self.settle_tile
        if sm == 'v':
            fn = lambda e: e.tensor_copy(out=st[0:1, 0:1], in_=st[0:1, 1:2])
        elif sm == 's':
            fn = lambda e: e.activation(out=st[0:1, 0:1], in_=st[0:1, 1:2], func=ACT.Copy)
        else:
            fn = lambda e: e.memset(st[0:1, 0:1], 0.0)
        self.items[eng].append(({}, fn, sm, 1))
        self.cnt[sm] += 1

    def dma(self, eng, out, in_, reads=(), writes=(), slow=False):
        q = DMA_Q[eng]
        if slow:
            fn = lambda e, o=out, i=in_: e.dma_start(out=o, in_=i, allow_slow_non_contiguous=True)
        else:
            fn = lambda e, o=out, i=in_: e.dma_start(out=o, in_=i)
        return self.do(eng, fn, reads=reads, writes=writes, inc=16, sem=q)

    def pad(self, sm, n):
        for _ in range(n):
            self._settle(sm)

    def fence(self):
        self.fence_deps = {sm: self.cnt[sm] for sm in SEM_NAMES if self.cnt[sm] > 0}

    def barrier_tok(self):
        return [(sm, self.cnt[sm]) for sm in SEM_NAMES if self.cnt[sm] > 0]

    def mk_emitter(self, its, sems, finals):
        def g(e):
            cur = {sm: 0 for sm in SEM_NAMES}
            for deps, fn, sem, inc in its:
                for sm in sorted(deps):
                    val = deps[sm]
                    if val > cur[sm]:
                        e.wait_ge(sems[sm], val)
                        cur[sm] = val
                fn(e).then_inc(sems[sem], inc)
            for sm, val in finals:
                if val > cur[sm]:
                    e.wait_ge(sems[sm], val)
                    cur[sm] = val
        return g


def build_program(phases=('A', 'B', 'C', 'D'), dbg=()):
    nc = bass.Bass("TRN2", target_bir_lowering=False)
    p = P(nc)

    def DI(name, shape, dt=F32):
        return nc.dram_tensor(name, shape, dt, kind="ExternalInput").ap()
    out_names = []
    def DO(name, shape, dt=F32):
        out_names.append(name)
        return nc.dram_tensor(name, shape, dt, kind="ExternalOutput").ap()

    i_x = DI("i_x", [D, TW])
    i_piw = DI("i_piw", [D, D]); i_pib = DI("i_pib", [D, 1])
    i_pow = DI("i_pow", [D, D]); i_pob = DI("i_pob", [D, 1])
    i_A = DI("i_A", [D, S]); i_BwT = DI("i_BwT", [D, S])
    i_C2 = DI("i_C2", [LF * S, D]); i_Dp = DI("i_Dp", [D, 1])
    i_hm = DI("i_hm", [S, HIST])
    i_idf = DI("i_idf", [128, 128], F32)
    i_idb = DI("i_idb", [128, 128], BF16)
    i_const = DI("i_const", [128, 8], F32)
    i_rep16 = DI("i_rep16", [16, 128], F32)
    i_ones = DI("i_ones", [128, 128], F32)
    i_wg = DI("i_wg", [D, HID], I32)
    i_wu = DI("i_wu", [D, HID], I32)
    i_wd = DI("i_wd", [HID, D], I32)
    i_sg = DI("i_sg", [HID, 1]); i_su = DI("i_su", [HID, 1]); i_sd2 = DI("i_sd2", [D, 1])
    i_wq = DI("i_wq", [D, D]); i_wk = DI("i_wk", [D, D]); i_wv = DI("i_wv", [D, D]); i_wo = DI("i_wo", [D, D])
    i_cos = DI("i_cos", [128, TL]); i_sin = DI("i_sin", [128, TL])
    i_gw = DI("i_gw", [D, II]); i_uw = DI("i_uw", [D, II]); i_dw = DI("i_dw", [II, D])
    i_gwn = DI("i_gwn", [TL, D]); i_uwn = DI("i_uwn", [TL, D]); i_dwn = DI("i_dwn", [D // NCORE, II])
    i_g2p = DI("i_g2p", [4, 128])

    d_xp = nc.dram_tensor("d_xp", [D, TW], F32).ap()
    d_s1 = nc.dram_tensor("d_s1", [D, TL], F32).ap()
    d_s2 = nc.dram_tensor("d_s2", [D, TL], F32).ap()
    d_s3 = nc.dram_tensor("d_s3", [D, TL], F32).ap()
    d_kin = nc.dram_tensor("d_kin", [D, TL], BF16).ap()
    d_kag = nc.dram_tensor("d_kag", [NCORE * D, TL], BF16).ap()
    d_vin = nc.dram_tensor("d_vin", [NH * TL, HD], BF16).ap()
    d_vag = nc.dram_tensor("d_vag", [NCORE * NH * TL, HD], BF16).ap()
    d_kmi = nc.dram_tensor("d_kmi", [D, 1], F32).ap()
    d_rm = nc.dram_tensor("d_rm", [128, 16], F32).ap()
    d_gsci = nc.dram_tensor("d_gsci", [TL, 128], F32).ap()
    d_gsco = nc.dram_tensor("d_gsco", [II, 128], F32).ap()
    d_usci = nc.dram_tensor("d_usci", [TL, 128], F32).ap()
    d_usco = nc.dram_tensor("d_usco", [II, 128], F32).ap()
    d_dsci = nc.dram_tensor("d_dsci", [D // NCORE, 512], F32).ap()
    d_dsco = nc.dram_tensor("d_dsco", [D, 512], F32).ap()
    d_hq = nc.dram_tensor("d_hq", [II, TL], BF16).ap()
    d_kmo = nc.dram_tensor("d_kmo", [D, 1], F32).ap()

    dbg_outs = {}
    if 'A' in dbg:
        dbg_outs['o_s1'] = DO("o_s1", [D, TL])
    if 'B' in dbg:
        dbg_outs['o_s2'] = DO("o_s2", [D, TL])
    if 'C' in dbg:
        dbg_outs['o_s3'] = DO("o_s3", [D, TL])
    if 'Dtaps' in dbg:
        dbg_outs['o_hq'] = DO("o_hq", [II, TL], BF16)
        dbg_outs['o_x3f'] = DO("o_x3f", [128, DC * TL], BF16)
    if 'Ctaps' in dbg:
        dbg_outs['o_ao'] = DO("o_ao", [128, 8 * D], BF16)
        dbg_outs['o_negm'] = DO("o_negm", [128, NH])
        dbg_outs['o_x4b'] = DO("o_x4b", [128, DC * TL], BF16)
        dbg_outs['o_qb'] = DO("o_qb", [128, NH * TL], BF16)
        dbg_outs['o_x8f'] = DO("o_x8f", [128, DC * TL], BF16)
        dbg_outs['o_x2ln'] = DO("o_x2ln", [128, DC * TL])
        dbg_outs['o_cm'] = DO("o_cm", [128, TL])
        dbg_outs['o_rm'] = DO("o_rm", [128, 16])
        dbg_outs['o_srow'] = DO("o_srow", [1, 2 * TL])
        dbg_outs['o_sb4'] = DO("o_sb4", [128, TL])
        dbg_outs['o_rsb4'] = DO("o_rsb4", [128, TL])
    if 'Ataps' in dbg:
        dbg_outs['o_x'] = DO("o_x", [128, DC * TW])
        dbg_outs['o_ln'] = DO("o_ln", [128, DC * TW], BF16)
        dbg_outs['o_xp'] = DO("o_xp", [D, TW])
        dbg_outs['o_dlt'] = DO("o_dlt", [16, TW])
        dbg_outs['o_esk'] = DO("o_esk", [128, 8 * TL], BF16)
        dbg_outs['o_yb'] = DO("o_yb", [128, DC * TL], BF16)
    final_out = DO("out_shard", [D, TL])

    stack = contextlib.ExitStack()
    sems = {sm: stack.enter_context(nc.semaphore("sem_" + sm)) for sm in SEM_NAMES}
    psF = stack.enter_context(nc.psum_tensor("psF", [128, 3584], F32))
    psB = stack.enter_context(nc.psum_tensor("psB", [128, 1024], BF16))

    pers = AutoArena(nc)
    t_settle = pers.sbuf_tensor("t_settle", [128, 4], F32)
    p.settle_tile = t_settle
    t_idf = pers.sbuf_tensor("t_idf", [128, 128], F32)
    t_idb = pers.sbuf_tensor("t_idb", [128, 128], BF16)
    t_const = pers.sbuf_tensor("t_const", [128, 8], F32)
    t_rep16 = pers.sbuf_tensor("t_rep16", [16, 128], F32)
    t_ones = pers.sbuf_tensor("t_ones", [128, 128], F32)
    pers.allocate()

    bufs = {}
    def B(name):
        if name not in bufs:
            bufs[name] = Buf(name)
        return bufs[name]
    pbank = [B(f"psF{i}") for i in range(7)]
    pbB = B("psB")

    def emit_ln2(tSQ, tRow, tMB, tRB, tNrm, xc, w_total, lnb, blnb, bx, tag):
        bSQ = [B(tag + '_sq0'), B(tag + '_sq1')]
        bRow = B(tag + '_row'); bMB = B(tag + '_mb'); bRB = B(tag + '_rb'); bNrm = B(tag + '_nrm')
        nbl = [(n0, min(512, w_total - n0)) for n0 in range(0, w_total, 512)]
        for (n0, nw) in nbl:
            pbx = pbank[4]; pbq = pbank[5]
            for c in range(DC):
                db = c % 2
                p.do('scalar', lambda e, c=c, n0=n0, nw=nw, db=db: e.activation(
                    out=tSQ[db][:, 0:nw], in_=xc(c)[:, n0:n0 + nw], func=ACT.Square),
                    reads=[bx], writes=[bSQ[db]])
                p.do('tensor', lambda e, c=c, n0=n0, nw=nw: e.matmul(
                    out=psF[0:1, 2048:2048 + nw], lhsT=t_ones[:, 0:1], rhs=xc(c)[:, n0:n0 + nw],
                    start=(c == 0), stop=(c == DC - 1)),
                    reads=[bx, B('ones')], writes=[pbx])
                p.do('tensor', lambda e, c=c, nw=nw, db=db: e.matmul(
                    out=psF[0:1, 2560:2560 + nw], lhsT=t_ones[:, 0:1], rhs=tSQ[db][:, 0:nw],
                    start=(c == 0), stop=(c == DC - 1)),
                    reads=[bSQ[db], B('ones')], writes=[pbq])
            p.do('vector', lambda e, nw=nw: e.tensor_scalar(out=tRow[0:1, 0:nw], in0=psF[0:1, 2048:2048 + nw],
                 scalar1=1.0 / D, scalar2=None, op0=ALU.mult), reads=[pbx], writes=[bRow])
            p.do('vector', lambda e, nw=nw: e.tensor_scalar(out=tRow[0:1, 512:512 + nw], in0=psF[0:1, 2560:2560 + nw],
                 scalar1=1.0 / D, scalar2=None, op0=ALU.mult), reads=[pbq], writes=[bRow])
            p.do('vector', lambda e, nw=nw: e.tensor_tensor(out=tRow[0:1, 1024:1024 + nw], in0=tRow[0:1, 0:nw],
                 in1=tRow[0:1, 0:nw], op=ALU.mult), reads=[bRow], writes=[bRow])
            p.do('vector', lambda e, nw=nw: e.tensor_tensor(out=tRow[0:1, 1536:1536 + nw], in0=tRow[0:1, 512:512 + nw],
                 in1=tRow[0:1, 1024:1024 + nw], op=ALU.subtract), reads=[bRow], writes=[bRow])
            p.do('scalar', lambda e, nw=nw: e.activation(out=tRow[0:1, 2048:2048 + nw], in_=tRow[0:1, 1536:1536 + nw],
                 func=ACT.Sqrt, bias=t_const[0:1, 0:1], scale=1.0), reads=[bRow], sreads=[B('const')], writes=[bRow])
            p.do('vector', lambda e, nw=nw: e.reciprocal(out=tRow[0:1, 1536:1536 + nw], in_=tRow[0:1, 2048:2048 + nw]),
                 reads=[bRow], writes=[bRow])
            p.do('tensor', lambda e, nw=nw: e.matmul(
                out=psF[:, 0:nw], lhsT=t_ones[0:1, 0:128], rhs=tRow[0:1, 0:nw], start=True, stop=True),
                reads=[bRow, B('ones')], writes=[pbank[0]])
            p.do('vector', lambda e, nw=nw: e.tensor_copy(out=tMB[:, 0:nw], in_=psF[:, 0:nw]),
                 reads=[pbank[0]], writes=[bMB])
            p.do('tensor', lambda e, nw=nw: e.matmul(
                out=psF[:, 512:512 + nw], lhsT=t_ones[0:1, 0:128], rhs=tRow[0:1, 1536:1536 + nw], start=True, stop=True),
                reads=[bRow, B('ones')], writes=[pbank[1]])
            p.do('vector', lambda e, nw=nw: e.tensor_copy(out=tRB[:, 0:nw], in_=psF[:, 512:512 + nw]),
                 reads=[pbank[1]], writes=[bRB])
            for c in range(DC):
                p.do('vector', lambda e, c=c, n0=n0, nw=nw: e.tensor_tensor(
                    out=tNrm[:, 0:nw], in0=xc(c)[:, n0:n0 + nw], in1=tMB[:, 0:nw], op=ALU.subtract),
                    reads=[bx, bMB], writes=[bNrm])
                p.do('vector', lambda e, c=c, n0=n0, nw=nw: e.tensor_tensor(
                    out=lnb(c)[:, n0:n0 + nw], in0=tNrm[:, 0:nw], in1=tRB[:, 0:nw], op=ALU.mult),
                    reads=[bNrm, bRB], writes=[blnb])


    p.dma('sync', t_idf[:, :], i_idf[:, :], writes=[B('idf')])
    p.dma('sync', t_idb[:, :], i_idb[:, :], writes=[B('idb')])
    p.dma('sync', t_const[:, :], i_const[:, :], writes=[B('const')])
    p.dma('sync', t_rep16[:, :], i_rep16[:, :], writes=[B('rep16')])
    p.dma('sync', t_ones[:, :], i_ones[:, :], writes=[B('ones')])

    if 'A' in phases:
        arL = AutoArena(nc)
        tLNB = arL.sbuf_tensor("A_LNB", [128, DC * TW], BF16)
        tEskB = arL.sbuf_tensor("A_eskB", [128, 8 * TL], BF16)
        tPib = arL.sbuf_tensor("A_pib", [128, DC], F32)
        tPob = arL.sbuf_tensor("A_pob", [128, DC], F32)
        tDp = arL.sbuf_tensor("A_dp", [128, DC], F32)
        tHm = arL.sbuf_tensor("A_hm", [16, HIST], F32)
        tBwf = arL.sbuf_tensor("A_bwf", [128, DC * S], F32)
        tAf = arL.sbuf_tensor("A_af", [128, DC * S], F32)
        arL.allocate()

        p.dma('sync', tPib[:, :].rearrange("p (c o) -> p c o", o=1), i_pib.rearrange("(c p) o -> p c o", p=128), writes=[B('A_pib')], slow=True)
        p.dma('sync', tPob[:, :].rearrange("p (c o) -> p c o", o=1), i_pob.rearrange("(c p) o -> p c o", p=128), writes=[B('A_pob')], slow=True)
        p.dma('sync', tDp[:, :].rearrange("p (c o) -> p c o", o=1), i_Dp.rearrange("(c p) o -> p c o", p=128), writes=[B('A_dp')], slow=True)
        p.dma('sync', tHm[:, :], i_hm[:, :], writes=[B('A_hm')])
        p.dma('sync', tAf[:, :].rearrange("p (c s) -> p c s", c=DC), i_A.rearrange("(c p) s -> p c s", p=128), writes=[B('A_af')], slow=True)
        p.dma('sync', tBwf[:, :].rearrange("p (c s) -> p c s", c=DC), i_BwT.rearrange("(c p) s -> p c s", p=128), writes=[B('A_bwf')], slow=True)

        ar1 = AutoArena(nc)
        tX = ar1.sbuf_tensor("A_X", [128, DC * TW], F32)
        tSQ = [ar1.sbuf_tensor(f"A_sq{i}", [128, 512], F32) for i in range(2)]
        tRow = ar1.sbuf_tensor("A_row", [1, 512 * 5], F32)
        tMB = ar1.sbuf_tensor("A_mb", [128, 512], F32)
        tRB = ar1.sbuf_tensor("A_rb", [128, 512], F32)
        tNrm = ar1.sbuf_tensor("A_nrm", [128, 512], F32)
        ar1.allocate()

        p.dma('sync', tX[:, :].rearrange("p (c t) -> p c t", c=DC),
              i_x.rearrange("(c p) t -> p c t", p=128), writes=[B('A_X')])
        bLNB = B('A_LNB')

        emit_ln2(tSQ, tRow, tMB, tRB, tNrm,
                 lambda c: tX[:, c * TW:(c + 1) * TW], TW,
                 lambda c: tLNB[:, c * TW:(c + 1) * TW], bLNB, B('A_X'), 'Aln')
        p.fence()
        ar1.free()

        ar2 = AutoArena(nc)
        tXPB = ar2.sbuf_tensor("A_XPB", [128, DC * TW], BF16)
        tP = ar2.sbuf_tensor("A_P", [128, DC * 8 * 128], BF16)
        tWf = [ar2.sbuf_tensor(f"A_wf{i}", [128, D], F32) for i in range(2)]
        tWb = [ar2.sbuf_tensor(f"A_wb{i}", [128, D], BF16) for i in range(2)]
        tSc = [ar2.sbuf_tensor(f"A_sc{i}", [128, TW], F32) for i in range(2)]
        tBwb = ar2.sbuf_tensor("A_bwb", [128, DC * S], BF16)
        tDlt = ar2.sbuf_tensor("A_dlt", [16, TW], F32)
        tDla = ar2.sbuf_tensor("A_dla", [16, TW], F32)
        tDrep = ar2.sbuf_tensor("A_drep", [128, TW], F32)
        tSig = ar2.sbuf_tensor("A_sig", [128, DC * S], F32)
        tPrev = ar2.sbuf_tensor("A_prev", [128, DC * S], F32)
        tEt = [ar2.sbuf_tensor(f"A_et{i}", [128, TW], F32) for i in range(2)]
        tEsk = ar2.sbuf_tensor("A_esk", [128, TL], F32)
        ar2.allocate()

        NBL = [(0, 512), (512, 512), (1024, TW - 1024)]
        bXPB = B('A_XPB')
        for m in range(DC):
            db = m % 2
            bWf = B(f'A_wf{db}'); bWb = B(f'A_wb{db}'); bSc = B(f'A_sc{db}')
            p.dma('sync', tWf[db][:, :].rearrange("p (c k) -> p c k", c=DC),
                  i_piw.rearrange("(c p) mm -> p c mm", p=128)[:, :, m * 128:(m + 1) * 128],
                  writes=[bWf])
            p.do('vector', lambda e, db=db: e.tensor_copy(out=tWb[db][:, :], in_=tWf[db][:, :]),
                 reads=[bWf], writes=[bWb])
            for (n0, nw) in NBL:
                pb = pbank[2 + n0 // 512]
                for c in range(DC):
                    p.do('tensor', lambda e, c=c, n0=n0, nw=nw, db=db: e.matmul(
                        out=psF[:, 1024 + n0: 1024 + n0 + nw],
                        lhsT=tWb[db][:, c * 128:(c + 1) * 128],
                        rhs=tLNB[:, c * TW + n0: c * TW + n0 + nw],
                        start=(c == 0), stop=(c == DC - 1)),
                        reads=[bWb, bLNB], writes=[pb])
                p.do('vector', lambda e, m=m, n0=n0, nw=nw, db=db: e.tensor_scalar(
                    out=tSc[db][:, n0:n0 + nw], in0=psF[:, 1024 + n0:1024 + n0 + nw],
                    scalar1=tPib[:, m:m + 1], scalar2=None, op0=ALU.add),
                    reads=[pb], sreads=[B('A_pib')], writes=[bSc])
            p.do('vector', lambda e, m=m, db=db: e.tensor_copy(
                out=tXPB[:, m * TW:(m + 1) * TW], in_=tSc[db][:, :]),
                reads=[bSc], writes=[bXPB])
            p.dma('sync', d_xp[m * 128:(m + 1) * 128, :], tSc[db][:, :], reads=[bSc], writes=[B('d_xp')])

        if 'Ataps' in dbg:
            for m in range(DC):
                p.dma('gpsimd', dbg_outs['o_xp'][m * 128:(m + 1) * 128, :], d_xp[m * 128:(m + 1) * 128, :],
                      reads=[B('d_xp')], writes=[B('o_xp')])
        bBwb = B('A_bwb')
        p.do('vector', lambda e: e.tensor_copy(out=tBwb[:, :], in_=tBwf[:, :]), reads=[B('A_bwf')], writes=[bBwb])
        bDlt = B('A_dlt'); bDla = B('A_dla')
        for (n0, nw) in NBL:
            pb = pbank[n0 // 512]
            for c in range(DC):
                p.do('tensor', lambda e, c=c, n0=n0, nw=nw: e.matmul(
                    out=psF[0:16, n0:n0 + nw],
                    lhsT=tBwb[:, c * S:(c + 1) * S],
                    rhs=tXPB[:, c * TW + n0: c * TW + n0 + nw],
                    start=(c == 0), stop=(c == DC - 1)),
                    reads=[bBwb, bXPB], writes=[pb])
            p.do('scalar', lambda e, n0=n0, nw=nw: e.activation(out=tDla[:, n0:n0 + nw], in_=psF[0:16, n0:n0 + nw], func=ACT.Abs),
                 reads=[pb], writes=[bDla])
            p.do('scalar', lambda e, n0=n0, nw=nw: e.activation(out=tDla[:, n0:n0 + nw], in_=tDla[:, n0:n0 + nw], func=ACT.Exp, scale=-1.0),
                 reads=[bDla], writes=[bDla])
            p.do('scalar', lambda e, n0=n0, nw=nw: e.activation(out=tDla[:, n0:n0 + nw], in_=tDla[:, n0:n0 + nw], func=ACT.Ln, bias=1.0, scale=1.0),
                 reads=[bDla], writes=[bDla])
            p.do('scalar', lambda e, n0=n0, nw=nw: e.activation(out=tDlt[:, n0:n0 + nw], in_=psF[0:16, n0:n0 + nw], func=ACT.Relu),
                 reads=[pb], writes=[bDlt])
            p.do('vector', lambda e, n0=n0, nw=nw: e.tensor_tensor(out=tDlt[:, n0:n0 + nw], in0=tDlt[:, n0:n0 + nw], in1=tDla[:, n0:n0 + nw], op=ALU.add),
                 reads=[bDla], writes=[bDlt])
        p.do('vector', lambda e: e.tensor_tensor(out=tDlt[:, 0:HIST], in0=tDlt[:, 0:HIST], in1=tHm[:, :], op=ALU.mult),
             reads=[B('A_hm')], writes=[bDlt])
        bDrep = B('A_drep')
        for (n0, nw) in NBL:
            pb = pbank[3 + n0 // 512]
            p.do('tensor', lambda e, n0=n0, nw=nw: e.matmul(
                out=psF[:, 1536 + n0:1536 + n0 + nw], lhsT=t_rep16[:, :], rhs=tDlt[:, n0:n0 + nw],
                start=True, stop=True),
                reads=[B('rep16'), bDlt], writes=[pb])
            p.do('vector', lambda e, n0=n0, nw=nw: e.tensor_copy(out=tDrep[:, n0:n0 + nw], in_=psF[:, 1536 + n0:1536 + n0 + nw]),
                 reads=[pb], writes=[bDrep])

        if 'Ataps' in dbg:
            p.dma('gpsimd', dbg_outs['o_dlt'][:, :], tDlt[:, :], reads=[bDlt], writes=[B('o_dlt')])
        bSig = B('A_sig'); bPrev = B('A_prev'); bP = B('A_P')
        p.do('scalar', lambda e: e.activation(out=tSig[:, :], in_=tAf[:, :], func=ACT.Sigmoid), reads=[B('A_af')], writes=[bSig])
        p.do('gpsimd', lambda e: e.memset(tPrev[:, :], 1.0), writes=[bPrev])
        for l in range(LF):
            j, jj = l // 8, l % 8
            p.do('vector', lambda e, j=j, jj=jj: e.tensor_copy(
                out=tP[:, :].rearrange("p (c q) -> p c q", q=8 * 128)[:, :, j * 128 + jj * 16: j * 128 + jj * 16 + 16],
                in_=tPrev[:, :].rearrange("p (c s) -> p c s", c=DC)),
                reads=[bPrev], writes=[bP])
            if l < LF - 1:
                p.do('vector', lambda e: e.tensor_tensor(out=tPrev[:, :], in0=tPrev[:, :], in1=tSig[:, :], op=ALU.mult),
                     reads=[bSig], writes=[bPrev])

        bEskB = B('A_eskB')
        for j in range(8):
            db = j % 2
            bEt = B(f'A_et{db}'); bEsk = B('A_esk')
            for (n0, nw) in NBL:
                pb = pbank[3 + n0 // 512]
                for c in range(DC):
                    p.do('tensor', lambda e, c=c, j=j, n0=n0, nw=nw: e.matmul(
                        out=psF[:, 1536 + n0:1536 + n0 + nw],
                        lhsT=tP[:, (c * 8 + j) * 128:(c * 8 + j + 1) * 128],
                        rhs=tXPB[:, c * TW + n0:c * TW + n0 + nw],
                        start=(c == 0), stop=(c == DC - 1)),
                        reads=[bP, bXPB], writes=[pb])
                p.do('vector', lambda e, n0=n0, nw=nw, db=db: e.tensor_tensor(
                    out=tEt[db][:, n0:n0 + nw], in0=psF[:, 1536 + n0:1536 + n0 + nw],
                    in1=tDrep[:, n0:n0 + nw], op=ALU.mult),
                    reads=[pb, bDrep], writes=[bEt])
            for jj in range(8):
                l = 8 * j + jj
                p.dma('gpsimd', tEsk[16 * jj:16 * (jj + 1), :],
                      tEt[db][16 * jj:16 * (jj + 1), HIST - l: HIST - l + TL],
                      reads=[bEt], writes=[bEsk])
            p.do('vector', lambda e, j=j: e.tensor_copy(out=tEskB[:, j * TL:(j + 1) * TL], in_=tEsk[:, :]),
                 reads=[bEsk], writes=[bEskB])
        if 'Ataps' in dbg:
            p.dma('gpsimd', dbg_outs['o_esk'][:, :], tEskB[:, :], reads=[bEskB], writes=[B('o_esk')])
        p.fence()
        ar2.free()

        ar3 = AutoArena(nc)
        tC2f = ar3.sbuf_tensor("A_c2f", [128, D], F32)
        tC2b = ar3.sbuf_tensor("A_c2b", [128, 8 * D], BF16)
        tYB = ar3.sbuf_tensor("A_yb", [128, DC * TL], BF16)
        tWf3 = [ar3.sbuf_tensor(f"A3_wf{i}", [128, D], F32) for i in range(2)]
        tWb3 = [ar3.sbuf_tensor(f"A3_wb{i}", [128, D], BF16) for i in range(2)]
        tSc3 = [ar3.sbuf_tensor(f"A3_sc{i}", [128, TL], F32) for i in range(2)]
        tXPm = [ar3.sbuf_tensor(f"A3_xpm{i}", [128, TL], F32) for i in range(2)]
        tXm = [ar3.sbuf_tensor(f"A3_xm{i}", [128, TL], F32) for i in range(2)]
        ar3.allocate()

        bC2b = B('A_c2b')
        for k in range(8):
            bc2f = B('A_c2f')
            p.dma('sync', tC2f[:, :], i_C2[k * 128:(k + 1) * 128, :], writes=[bc2f])
            p.do('vector', lambda e, k=k: e.tensor_copy(out=tC2b[:, k * D:(k + 1) * D], in_=tC2f[:, :]),
                 reads=[bc2f], writes=[bC2b])
        bYB = B('A_yb')
        for m in range(DC):
            db = m % 2
            bXPm = B(f'A3_xpm{db}'); bSc = B(f'A3_sc{db}')
            p.dma('sync', tXPm[db][:, :], d_xp[m * 128:(m + 1) * 128, HIST:TW], reads=[B('d_xp')], writes=[bXPm])
            p.do('vector', lambda e, m=m, db=db: e.tensor_scalar(
                out=tXPm[db][:, :], in0=tXPm[db][:, :], scalar1=tDp[:, m:m + 1], scalar2=None, op0=ALU.mult),
                sreads=[B('A_dp')], writes=[bXPm])
            for nn in range(2):
                pb = pbank[nn]
                for k in range(8):
                    p.do('tensor', lambda e, k=k, m=m, nn=nn: e.matmul(
                        out=psF[:, nn * 512:(nn + 1) * 512],
                        lhsT=tC2b[:, k * D + m * 128: k * D + (m + 1) * 128],
                        rhs=tEskB[:, k * TL + nn * 512: k * TL + (nn + 1) * 512],
                        start=(k == 0), stop=(k == 7)),
                        reads=[bC2b, bEskB], writes=[pb])
                p.do('vector', lambda e, m=m, nn=nn, db=db: e.tensor_tensor(
                    out=tSc3[db][:, nn * 512:(nn + 1) * 512], in0=psF[:, nn * 512:(nn + 1) * 512],
                    in1=tXPm[db][:, nn * 512:(nn + 1) * 512], op=ALU.add),
                    reads=[pbank[nn], bXPm], writes=[bSc])
            p.do('vector', lambda e, m=m, db=db: e.tensor_copy(
                out=tYB[:, m * TL:(m + 1) * TL], in_=tSc3[db][:, :]),
                reads=[bSc], writes=[bYB])

        if 'Ataps' in dbg:
            p.dma('gpsimd', dbg_outs['o_yb'][:, :], tYB[:, :], reads=[bYB], writes=[B('o_yb')])
        for m in range(DC):
            db = m % 2
            bWf = B(f'A3_wf{db}'); bWb = B(f'A3_wb{db}'); bXm = B(f'A3_xm{db}'); bSc = B(f'A3_sc{db}')
            p.dma('sync', tWf3[db][:, :].rearrange("p (c k) -> p c k", c=DC),
                  i_pow.rearrange("(c p) mm -> p c mm", p=128)[:, :, m * 128:(m + 1) * 128],
                  writes=[bWf])
            p.do('vector', lambda e, db=db: e.tensor_copy(out=tWb3[db][:, :], in_=tWf3[db][:, :]),
                 reads=[bWf], writes=[bWb])
            p.dma('gpsimd', tXm[db][:, :], i_x[m * 128:(m + 1) * 128, HIST:TW], writes=[bXm])
            for nn in range(2):
                pb = pbank[2 + nn]
                for c in range(DC):
                    p.do('tensor', lambda e, c=c, nn=nn, db=db: e.matmul(
                        out=psF[:, 1024 + nn * 512:1024 + (nn + 1) * 512],
                        lhsT=tWb3[db][:, c * 128:(c + 1) * 128],
                        rhs=tYB[:, c * TL + nn * 512: c * TL + (nn + 1) * 512],
                        start=(c == 0), stop=(c == DC - 1)),
                        reads=[bWb, bYB], writes=[pb])
                p.do('vector', lambda e, m=m, nn=nn, db=db: e.tensor_scalar(
                    out=tSc3[db][:, nn * 512:(nn + 1) * 512], in0=psF[:, 1024 + nn * 512:1024 + (nn + 1) * 512],
                    scalar1=tPob[:, m:m + 1], scalar2=None, op0=ALU.add),
                    reads=[pb], sreads=[B('A_pob')], writes=[bSc])
                p.do('vector', lambda e, m=m, nn=nn, db=db: e.tensor_tensor(
                    out=tSc3[db][:, nn * 512:(nn + 1) * 512], in0=tSc3[db][:, nn * 512:(nn + 1) * 512],
                    in1=tXm[db][:, nn * 512:(nn + 1) * 512], op=ALU.add),
                    reads=[bXm], writes=[bSc])
                p.do('vector', lambda e, m=m, nn=nn, db=db: e.tensor_tensor(
                    out=tSc3[db][:, nn * 512:(nn + 1) * 512], in0=tSc3[db][:, nn * 512:(nn + 1) * 512],
                    in1=tLNB[:, m * TW + HIST + nn * 512: m * TW + HIST + (nn + 1) * 512], op=ALU.add),
                    reads=[bLNB], writes=[bSc])
            p.dma('sync', d_s1[m * 128:(m + 1) * 128, :], tSc3[db][:, :], reads=[bSc], writes=[B('d_s1')])

        if 'A' in dbg:
            for m in range(DC):
                p.dma('gpsimd', dbg_outs['o_s1'][m * 128:(m + 1) * 128, :], d_s1[m * 128:(m + 1) * 128, :],
                      reads=[B('d_s1')], writes=[B('o_s1')])
        p.fence()
        ar3.free()
        arL.free()

    HC = HID // 128
    if 'B' in phases:
        arBL = AutoArena(nc)
        tX1B = arBL.sbuf_tensor("B_x1b", [128, DC * TL], BF16)
        tSG = arBL.sbuf_tensor("B_sg", [128, HC], F32)
        tSU = arBL.sbuf_tensor("B_su", [128, HC], F32)
        tSD = arBL.sbuf_tensor("B_sd", [128, DC], F32)
        arBL.allocate()
        p.dma('sync', tSG[:, :].rearrange("p (c o) -> p c o", o=1), i_sg.rearrange("(c p) o -> p c o", p=128), writes=[B('B_sg')], slow=True)
        p.dma('sync', tSU[:, :].rearrange("p (c o) -> p c o", o=1), i_su.rearrange("(c p) o -> p c o", p=128), writes=[B('B_su')], slow=True)
        p.dma('sync', tSD[:, :].rearrange("p (c o) -> p c o", o=1), i_sd2.rearrange("(c p) o -> p c o", p=128), writes=[B('B_sd')], slow=True)

        arB1 = AutoArena(nc)
        tX1 = arB1.sbuf_tensor("B_x1", [128, DC * TL], F32)
        tSQb = [arB1.sbuf_tensor(f"B_sq{i}", [128, 512], F32) for i in range(2)]
        tRowb = arB1.sbuf_tensor("B_row", [1, 512 * 5], F32)
        tMBb = arB1.sbuf_tensor("B_mb", [128, 512], F32)
        tRBb = arB1.sbuf_tensor("B_rb", [128, 512], F32)
        tNrmb = arB1.sbuf_tensor("B_nrm", [128, 512], F32)
        arB1.allocate()
        p.dma('sync', tX1[:, :].rearrange("p (c t) -> p c t", c=DC),
              d_s1.rearrange("(c p) t -> p c t", p=128), reads=[B('d_s1')], writes=[B('B_x1')])
        emit_ln2(tSQb, tRowb, tMBb, tRBb, tNrmb,
                 lambda c: tX1[:, c * TL:(c + 1) * TL], TL,
                 lambda c: tX1B[:, c * TL:(c + 1) * TL], B('B_x1b'), B('B_x1'), 'Bln')
        p.fence()
        arB1.free()

        arB2 = AutoArena(nc)
        tHH = arB2.sbuf_tensor("B_hh", [128, HC * TL], BF16)
        tIW = [arB2.sbuf_tensor(f"B_iw{i}", [128, D], I32) for i in range(2)]
        tIU = [arB2.sbuf_tensor(f"B_iu{i}", [128, D], I32) for i in range(2)]
        tGW = [arB2.sbuf_tensor(f"B_gw{i}", [128, D], BF16) for i in range(2)]
        tUW = [arB2.sbuf_tensor(f"B_uw{i}", [128, D], BF16) for i in range(2)]
        tGb = arB2.sbuf_tensor("B_gb", [128, TL], BF16)
        tUb = arB2.sbuf_tensor("B_ub", [128, TL], BF16)
        arB2.allocate()
        bHH = B('B_hh')
        for hc in range(HC):
            db = hc % 2
            bIW = B(f'B_iw{db}'); bIU = B(f'B_iu{db}'); bGW = B(f'B_gw{db}'); bUW = B(f'B_uw{db}')
            p.dma('sync', tIW[db][:, :].rearrange("p (c k) -> p c k", c=DC),
                  i_wg.rearrange("(c p) mm -> p c mm", p=128)[:, :, hc * 128:(hc + 1) * 128], writes=[bIW])
            p.dma('gpsimd', tIU[db][:, :].rearrange("p (c k) -> p c k", c=DC),
                  i_wu.rearrange("(c p) mm -> p c mm", p=128)[:, :, hc * 128:(hc + 1) * 128], writes=[bIU])
            p.do('vector', lambda e, db=db: e.tensor_scalar(out=tGW[db][:, :], in0=tIW[db][:, :],
                 scalar1=-128.0, scalar2=None, op0=ALU.add), reads=[bIW], writes=[bGW])
            p.do('vector', lambda e, db=db: e.tensor_scalar(out=tUW[db][:, :], in0=tIU[db][:, :],
                 scalar1=-128.0, scalar2=None, op0=ALU.add), reads=[bIU], writes=[bUW])
            for nn in range(2):
                pbg = pbank[nn]; pbu = pbank[2 + nn]
                for c in range(DC):
                    p.do('tensor', lambda e, c=c, nn=nn, db=db: e.matmul(
                        out=psF[:, nn * 512:(nn + 1) * 512],
                        lhsT=tGW[db][:, c * 128:(c + 1) * 128],
                        rhs=tX1B[:, c * TL + nn * 512: c * TL + (nn + 1) * 512],
                        start=(c == 0), stop=(c == DC - 1)),
                        reads=[bGW, B('B_x1b')], writes=[pbg])
                    p.do('tensor', lambda e, c=c, nn=nn, db=db: e.matmul(
                        out=psF[:, 1024 + nn * 512:1024 + (nn + 1) * 512],
                        lhsT=tUW[db][:, c * 128:(c + 1) * 128],
                        rhs=tX1B[:, c * TL + nn * 512: c * TL + (nn + 1) * 512],
                        start=(c == 0), stop=(c == DC - 1)),
                        reads=[bUW, B('B_x1b')], writes=[pbu])
                p.do('scalar', lambda e, hc=hc, nn=nn: e.activation(
                    out=tGb[:, nn * 512:(nn + 1) * 512], in_=psF[:, nn * 512:(nn + 1) * 512],
                    func=ACT.Silu, scale=tSG[:, hc:hc + 1]),
                    reads=[pbg], sreads=[B('B_sg')], writes=[B('B_gb')])
                p.do('scalar', lambda e, hc=hc, nn=nn: e.activation(
                    out=tUb[:, nn * 512:(nn + 1) * 512], in_=psF[:, 1024 + nn * 512:1024 + (nn + 1) * 512],
                    func=ACT.Copy, scale=tSU[:, hc:hc + 1]),
                    reads=[pbu], sreads=[B('B_su')], writes=[B('B_ub')])
                p.do('vector', lambda e, hc=hc, nn=nn: e.tensor_tensor(
                    out=tHH[:, hc * TL + nn * 512: hc * TL + (nn + 1) * 512],
                    in0=tGb[:, nn * 512:(nn + 1) * 512], in1=tUb[:, nn * 512:(nn + 1) * 512], op=ALU.mult),
                    reads=[B('B_gb'), B('B_ub')], writes=[bHH])
        p.fence()
        arB2_weights_done = True
        arB3 = AutoArena(nc)
        tID = [arB3.sbuf_tensor(f"B_id{i}", [128, 256], I32) for i in range(2)]
        tDW = [arB3.sbuf_tensor(f"B_dw{i}", [128, 256], BF16) for i in range(2)]
        tRes = [arB3.sbuf_tensor(f"B_res{i}", [128, TL], F32) for i in range(2)]
        tSc2 = [arB3.sbuf_tensor(f"B_sc{i}", [128, TL], F32) for i in range(2)]
        arB3.allocate()
        for mg in range(8):
            for k in range(HC):
                db = k % 2
                bID = B(f'B_id{db}'); bDW = B(f'B_dw{db}')
                p.dma('sync', tID[db][:, :], i_wd[k * 128:(k + 1) * 128, mg * 256:(mg + 1) * 256], writes=[bID])
                p.do('vector', lambda e, db=db: e.tensor_scalar(out=tDW[db][:, :], in0=tID[db][:, :],
                     scalar1=-128.0, scalar2=None, op0=ALU.add), reads=[bID], writes=[bDW])
                for mm in range(2):
                    for nn in range(2):
                        pb = pbank[mm * 2 + nn]
                        p.do('tensor', lambda e, k=k, mm=mm, nn=nn, db=db: e.matmul(
                            out=psF[:, (mm * 2 + nn) * 512:(mm * 2 + nn + 1) * 512],
                            lhsT=tDW[db][:, mm * 128:(mm + 1) * 128],
                            rhs=tHH[:, k * TL + nn * 512: k * TL + (nn + 1) * 512],
                            start=(k == 0), stop=(k == HC - 1)),
                            reads=[bDW, bHH], writes=[pb])
            for mm in range(2):
                m = mg * 2 + mm
                db = mm % 2
                bRes = B(f'B_res{db}'); bSc = B(f'B_sc{db}')
                p.dma('gpsimd', tRes[db][:, :], d_s1[m * 128:(m + 1) * 128, :], reads=[B('d_s1')], writes=[bRes])
                for nn in range(2):
                    pb = pbank[mm * 2 + nn]
                    p.do('vector', lambda e, m=m, mm=mm, nn=nn, db=db: e.tensor_scalar(
                        out=tSc2[db][:, nn * 512:(nn + 1) * 512],
                        in0=psF[:, (mm * 2 + nn) * 512:(mm * 2 + nn + 1) * 512],
                        scalar1=tSD[:, m:m + 1], scalar2=None, op0=ALU.mult),
                        reads=[pb], sreads=[B('B_sd')], writes=[bSc])
                    p.do('vector', lambda e, nn=nn, db=db: e.tensor_tensor(
                        out=tSc2[db][:, nn * 512:(nn + 1) * 512], in0=tSc2[db][:, nn * 512:(nn + 1) * 512],
                        in1=tRes[db][:, nn * 512:(nn + 1) * 512], op=ALU.add),
                        reads=[bRes], writes=[bSc])
                p.dma('sync', d_s2[m * 128:(m + 1) * 128, :], tSc2[db][:, :], reads=[bSc], writes=[B('d_s2')])
        if 'B' in dbg:
            for m in range(DC):
                p.dma('gpsimd', dbg_outs['o_s2'][m * 128:(m + 1) * 128, :], d_s2[m * 128:(m + 1) * 128, :],
                      reads=[B('d_s2')], writes=[B('o_s2')])
        p.fence()
        arB3.free()
        arB2.free()
        arBL.free()

    if 'C' in phases:
        arCL = AutoArena(nc)
        tX4B = arCL.sbuf_tensor("C_x4b", [128, DC * TL], BF16)
        tCos = arCL.sbuf_tensor("C_cos", [128, TL], F32)
        tSin = arCL.sbuf_tensor("C_sin", [128, TL], F32)
        tKM = arCL.sbuf_tensor("C_km", [128, NH], F32)
        tNegM = arCL.sbuf_tensor("C_negm", [128, NH], F32)
        tRowC = arCL.sbuf_tensor("C_rowc", [1, 64], F32)
        arCL.allocate()
        p.dma('sync', tCos[:, :], i_cos[:, :], writes=[B('C_cos')])
        p.dma('sync', tSin[:, :], i_sin[:, :], writes=[B('C_sin')])

        arCx = AutoArena(nc)
        tX2LN = arCx.sbuf_tensor("C_x2ln", [128, DC * TL], F32)
        arCx.allocate()

        arC1 = AutoArena(nc)
        tX2 = arC1.sbuf_tensor("C_x2", [128, DC * TL], F32)
        tSQc = [arC1.sbuf_tensor(f"C_sq{i}", [128, 512], F32) for i in range(2)]
        tRowc = arC1.sbuf_tensor("C_row", [1, 512 * 5], F32)
        tMBc = arC1.sbuf_tensor("C_mb", [128, 512], F32)
        tRBc = arC1.sbuf_tensor("C_rb", [128, 512], F32)
        tNrmc = arC1.sbuf_tensor("C_nrm", [128, 512], F32)
        arC1.allocate()
        p.dma('sync', tX2[:, :].rearrange("p (c t) -> p c t", c=DC),
              d_s2.rearrange("(c p) t -> p c t", p=128), reads=[B('d_s2')], writes=[B('C_x2')])
        emit_ln2(tSQc, tRowc, tMBc, tRBc, tNrmc,
                 lambda c: tX2[:, c * TL:(c + 1) * TL], TL,
                 lambda c: tX2LN[:, c * TL:(c + 1) * TL], B('C_x2ln'), B('C_x2'), 'Cln')
        p.fence()
        arC1.free()

        arC1b = AutoArena(nc)
        tCM = arC1b.sbuf_tensor("C_cm", [128, TL], F32)
        tCMT = arC1b.sbuf_tensor("C_cmt", [128, 8 * 128], F32)
        tRM = arC1b.sbuf_tensor("C_rm", [128, 16], F32)
        tSrow = arC1b.sbuf_tensor("C_srow", [1, 2 * TL], F32)
        tSB4 = arC1b.sbuf_tensor("C_sb4", [128, TL], F32)
        tRSB4 = arC1b.sbuf_tensor("C_rsb4", [128, TL], F32)
        tQT = arC1b.sbuf_tensor("C_qt", [128, TL], F32)
        tQI = arC1b.sbuf_tensor("C_qi", [128, TL], I32)
        arC1b.allocate()
        bCM = B('C_cm'); bCMT = B('C_cmt'); bRM = B('C_rm'); bSrow = B('C_srow')
        bSB4 = B('C_sb4'); bRSB4 = B('C_rsb4'); bQT = B('C_qt'); bQI = B('C_qi')
        bX2LN = B('C_x2ln')
        p.do('scalar', lambda e: e.activation(out=tCM[:, :], in_=tX2LN[:, 0:TL], func=ACT.Abs),
             reads=[bX2LN], writes=[bCM])
        for c in range(1, DC):
            p.do('scalar', lambda e, c=c: e.activation(out=tQT[:, :], in_=tX2LN[:, c * TL:(c + 1) * TL], func=ACT.Abs),
                 reads=[bX2LN], writes=[bQT])
            p.do('vector', lambda e, c=c: e.tensor_tensor(out=tCM[:, :], in0=tQT[:, :], in1=tCM[:, :], op=ALU.max),
                 reads=[bQT], writes=[bCM])
        for tt in range(8):
            p.do('tensor', lambda e, tt=tt: e.transpose(
                out=psF[:, 3072:3072 + 128], in_=tCM[:, tt * 128:(tt + 1) * 128], identity=t_idf[:, :]),
                reads=[bCM, B('idf')], writes=[pbank[6]])
            p.do('vector', lambda e, tt=tt: e.tensor_copy(out=tCMT[:, tt * 128:(tt + 1) * 128], in_=psF[:, 3072:3072 + 128]),
                 reads=[pbank[6]], writes=[bCMT])
        p.do('vector', lambda e: e.tensor_reduce(out=tRM[:, 0:8], in_=tCMT[:, :].rearrange("p (t k) -> p t k", t=8),
             axis=AX.X, op=ALU.max), reads=[bCMT], writes=[bRM])
        p.pad('v', 2)
        p.do('vector', lambda e: e.tensor_scalar(out=tRM[:, 8:16], in0=tRM[:, 0:8], scalar1=EPS_LN, scalar2=None, op0=ALU.max),
             reads=[bRM], writes=[bRM])
        p.pad('v', 2)
        p.do('vector', lambda e: e.reciprocal(out=tRM[:, 0:8], in_=tRM[:, 8:16]), reads=[bRM], writes=[bRM])
        p.pad('v', 2)
        p.do('vector', lambda e: e.tensor_scalar(out=tRM[:, 8:16], in0=tRM[:, 0:8], scalar1=7.0, scalar2=None, op0=ALU.mult),
             reads=[bRM], writes=[bRM])
        p.pad('v', 2)
        p.do('vector', lambda e: e.reciprocal(out=tRM[:, 0:8], in_=tRM[:, 8:16]), reads=[bRM], writes=[bRM])
        p.pad('v', 2)
        p.dma('gpsimd', d_rm[:, :], tRM[:, :], reads=[bRM], writes=[B('d_rm')])
        p.dma('gpsimd', tSrow[0:1, :].rearrange("o (x t q) -> o x t q", x=2, t=8),
              d_rm.rearrange("(o q) (x t) -> o x t q", o=1, x=2), reads=[B('d_rm')], writes=[bSrow], slow=True)
        for (dst, bdst, off) in ((tSB4, bSB4, TL), (tRSB4, bRSB4, 0)):
            for g in range(2):
                p.do('tensor', lambda e, off=off, g=g: e.matmul(
                    out=psF[:, g * 512:(g + 1) * 512], lhsT=t_ones[0:1, 0:128],
                    rhs=tSrow[0:1, off + g * 512: off + (g + 1) * 512], start=True, stop=True),
                    reads=[bSrow, B('ones')], writes=[pbank[g]])
                p.do('vector', lambda e, dst=dst, g=g: e.tensor_copy(out=dst[:, g * 512:(g + 1) * 512], in_=psF[:, g * 512:(g + 1) * 512]),
                     reads=[pbank[g]], writes=[bdst])
        bX4B = B('C_x4b')
        for c in range(DC):
            p.do('vector', lambda e, c=c: e.tensor_tensor(out=tQT[:, :], in0=tX2LN[:, c * TL:(c + 1) * TL], in1=tSB4[:, :], op=ALU.mult),
                 reads=[bX2LN, bSB4], writes=[bQT])
            p.do('vector', lambda e: e.tensor_copy(out=tQI[:, :], in_=tQT[:, :]), reads=[bQT], writes=[bQI])
            p.do('vector', lambda e: e.tensor_copy(out=tQT[:, :], in_=tQI[:, :]), reads=[bQI], writes=[bQT])
            p.do('vector', lambda e, c=c: e.tensor_tensor(out=tX4B[:, c * TL:(c + 1) * TL], in0=tQT[:, :], in1=tRSB4[:, :], op=ALU.mult),
                 reads=[bQT, bRSB4], writes=[bX4B])
        if 'Ctaps' in dbg:
            p.dma('gpsimd', dbg_outs['o_x2ln'][:, :], tX2LN[:, :], reads=[B('C_x2ln')], writes=[B('o_x2ln')])
            p.dma('gpsimd', dbg_outs['o_cm'][:, :], tCM[:, :], reads=[bCM], writes=[B('o_cm')])
            p.dma('gpsimd', dbg_outs['o_rm'][:, :], tRM[:, :], reads=[bRM], writes=[B('o_rm')])
            p.dma('gpsimd', dbg_outs['o_srow'][:, :], tSrow[:, :], reads=[bSrow], writes=[B('o_srow')])
            p.dma('gpsimd', dbg_outs['o_sb4'][:, :], tSB4[:, :], reads=[bSB4], writes=[B('o_sb4')])
            p.dma('gpsimd', dbg_outs['o_rsb4'][:, :], tRSB4[:, :], reads=[bRSB4], writes=[B('o_rsb4')])
        p.fence()
        arC1b.free()
        arCx.free()
        arCL2 = AutoArena(nc)
        tAO = arCL2.sbuf_tensor("C_ao", [128, 8 * D], BF16)
        arCL2.allocate()
        arC2a = AutoArena(nc)
        tQB = arC2a.sbuf_tensor("C_qb", [128, NH * TL], BF16)
        arC2a.allocate()
        arC2 = AutoArena(nc)
        tWfc = [arC2.sbuf_tensor(f"C_wf{i}", [128, D], F32) for i in range(2)]
        tWbc = [arC2.sbuf_tensor(f"C_wb{i}", [128, D], BF16) for i in range(2)]
        tKf = arC2.sbuf_tensor("C_kf", [128, TL], F32)
        tRot = arC2.sbuf_tensor("C_rot", [128, TL], F32)
        tT1 = arC2.sbuf_tensor("C_t1", [128, TL], F32)
        tT2 = arC2.sbuf_tensor("C_t2", [128, TL], F32)
        tKb = arC2.sbuf_tensor("C_kb", [128, TL], BF16)
        tVb = arC2.sbuf_tensor("C_vb", [128, TL], BF16)
        tVt = arC2.sbuf_tensor("C_vt", [128, 128], BF16)
        tAbsC = arC2.sbuf_tensor("C_absc", [128, D], F32)
        tWiC = arC2.sbuf_tensor("C_wic", [128, D], I32)
        tBR3 = arC2.sbuf_tensor("C_br3", [128, 8], F32)
        arC2.allocate()
        bQB = B('C_qb'); bKf = B('C_kf'); bRot = B('C_rot'); bT1 = B('C_t1'); bT2 = B('C_t2')
        bKb = B('C_kb'); bVb = B('C_vb'); bVt = B('C_vt'); bKM = B('C_km')
        bAbsC = B('C_absc'); bWiC = B('C_wic'); bBR3 = B('C_br3'); bRowCm = B('C_rowc')
        ISQ = 0.08838834764831845

        def emit_tern_scale(iw, wfl, bwfl, brcol):
            for m in range(DC):
                db = m % 2
                bWf = bwfl[db]
                p.dma('sync', wfl[db][:, :].rearrange("p (c k) -> p c k", c=DC),
                      iw.rearrange("(c p) mm -> p c mm", p=128)[:, :, m * 128:(m + 1) * 128], writes=[bWf])
                p.do('scalar', lambda e, db=db: e.activation(out=tAbsC[:, :], in_=wfl[db][:, :], func=ACT.Abs),
                     reads=[bWf], writes=[bAbsC])
                for g in range(4):
                    p.do('tensor', lambda e, m=m, g=g: e.matmul(
                        out=psF[0:1, 3072:3072 + 512], lhsT=t_ones[:, 0:1], rhs=tAbsC[:, g * 512:(g + 1) * 512],
                        start=(m == 0 and g == 0), stop=(m == DC - 1 and g == 3)),
                        reads=[bAbsC, B('ones')], writes=[pbank[6]])
            p.do('vector', lambda e: e.tensor_reduce(out=tRowC[0:1, 40:41], in_=psF[0:1, 3072:3072 + 512],
                 axis=AX.X, op=ALU.add), reads=[pbank[6]], writes=[bRowCm])
            p.pad('v', 2)
            p.do('vector', lambda e: e.tensor_scalar(out=tRowC[0:1, 41:42], in0=tRowC[0:1, 40:41],
                 scalar1=1.0 / (D * D), scalar2=None, op0=ALU.mult), reads=[bRowCm], writes=[bRowCm])
            p.pad('v', 2)
            p.do('vector', lambda e: e.tensor_scalar(out=tRowC[0:1, 42:43], in0=tRowC[0:1, 41:42],
                 scalar1=EPS_LN, scalar2=None, op0=ALU.add), reads=[bRowCm], writes=[bRowCm])
            p.pad('v', 2)
            p.do('vector', lambda e: e.reciprocal(out=tRowC[0:1, 43:44], in_=tRowC[0:1, 42:43]),
                 reads=[bRowCm], writes=[bRowCm])
            p.pad('v', 2)
            p.do('tensor', lambda e: e.matmul(out=psF[:, 3072:3072 + 1], lhsT=t_ones[0:1, 0:128],
                 rhs=tRowC[0:1, 41:42], start=True, stop=True), reads=[bRowCm, B('ones')], writes=[pbank[6]])
            p.do('tensor', lambda e: e.matmul(out=psF[:, 3073:3073 + 1], lhsT=t_ones[0:1, 0:128],
                 rhs=tRowC[0:1, 43:44], start=True, stop=True), reads=[bRowCm, B('ones')], writes=[pbank[6]])
            p.do('vector', lambda e, brcol=brcol: e.tensor_copy(out=tBR3[:, brcol:brcol + 2], in_=psF[:, 3072:3074]),
                 reads=[pbank[6]], writes=[bBR3])
            p.pad('v', 2)

        def rope_into(dst_ap, bdst, scale):
            p.do('vector', lambda e: e.tensor_scalar(out=tRot[0:64, :], in0=tKf[64:128, :], scalar1=-1.0, scalar2=None, op0=ALU.mult),
                 reads=[bKf], writes=[bRot])
            p.do('vector', lambda e: e.tensor_scalar(out=tRot[64:128, :], in0=tKf[0:64, :], scalar1=1.0, scalar2=None, op0=ALU.mult),
                 reads=[bKf], writes=[bRot])
            p.do('vector', lambda e: e.tensor_tensor(out=tT1[:, :], in0=tKf[:, :], in1=tCos[:, :], op=ALU.mult),
                 reads=[bKf, B('C_cos')], writes=[bT1])
            p.do('vector', lambda e: e.tensor_tensor(out=tT2[:, :], in0=tRot[:, :], in1=tSin[:, :], op=ALU.mult),
                 reads=[bRot, B('C_sin')], writes=[bT2])
            p.do('vector', lambda e, dst_ap=dst_ap: e.tensor_tensor(out=dst_ap, in0=tT1[:, :], in1=tT2[:, :], op=ALU.add),
                 reads=[bT1, bT2], writes=[bdst])

        emit_tern_scale(i_wq, tWfc, [B('C_wf0'), B('C_wf1')], 0)
        emit_tern_scale(i_wk, tWfc, [B('C_wf0'), B('C_wf1')], 2)
        emit_tern_scale(i_wv, tWfc, [B('C_wf0'), B('C_wf1')], 4)
        for pj, (proj, iw) in enumerate((('q', i_wq), ('k', i_wk), ('v', i_wv))):
            for m in range(NH):
                db = m % 2
                bWf = B(f'C_wf{db}'); bWb = B(f'C_wb{db}')
                p.dma('sync', tWfc[db][:, :].rearrange("p (c k) -> p c k", c=DC),
                      iw.rearrange("(c p) mm -> p c mm", p=128)[:, :, m * 128:(m + 1) * 128], writes=[bWf])
                p.do('vector', lambda e, db=db, brcol=2 * pj: e.tensor_scalar(out=tAbsC[:, :], in0=tWfc[db][:, :],
                     scalar1=tBR3[:, brcol + 1:brcol + 2], scalar2=None, op0=ALU.mult),
                     reads=[bWf], sreads=[bBR3], writes=[bAbsC])
                p.do('vector', lambda e: e.tensor_copy(out=tWiC[:, :], in_=tAbsC[:, :]), reads=[bAbsC], writes=[bWiC])
                p.do('vector', lambda e: e.tensor_copy(out=tAbsC[:, :], in_=tWiC[:, :]), reads=[bWiC], writes=[bAbsC])
                p.do('vector', lambda e: e.tensor_scalar(out=tAbsC[:, :], in0=tAbsC[:, :],
                     scalar1=1.0, scalar2=-1.0, op0=ALU.min, op1=ALU.max), writes=[bAbsC])
                p.do('vector', lambda e, db=db, brcol=2 * pj: e.tensor_scalar(out=tWbc[db][:, :], in0=tAbsC[:, :],
                     scalar1=tBR3[:, brcol:brcol + 1], scalar2=None, op0=ALU.mult),
                     reads=[bAbsC], sreads=[bBR3], writes=[bWb])
                for nn in range(2):
                    pb = pbank[4 + nn]
                    for c in range(DC):
                        p.do('tensor', lambda e, c=c, nn=nn, db=db: e.matmul(
                            out=psF[:, 2048 + nn * 512:2048 + (nn + 1) * 512],
                            lhsT=tWbc[db][:, c * 128:(c + 1) * 128],
                            rhs=tX4B[:, c * TL + nn * 512: c * TL + (nn + 1) * 512],
                            start=(c == 0), stop=(c == DC - 1)),
                            reads=[bWb, bX4B], writes=[pb])
                if proj == 'v':
                    for nn in range(2):
                        p.do('vector', lambda e, nn=nn: e.tensor_copy(
                            out=tVb[:, nn * 512:(nn + 1) * 512], in_=psF[:, 2048 + nn * 512:2048 + (nn + 1) * 512]),
                            reads=[pbank[4 + nn]], writes=[bVb])
                    for tt in range(8):
                        p.do('tensor', lambda e, tt=tt: e.transpose(
                            out=psB[:, 0:128], in_=tVb[:, tt * 128:(tt + 1) * 128], identity=t_idb[:, :]),
                            reads=[bVb, B('idb')], writes=[pbB])
                        p.do('vector', lambda e: e.tensor_copy(out=tVt[:, :], in_=psB[:, 0:128]),
                             reads=[pbB], writes=[bVt])
                        p.dma('sync', d_vin[m * TL + tt * 128: m * TL + (tt + 1) * 128, :], tVt[:, :],
                              reads=[bVt], writes=[B('d_vin')])
                else:
                    sc = ISQ if proj == 'k' else 1.0
                    for nn in range(2):
                        p.do('scalar', lambda e, nn=nn, sc=sc: e.activation(
                            out=tKf[:, nn * 512:(nn + 1) * 512], in_=psF[:, 2048 + nn * 512:2048 + (nn + 1) * 512],
                            func=ACT.Copy, scale=sc),
                            reads=[pbank[4 + nn]], writes=[bKf])
                    if proj == 'q':
                        rope_into(tQB[:, m * TL:(m + 1) * TL], bQB, 1.0)
                    else:
                        rope_into(tKb[:, :], bKb, 1.0)
                        p.dma('sync', d_kin[m * 128:(m + 1) * 128, :], tKb[:, :], reads=[bKb], writes=[B('d_kin')])
                        p.do('vector', lambda e, m=m: e.tensor_reduce(
                            out=tKM[:, m:m + 1], in_=tKb[:, :], axis=AX.X, op=ALU.max, apply_absolute_value=True),
                            reads=[bKb], writes=[bKM])
        p.dma('gpsimd', d_kmi.rearrange("(c p) o -> p c o", p=128), tKM[:, :].rearrange("p (c o) -> p c o", o=1),
              reads=[bKM], writes=[B('d_kmi')], slow=True)
        p.fence()

        p.do('gpsimd', lambda e: e.collective_compute(
            "AllGather", ALU.bypass, replica_groups=[list(range(NCORE))],
            ins=[d_kin.tensor.ap().opt() if hasattr(d_kin, 'tensor') else d_kin.opt()],
            outs=[d_kag.opt()]), reads=[B('d_kin')], writes=[B('d_kag')])
        p.do('gpsimd', lambda e: e.collective_compute(
            "AllGather", ALU.bypass, replica_groups=[list(range(NCORE))],
            ins=[d_vin.opt()], outs=[d_vag.opt()]), reads=[B('d_vin')], writes=[B('d_vag')])
        p.do('gpsimd', lambda e: e.collective_compute(
            "AllReduce", ALU.max, replica_groups=[list(range(NCORE))],
            ins=[d_kmi.opt()], outs=[d_kmo.opt()]), reads=[B('d_kmi')], writes=[B('d_kmo')])

        p.dma('sync', tKM[:, :].rearrange("p (c o) -> p c o", o=1), d_kmo.rearrange("(c p) o -> p c o", p=128),
              reads=[B('d_kmo')], writes=[bKM], slow=True)
        bNegM = B('C_negm'); bRowC = B('C_rowc')
        for m in range(NH):
            p.do('scalar', lambda e, m=m: e.activation(out=tKb[:, :], in_=tQB[:, m * TL:(m + 1) * TL],
                 func=ACT.Abs), reads=[bQB], writes=[bKb])
            p.do('vector', lambda e, m=m: e.tensor_copy(out=tVb[:, 0:1], in_=tKM[:, m:m + 1]),
                 reads=[bKM], writes=[bVb])
            p.do('tensor', lambda e, m=m: e.matmul(
                out=psF[0:1, 3072:3072 + 512], lhsT=tVb[:, 0:1], rhs=tKb[:, 0:512], start=True, stop=False),
                reads=[bVb, bKb], writes=[pbank[6]])
            p.do('tensor', lambda e, m=m: e.matmul(
                out=psF[0:1, 3072:3072 + 512], lhsT=tVb[:, 0:1], rhs=tKb[:, 512:1024], start=False, stop=True),
                reads=[bVb, bKb], writes=[pbank[6]])
            p.do('vector', lambda e, m=m: e.tensor_reduce(out=tRowC[0:1, m:m + 1], in_=psF[0:1, 3072:3072 + 512],
                 axis=AX.X, op=ALU.max), reads=[pbank[6]], writes=[bRowC])
        p.pad('v', 2)
        p.do('vector', lambda e: e.tensor_scalar(out=tRowC[0:1, 16:32], in0=tRowC[0:1, 0:16],
             scalar1=-1.0, scalar2=None, op0=ALU.mult), reads=[bRowC], writes=[bRowC])
        p.pad('v', 2)
        p.do('tensor', lambda e: e.matmul(out=psF[:, 3072:3072 + 16], lhsT=t_ones[0:1, 0:128],
             rhs=tRowC[0:1, 16:32], start=True, stop=True), reads=[bRowC, B('ones')], writes=[pbank[6]])
        p.do('vector', lambda e: e.tensor_copy(out=tNegM[:, :], in_=psF[:, 3072:3072 + 16]),
             reads=[pbank[6]], writes=[bNegM])
        p.fence()
        p.fence()
        arC2.free()
        arC3 = AutoArena(nc)
        tKH = [arC3.sbuf_tensor(f"C_kh{i}", [128, T], BF16) for i in range(2)]
        tVH = [arC3.sbuf_tensor(f"C_vh{i}", [128, 64 * 129], BF16) for i in range(2)]
        tEX = [arC3.sbuf_tensor(f"C_ex{i}", [128, 512], BF16) for i in range(2)]
        tRZ = arC3.sbuf_tensor("C_rz", [128, 8], F32)
        arC3.allocate()
        for i in range(2):
            p.do('gpsimd', lambda e, i=i: e.memset(
                tVH[i][:, :].rearrange("p (kc s) -> p kc s", s=129)[:, :, 128:129], 1.0),
                writes=[B(f'C_vh{i}')])
        bAO = B('C_ao'); bRZ = B('C_rz')
        for h in range(NH):
            hb = h % 2
            bKH = B(f'C_kh{hb}'); bVH = B(f'C_vh{hb}')
            for b in range(NCORE):
                p.dma('sync', tKH[hb][:, b * TL:(b + 1) * TL],
                      d_kag[(b * D + h * 128):(b * D + (h + 1) * 128), :],
                      reads=[B('d_kag')], writes=[bKH])
                p.dma('gpsimd', tVH[hb][:, :].rearrange("p (kc s) -> p kc s", s=129)[:, b * 8:(b + 1) * 8, 0:128],
                      d_vag[(b * NH + h) * TL: (b * NH + h + 1) * TL, :].rearrange("(kc p) hd -> p kc hd", p=128),
                      reads=[B('d_vag')], writes=[bVH])
            for qsg in range(2):
                pend = None
                for kc in range(64):
                    db = kc % 2
                    psc = pbank[4 + db]
                    p.do('tensor', lambda e, h=h, qsg=qsg, kc=kc, db=db, hb=hb: e.matmul(
                        out=psF[:, 2048 + db * 512:2048 + (db + 1) * 512],
                        lhsT=tKH[hb][:, kc * 128:(kc + 1) * 128],
                        rhs=tQB[:, h * TL + qsg * 512: h * TL + (qsg + 1) * 512],
                        start=True, stop=True),
                        reads=[bKH, bQB], writes=[psc])
                    p.do('scalar', lambda e, h=h, db=db: e.activation(
                        out=tEX[db][:, :], in_=psF[:, 2048 + db * 512:2048 + (db + 1) * 512],
                        func=ACT.Exp, bias=tNegM[:, h:h + 1], scale=1.0),
                        reads=[psc], sreads=[bNegM], writes=[B(f'C_ex{db}')])
                    if pend is not None:
                        pkc, pdb = pend
                        for qt in range(4):
                            p.do('tensor', lambda e, qt=qt, pkc=pkc, pdb=pdb, hb=hb: e.matmul(
                                out=psF[:, qt * 512:qt * 512 + 129],
                                lhsT=tEX[pdb][:, qt * 128:(qt + 1) * 128],
                                rhs=tVH[hb][:, pkc * 129:(pkc + 1) * 129],
                                start=(pkc == 0), stop=(pkc == 63)),
                                reads=[B(f'C_ex{pdb}'), bVH], writes=[pbank[qt]])
                    pend = (kc, db)
                pkc, pdb = pend
                for qt in range(4):
                    p.do('tensor', lambda e, qt=qt, pkc=pkc, pdb=pdb, hb=hb: e.matmul(
                        out=psF[:, qt * 512:qt * 512 + 129],
                        lhsT=tEX[pdb][:, qt * 128:(qt + 1) * 128],
                        rhs=tVH[hb][:, pkc * 129:(pkc + 1) * 129],
                        start=(pkc == 0), stop=(pkc == 63)),
                        reads=[B(f'C_ex{pdb}'), bVH], writes=[pbank[qt]])
                for qt in range(4):
                    p.do('vector', lambda e, qt=qt: e.reciprocal(out=tRZ[:, qt:qt + 1], in_=psF[:, qt * 512 + 128:qt * 512 + 129]),
                         reads=[pbank[qt]], writes=[bRZ])
                p.pad('v', 4)
                for qt in range(4):
                    tt = qsg * 4 + qt
                    p.do('scalar', lambda e, h=h, tt=tt, qt=qt: e.activation(
                        out=tAO[:, tt * D + h * 128: tt * D + (h + 1) * 128],
                        in_=psF[:, qt * 512:qt * 512 + 128], func=ACT.Copy, scale=tRZ[:, qt:qt + 1]),
                        reads=[pbank[qt]], sreads=[bRZ], writes=[bAO])
        p.fence()
        arC3.free()
        arC2a.free()

        arCL3 = AutoArena(nc)
        tX8F = arCL3.sbuf_tensor("C_x8f", [128, DC * TL], BF16)
        arCL3.allocate()
        arC4 = AutoArena(nc)
        tAOa = arC4.sbuf_tensor("C_aoa", [128, 8 * D], BF16)
        tBS = arC4.sbuf_tensor("C_bs", [128, 64], F32)
        tJunk = arC4.sbuf_tensor("C_junk", [128, D], BF16)
        tQTo = arC4.sbuf_tensor("C_qto", [128, D], F32)
        tQIo = arC4.sbuf_tensor("C_qio", [128, D], I32)
        tMask = arC4.sbuf_tensor("C_mask", [128, D], F32)
        arC4.allocate()
        bAOa = B('C_aoa'); bBS = B('C_bs'); bJunk = B('C_junk')
        bQTo = B('C_qto'); bQIo = B('C_qio'); bMask = B('C_mask'); bX8F = B('C_x8f')
        p.do('scalar', lambda e: e.activation(out=tAOa[:, :], in_=tAO[:, :], func=ACT.Abs),
             reads=[bAO], writes=[bAOa])
        p.do('vector', lambda e: e.tensor_reduce(out=tBS[:, 56:64], in_=tAOa[:, :].rearrange("p (t k) -> p t k", t=8),
             axis=AX.X, op=ALU.max), reads=[bAOa], writes=[bBS])
        p.do('gpsimd', lambda e: e.memset(tBS[:, 0:8], 0.0), writes=[bBS])
        p.pad('v', 2)
        p.do('vector', lambda e: e.tensor_scalar(out=tBS[:, 8:16], in0=tBS[:, 56:64], scalar1=1.0000001, scalar2=None, op0=ALU.mult),
             reads=[bBS], writes=[bBS])
        p.do('vector', lambda e: e.tensor_scalar(out=tBS[:, 40:48], in0=tBS[:, 56:64], scalar1=EPS_LN, scalar2=None, op0=ALU.max),
             reads=[bBS], writes=[bBS])
        p.pad('v', 2)
        p.do('vector', lambda e: e.reciprocal(out=tBS[:, 48:56], in_=tBS[:, 40:48]), reads=[bBS], writes=[bBS])
        p.pad('v', 2)
        p.do('vector', lambda e: e.tensor_scalar(out=tBS[:, 40:48], in0=tBS[:, 48:56], scalar1=127.0, scalar2=None, op0=ALU.mult),
             reads=[bBS], writes=[bBS])
        p.pad('v', 2)
        p.do('vector', lambda e: e.reciprocal(out=tBS[:, 48:56], in_=tBS[:, 40:48]), reads=[bBS], writes=[bBS])
        NBIS = 26
        KTGT = 1126.0
        for it in range(NBIS):
            p.do('vector', lambda e: e.tensor_tensor(out=tBS[:, 16:24], in0=tBS[:, 0:8], in1=tBS[:, 8:16], op=ALU.add),
                 reads=[bBS], writes=[bBS])
            p.pad('v', 2)
            p.do('vector', lambda e: e.tensor_scalar(out=tBS[:, 16:24], in0=tBS[:, 16:24], scalar1=0.5, scalar2=None, op0=ALU.mult),
                 reads=[bBS], writes=[bBS])
            p.pad('v', 6)
            for tt in range(8):
                p.do('vector', lambda e, tt=tt: e.tensor_scalar(
                    out=tJunk[:, :], in0=tAOa[:, tt * D:(tt + 1) * D],
                    scalar1=tBS[:, 16 + tt:17 + tt], scalar2=None, op0=ALU.is_ge, op1=ALU.add,
                    accum_out=tBS[:, 24 + tt:25 + tt]),
                    reads=[bAOa], sreads=[bBS], writes=[bJunk, bBS])
            p.pad('v', 2)
            p.do('vector', lambda e: e.tensor_scalar(out=tBS[:, 32:40], in0=tBS[:, 24:32], scalar1=KTGT, scalar2=None, op0=ALU.is_ge),
                 reads=[bBS], writes=[bBS])
            p.pad('v', 2)
            p.do('vector', lambda e: e.tensor_tensor(out=tBS[:, 24:32], in0=tBS[:, 16:24], in1=tBS[:, 0:8], op=ALU.subtract),
                 reads=[bBS], writes=[bBS])
            p.pad('v', 2)
            p.do('vector', lambda e: e.tensor_tensor(out=tBS[:, 24:32], in0=tBS[:, 24:32], in1=tBS[:, 32:40], op=ALU.mult),
                 reads=[bBS], writes=[bBS])
            p.pad('v', 2)
            p.do('vector', lambda e: e.tensor_tensor(out=tBS[:, 0:8], in0=tBS[:, 0:8], in1=tBS[:, 24:32], op=ALU.add),
                 reads=[bBS], writes=[bBS])
            p.pad('v', 2)
            p.do('vector', lambda e: e.tensor_tensor(out=tBS[:, 24:32], in0=tBS[:, 8:16], in1=tBS[:, 16:24], op=ALU.subtract),
                 reads=[bBS], writes=[bBS])
            p.pad('v', 2)
            p.do('vector', lambda e: e.tensor_tensor(out=tBS[:, 24:32], in0=tBS[:, 24:32], in1=tBS[:, 32:40], op=ALU.mult),
                 reads=[bBS], writes=[bBS])
            p.pad('v', 2)
            p.do('vector', lambda e: e.tensor_tensor(out=tBS[:, 8:16], in0=tBS[:, 16:24], in1=tBS[:, 24:32], op=ALU.add),
                 reads=[bBS], writes=[bBS])
            p.pad('v', 2)
        p.pad('v', 6)
        for tt in range(8):
            p.do('vector', lambda e, tt=tt: e.tensor_scalar(out=tMask[:, :], in0=tAOa[:, tt * D:(tt + 1) * D],
                 scalar1=tBS[:, 0 + tt:1 + tt], scalar2=None, op0=ALU.is_ge), reads=[bAOa], sreads=[bBS], writes=[bMask])
            p.do('vector', lambda e, tt=tt: e.tensor_scalar(out=tQTo[:, :], in0=tAO[:, tt * D:(tt + 1) * D],
                 scalar1=tBS[:, 40 + tt:41 + tt], scalar2=None, op0=ALU.mult), reads=[bAO], sreads=[bBS], writes=[bQTo])
            p.do('vector', lambda e: e.tensor_copy(out=tQIo[:, :], in_=tQTo[:, :]), reads=[bQTo], writes=[bQIo])
            p.do('vector', lambda e: e.tensor_copy(out=tQTo[:, :], in_=tQIo[:, :]), reads=[bQIo], writes=[bQTo])
            p.do('vector', lambda e, tt=tt: e.tensor_scalar(out=tQTo[:, :], in0=tQTo[:, :],
                 scalar1=tBS[:, 48 + tt:49 + tt], scalar2=None, op0=ALU.mult), sreads=[bBS], writes=[bQTo])
            p.do('vector', lambda e: e.tensor_tensor(out=tJunk[:, :], in0=tQTo[:, :], in1=tMask[:, :], op=ALU.mult),
                 reads=[bQTo, bMask], writes=[bJunk])
            for c in range(DC):
                p.do('tensor', lambda e, c=c: e.transpose(
                    out=psB[:, 0:128], in_=tJunk[:, c * 128:(c + 1) * 128], identity=t_idb[:, :]),
                    reads=[bJunk, B('idb')], writes=[pbB])
                p.do('vector', lambda e, c=c, tt=tt: e.tensor_copy(
                    out=tX8F[:, c * TL + tt * 128:(c * TL + (tt + 1) * 128)], in_=psB[:, 0:128]),
                    reads=[pbB], writes=[bX8F])
        if 'Ctaps' in dbg:
            p.dma('gpsimd', dbg_outs['o_x4b'][:, :], tX4B[:, :], reads=[bX4B], writes=[B('o_x4b')])
            p.dma('gpsimd', dbg_outs['o_x8f'][:, :], tX8F[:, :], reads=[bX8F], writes=[B('o_x8f')])
        p.fence()
        arC4.free()

        arC5 = AutoArena(nc)
        tWoF = [arC5.sbuf_tensor(f"C_wof{i}", [128, D], F32) for i in range(2)]
        tWoB = [arC5.sbuf_tensor(f"C_wob{i}", [128, D], BF16) for i in range(2)]
        tWoI = arC5.sbuf_tensor("C_woi", [128, D], I32)
        tAbs = arC5.sbuf_tensor("C_abs", [128, D], F32)
        tBR = arC5.sbuf_tensor("C_br", [128, 2], F32)
        tResC = [arC5.sbuf_tensor(f"C_res{i}", [128, TL], F32) for i in range(2)]
        tScC = [arC5.sbuf_tensor(f"C_sc{i}", [128, TL], F32) for i in range(2)]
        arC5.allocate()
        bAbs = B('C_abs'); bBR = B('C_br'); bRowC2 = B('C_rowc')
        for m in range(DC):
            db = m % 2
            bWf = B(f'C_wof{db}')
            p.dma('sync', tWoF[db][:, :].rearrange("p (c k) -> p c k", c=DC),
                  i_wo.rearrange("(c p) mm -> p c mm", p=128)[:, :, m * 128:(m + 1) * 128], writes=[bWf])
            p.do('scalar', lambda e, db=db: e.activation(out=tAbs[:, :], in_=tWoF[db][:, :], func=ACT.Abs),
                 reads=[bWf], writes=[bAbs])
            for g in range(4):
                p.do('tensor', lambda e, m=m, g=g: e.matmul(
                    out=psF[0:1, 3072 + 0:3072 + 512], lhsT=t_ones[:, 0:1], rhs=tAbs[:, g * 512:(g + 1) * 512],
                    start=(m == 0 and g == 0), stop=(m == DC - 1 and g == 3)),
                    reads=[bAbs, B('ones')], writes=[pbank[6]])
        p.do('vector', lambda e: e.tensor_reduce(out=tRowC[0:1, 32:33], in_=psF[0:1, 3072:3072 + 512],
             axis=AX.X, op=ALU.add), reads=[pbank[6]], writes=[bRowC2])
        p.pad('v', 2)
        p.do('vector', lambda e: e.tensor_scalar(out=tRowC[0:1, 33:34], in0=tRowC[0:1, 32:33],
             scalar1=1.0 / (D * D), scalar2=None, op0=ALU.mult), reads=[bRowC2], writes=[bRowC2])
        p.pad('v', 2)
        p.do('vector', lambda e: e.tensor_scalar(out=tRowC[0:1, 34:35], in0=tRowC[0:1, 33:34],
             scalar1=EPS_LN, scalar2=None, op0=ALU.add), reads=[bRowC2], writes=[bRowC2])
        p.pad('v', 2)
        p.do('vector', lambda e: e.reciprocal(out=tRowC[0:1, 35:36], in_=tRowC[0:1, 34:35]),
             reads=[bRowC2], writes=[bRowC2])
        p.pad('v', 2)
        p.do('tensor', lambda e: e.matmul(out=psF[:, 3072:3072 + 1], lhsT=t_ones[0:1, 0:128],
             rhs=tRowC[0:1, 33:34], start=True, stop=True), reads=[bRowC2, B('ones')], writes=[pbank[6]])
        p.do('tensor', lambda e: e.matmul(out=psF[:, 3073:3073 + 1], lhsT=t_ones[0:1, 0:128],
             rhs=tRowC[0:1, 35:36], start=True, stop=True), reads=[bRowC2, B('ones')], writes=[pbank[6]])
        p.do('vector', lambda e: e.tensor_copy(out=tBR[:, :], in_=psF[:, 3072:3072 + 2]),
             reads=[pbank[6]], writes=[bBR])
        p.pad('v', 2)
        for m in range(DC):
            db = m % 2
            bWf = B(f'C_wof{db}'); bWb = B(f'C_wob{db}')
            p.dma('sync', tWoF[db][:, :].rearrange("p (c k) -> p c k", c=DC),
                  i_wo.rearrange("(c p) mm -> p c mm", p=128)[:, :, m * 128:(m + 1) * 128], writes=[bWf])
            p.do('vector', lambda e, db=db: e.tensor_scalar(out=tAbs[:, :], in0=tWoF[db][:, :],
                 scalar1=tBR[:, 1:2], scalar2=None, op0=ALU.mult), reads=[bWf], sreads=[bBR], writes=[bAbs])
            p.do('vector', lambda e: e.tensor_copy(out=tWoI[:, :], in_=tAbs[:, :]), reads=[bAbs], writes=[B('C_woi')])
            p.do('vector', lambda e: e.tensor_copy(out=tAbs[:, :], in_=tWoI[:, :]), reads=[B('C_woi')], writes=[bAbs])
            p.do('vector', lambda e: e.tensor_scalar(out=tAbs[:, :], in0=tAbs[:, :],
                 scalar1=1.0, scalar2=-1.0, op0=ALU.min, op1=ALU.max), writes=[bAbs])
            p.do('vector', lambda e, db=db: e.tensor_scalar(out=tWoB[db][:, :], in0=tAbs[:, :],
                 scalar1=tBR[:, 0:1], scalar2=None, op0=ALU.mult), reads=[bAbs], sreads=[bBR], writes=[bWb])
            bResC = B(f'C_res{db}'); bScC = B(f'C_sc{db}')
            p.dma('gpsimd', tResC[db][:, :], d_s2[m * 128:(m + 1) * 128, :], reads=[B('d_s2')], writes=[bResC])
            for nn in range(2):
                pb = pbank[2 + nn]
                for c in range(DC):
                    p.do('tensor', lambda e, c=c, nn=nn, db=db: e.matmul(
                        out=psF[:, 1024 + nn * 512:1024 + (nn + 1) * 512],
                        lhsT=tWoB[db][:, c * 128:(c + 1) * 128],
                        rhs=tX8F[:, c * TL + nn * 512: c * TL + (nn + 1) * 512],
                        start=(c == 0), stop=(c == DC - 1)),
                        reads=[bWb, bX8F], writes=[pb])
                p.do('vector', lambda e, nn=nn, db=db: e.tensor_tensor(
                    out=tScC[db][:, nn * 512:(nn + 1) * 512], in0=psF[:, 1024 + nn * 512:1024 + (nn + 1) * 512],
                    in1=tResC[db][:, nn * 512:(nn + 1) * 512], op=ALU.add),
                    reads=[pb, bResC], writes=[bScC])
            p.dma('sync', d_s3[m * 128:(m + 1) * 128, :], tScC[db][:, :], reads=[bScC], writes=[B('d_s3')])
        if 'C' in dbg:
            for m in range(DC):
                p.dma('gpsimd', dbg_outs['o_s3'][m * 128:(m + 1) * 128, :], d_s3[m * 128:(m + 1) * 128, :],
                      reads=[B('d_s3')], writes=[B('o_s3')])
        p.fence()
        arC5.free()
        arCL3.free()
        arCL2.free()
        arCL.free()

    if 'D' in phases:
        IC = II // 512
        arD0 = AutoArena(nc)
        tNW = arD0.sbuf_tensor("D_nw", [128, II], F32)
        tBM = arD0.sbuf_tensor("D_bm", [128, 512], F32)
        tS2 = arD0.sbuf_tensor("D_s2", [128, 512], F32)
        tR2 = arD0.sbuf_tensor("D_r2", [128, 512], F32)
        tPK = arD0.sbuf_tensor("D_pk", [128, 1024], F32)
        arD0.allocate()
        bNW = B('D_nw'); bBM = B('D_bm'); bS2 = B('D_s2'); bR2 = B('D_r2'); bPK = B('D_pk')

        def mx_scales(nblk):
            p.pad('v', 2)
            p.do('vector', lambda e, nblk=nblk: e.tensor_scalar(out=tBM[:, 0:nblk], in0=tBM[:, 0:nblk],
                 scalar1=1.0 / 448.0, scalar2=None, op0=ALU.mult), reads=[bBM], writes=[bBM])
            p.pad('v', 2)
            p.do('vector', lambda e, nblk=nblk: e.tensor_scalar(out=tBM[:, 0:nblk], in0=tBM[:, 0:nblk],
                 scalar1=1e-30, scalar2=None, op0=ALU.max), reads=[bBM], writes=[bBM])
            p.pad('v', 2)
            p.do('vector', lambda e, nblk=nblk: e.tensor_scalar(out=tS2[:, 0:nblk], in0=tBM[:, 0:nblk],
                 scalar1=2.0, scalar2=None, op0=ALU.mult), reads=[bBM], writes=[bS2])
            p.do('vector', lambda e, nblk=nblk: e.reciprocal(out=tR2[:, 0:nblk], in_=tBM[:, 0:nblk]),
                 reads=[bBM], writes=[bR2])
            p.pad('v', 2)
            p.do('vector', lambda e, nblk=nblk: e.tensor_scalar(out=tR2[:, 0:nblk], in0=tR2[:, 0:nblk],
                 scalar1=0.5, scalar2=None, op0=ALU.mult), reads=[bR2], writes=[bR2])
            p.pad('v', 2)

        for (inat, dsc, tag) in ((i_gwn, d_gsci, 'g'), (i_uwn, d_usci, 'u')):
            for ch in range(8):
                p.dma('sync', tNW[:, 0:D], inat[ch * 128:(ch + 1) * 128, :], writes=[bNW])
                p.do('vector', lambda e: e.tensor_reduce(out=tBM[:, 0:64],
                     in_=tNW[:, 0:D].rearrange("p (g b) -> p g b", b=32), axis=AX.X, op=ALU.max,
                     apply_absolute_value=True), reads=[bNW], writes=[bBM])
                mx_scales(64)
                p.do('vector', lambda e: e.tensor_copy(out=tPK[:, 0:64], in_=tS2[:, 0:64]), reads=[bS2], writes=[bPK])
                p.do('vector', lambda e: e.tensor_copy(out=tPK[:, 64:128], in_=tR2[:, 0:64]), reads=[bR2], writes=[bPK])
                p.dma('sync', dsc[ch * 128:(ch + 1) * 128, :], tPK[:, 0:128], reads=[bPK], writes=[B('d_' + tag + 'sci')])
        for ch in range(2):
            p.dma('sync', tNW[:, 0:II], i_dwn[ch * 128:(ch + 1) * 128, :], writes=[bNW])
            p.do('vector', lambda e: e.tensor_reduce(out=tBM[:, 0:256],
                 in_=tNW[:, 0:II].rearrange("p (g b) -> p g b", b=32), axis=AX.X, op=ALU.max,
                 apply_absolute_value=True), reads=[bNW], writes=[bBM])
            mx_scales(256)
            p.do('vector', lambda e: e.tensor_copy(out=tPK[:, 0:256], in_=tS2[:, 0:256]), reads=[bS2], writes=[bPK])
            p.do('vector', lambda e: e.tensor_copy(out=tPK[:, 256:512], in_=tR2[:, 0:256]), reads=[bR2], writes=[bPK])
            p.dma('sync', d_dsci[ch * 128:(ch + 1) * 128, :], tPK[:, 0:512], reads=[bPK], writes=[B('d_dsci')])
        p.fence()
        p.do('gpsimd', lambda e: e.collective_compute(
            "AllGather", ALU.bypass, replica_groups=[list(range(NCORE))],
            ins=[d_gsci.opt()], outs=[d_gsco.opt()]), reads=[B('d_gsci')], writes=[B('d_gsco')])
        p.do('gpsimd', lambda e: e.collective_compute(
            "AllGather", ALU.bypass, replica_groups=[list(range(NCORE))],
            ins=[d_usci.opt()], outs=[d_usco.opt()]), reads=[B('d_usci')], writes=[B('d_usco')])
        p.do('gpsimd', lambda e: e.collective_compute(
            "AllGather", ALU.bypass, replica_groups=[list(range(NCORE))],
            ins=[d_dsci.opt()], outs=[d_dsco.opt()]), reads=[B('d_dsci')], writes=[B('d_dsco')])
        p.fence()
        arD0.free()

        arDL = AutoArena(nc)
        tX3F = arDL.sbuf_tensor("D_x3f", [128, DC * TL], BF16)
        tG2P = arDL.sbuf_tensor("D_g2p", [4, 128], F32)
        arDL.allocate()
        p.dma('sync', tG2P[:, :], i_g2p[:, :], writes=[B('D_g2p')])
        arD1 = AutoArena(nc)
        tX3 = arD1.sbuf_tensor("D_x3", [128, DC * TL], F32)
        tSQd = [arD1.sbuf_tensor(f"D_sq{i}", [128, 512], F32) for i in range(2)]
        tRowd = arD1.sbuf_tensor("D_row", [1, 512 * 5], F32)
        tMBd = arD1.sbuf_tensor("D_mb", [128, 512], F32)
        tRBd = arD1.sbuf_tensor("D_rb", [128, 512], F32)
        tNrmd = arD1.sbuf_tensor("D_nrm", [128, 512], F32)
        arD1.allocate()
        p.dma('sync', tX3[:, :].rearrange("p (c t) -> p c t", c=DC),
              d_s3.rearrange("(c p) t -> p c t", p=128), reads=[B('d_s3')], writes=[B('D_x3')])
        emit_ln2(tSQd, tRowd, tMBd, tRBd, tNrmd,
                 lambda c: tX3[:, c * TL:(c + 1) * TL], TL,
                 lambda c: tX3[:, c * TL:(c + 1) * TL], B('D_x3'), B('D_x3'), 'Dln')
        p.fence()
        arD2 = AutoArena(nc)
        tXTt = arD2.sbuf_tensor("D_xtt", [128, D], F32)
        tBMx = arD2.sbuf_tensor("D_bmx", [128, 64], F32)
        tS2x = arD2.sbuf_tensor("D_s2x", [128, 64], F32)
        tR2x = arD2.sbuf_tensor("D_r2x", [128, 64], F32)
        tQ8x = arD2.sbuf_tensor("D_q8x", [128, D], FP8)
        tQBx = arD2.sbuf_tensor("D_qbx", [128, D], BF16)
        arD2.allocate()
        bXTt = B('D_xtt'); bBMx = B('D_bmx'); bS2x = B('D_s2x'); bR2x = B('D_r2x')
        bQ8x = B('D_q8x'); bQBx = B('D_qbx'); bX3F = B('D_x3f')
        for tt in range(8):
            for c in range(DC):
                p.do('tensor', lambda e, c=c, tt=tt: e.transpose(
                    out=psF[:, 3072:3072 + 128], in_=tX3[:, c * TL + tt * 128: c * TL + (tt + 1) * 128], identity=t_idf[:, :]),
                    reads=[B('D_x3'), B('idf')], writes=[pbank[6]])
                p.do('vector', lambda e, c=c: e.tensor_copy(
                    out=tXTt[:, c * 128:(c + 1) * 128], in_=psF[:, 3072:3072 + 128]),
                    reads=[pbank[6]], writes=[bXTt])
            p.do('vector', lambda e: e.tensor_reduce(out=tBMx[:, 0:64],
                 in_=tXTt[:, :].rearrange("p (g b) -> p g b", b=32), axis=AX.X, op=ALU.max,
                 apply_absolute_value=True), reads=[bXTt], writes=[bBMx])
            p.pad('v', 2)
            p.do('vector', lambda e: e.tensor_scalar(out=tBMx[:, :], in0=tBMx[:, :], scalar1=1.0 / 448.0, scalar2=None, op0=ALU.mult),
                 reads=[bBMx], writes=[bBMx])
            p.pad('v', 2)
            p.do('vector', lambda e: e.tensor_scalar(out=tBMx[:, :], in0=tBMx[:, :], scalar1=1e-30, scalar2=None, op0=ALU.max),
                 reads=[bBMx], writes=[bBMx])
            p.pad('v', 2)
            p.do('vector', lambda e: e.tensor_scalar(out=tS2x[:, :], in0=tBMx[:, :], scalar1=2.0, scalar2=None, op0=ALU.mult),
                 reads=[bBMx], writes=[bS2x])
            p.do('vector', lambda e: e.reciprocal(out=tR2x[:, :], in_=tBMx[:, :]), reads=[bBMx], writes=[bR2x])
            p.pad('v', 2)
            p.do('vector', lambda e: e.tensor_scalar(out=tR2x[:, :], in0=tR2x[:, :], scalar1=0.5, scalar2=None, op0=ALU.mult),
                 reads=[bR2x], writes=[bR2x])
            p.pad('v', 2)
            p.do('vector', lambda e: e.tensor_tensor(
                out=tQ8x[:, :].rearrange("p (g b) -> p g b", b=32),
                in0=tXTt[:, :].rearrange("p (g b) -> p g b", b=32),
                in1=tR2x[:, :].to_broadcast((128, 64, 32)), op=ALU.mult),
                reads=[bXTt, bR2x], writes=[bQ8x])
            p.do('vector', lambda e: e.tensor_tensor(
                out=tQBx[:, :].rearrange("p (g b) -> p g b", b=32),
                in0=tQ8x[:, :].rearrange("p (g b) -> p g b", b=32),
                in1=tS2x[:, :].to_broadcast((128, 64, 32)), op=ALU.mult),
                reads=[bQ8x, bS2x], writes=[bQBx])
            for c in range(DC):
                p.do('tensor', lambda e, c=c, tt=tt: e.transpose(
                    out=psB[:, 0:128], in_=tQBx[:, c * 128:(c + 1) * 128], identity=t_idb[:, :]),
                    reads=[bQBx, B('idb')], writes=[pbB])
                p.do('vector', lambda e, c=c, tt=tt: e.tensor_copy(
                    out=tX3F[:, c * TL + tt * 128: c * TL + (tt + 1) * 128], in_=psB[:, 0:128]),
                    reads=[pbB], writes=[bX3F])
        p.fence()
        arD2.free()
        arD1.free()

        arD3 = AutoArena(nc)
        tGW2 = [arD3.sbuf_tensor(f"D_gw{i}", [128, 512], F32) for i in range(2)]
        tUW2 = [arD3.sbuf_tensor(f"D_uw{i}", [128, 512], F32) for i in range(2)]
        tGQ = [arD3.sbuf_tensor(f"D_gq{i}", [128, DC * 512], BF16) for i in range(2)]
        tUQ = [arD3.sbuf_tensor(f"D_uq{i}", [128, DC * 512], BF16) for i in range(2)]
        tQ8w = [arD3.sbuf_tensor(f"D_q8w{i}", [128, 512], FP8) for i in range(2)]
        tSC4 = [arD3.sbuf_tensor(f"D_sc4{i}", [4, 1024], F32) for i in range(2)]
        tSbw = [arD3.sbuf_tensor(f"D_sbw{i}", [128, 512], F32) for i in range(2)]
        tRbw = [arD3.sbuf_tensor(f"D_rbw{i}", [128, 512], F32) for i in range(2)]
        tGf = arD3.sbuf_tensor("D_gf", [128, 512], F32)
        tUf = arD3.sbuf_tensor("D_uf", [128, 512], F32)
        tHf = arD3.sbuf_tensor("D_hf", [128, 8 * 512], F32)
        tBMh = arD3.sbuf_tensor("D_bmh", [128, 128], F32)
        tS2h = arD3.sbuf_tensor("D_s2h", [128, 128], F32)
        tR2h = arD3.sbuf_tensor("D_r2h", [128, 128], F32)
        tQ8h = arD3.sbuf_tensor("D_q8h", [128, 8 * 512], FP8)
        tHQ = arD3.sbuf_tensor("D_hq", [128, 8 * 512], BF16)
        tHT = arD3.sbuf_tensor("D_ht", [128, 128], BF16)
        arD3.allocate()
        bGf = B('D_gf'); bUf = B('D_uf')
        bHf = B('D_hf'); bBMh = B('D_bmh'); bS2h = B('D_s2h'); bR2h = B('D_r2h')
        bQ8h = B('D_q8h'); bHQ = B('D_hq'); bHT = B('D_ht')

        def quant_wslab(db, wf, bwf, dsc, bdsc, c, ig, out_tile, bout, nwid):
            bSC4 = B(f'D_sc4{db}'); bSbw = B(f'D_sbw{db}'); bRbw = B(f'D_rbw{db}'); bQ8w = B(f'D_q8w{db}')
            p.dma('sync', tSC4[db][:, 0:nwid],
                  dsc[ig * nwid:(ig + 1) * nwid, c * 4: c * 4 + 4].rearrange("i g -> g i"),
                  reads=[bdsc], writes=[bSC4], slow=True)
            p.dma('sync', tSC4[db][:, nwid:2 * nwid],
                  dsc[ig * nwid:(ig + 1) * nwid, 64 + c * 4: 64 + c * 4 + 4].rearrange("i g -> g i"),
                  reads=[bdsc], writes=[bSC4], slow=True)
            p.do('tensor', lambda e, db=db, nwid=nwid: e.matmul(out=psF[:, 2048 + db * 512:2048 + db * 512 + nwid],
                 lhsT=tG2P[:, :], rhs=tSC4[db][:, 0:nwid], start=True, stop=True),
                 reads=[bSC4, B('D_g2p')], writes=[pbank[4 + db]])
            p.do('vector', lambda e, db=db, nwid=nwid: e.tensor_copy(out=tSbw[db][:, 0:nwid], in_=psF[:, 2048 + db * 512:2048 + db * 512 + nwid]),
                 reads=[pbank[4 + db]], writes=[bSbw])
            p.do('tensor', lambda e, db=db, nwid=nwid: e.matmul(out=psF[:, 2048 + db * 512:2048 + db * 512 + nwid],
                 lhsT=tG2P[:, :], rhs=tSC4[db][:, nwid:2 * nwid], start=True, stop=True),
                 reads=[bSC4, B('D_g2p')], writes=[pbank[4 + db]])
            p.do('vector', lambda e, db=db, nwid=nwid: e.tensor_copy(out=tRbw[db][:, 0:nwid], in_=psF[:, 2048 + db * 512:2048 + db * 512 + nwid]),
                 reads=[pbank[4 + db]], writes=[bRbw])
            p.do('vector', lambda e, db=db, nwid=nwid: e.tensor_tensor(out=tQ8w[db][:, 0:nwid], in0=wf[:, 0:nwid], in1=tRbw[db][:, 0:nwid], op=ALU.mult),
                 reads=[bwf, bRbw], writes=[bQ8w])
            p.do('vector', lambda e, db=db, out_tile=out_tile, nwid=nwid: e.tensor_tensor(out=out_tile, in0=tQ8w[db][:, 0:nwid], in1=tSbw[db][:, 0:nwid], op=ALU.mult),
                 reads=[bQ8w, bSbw], writes=[bout])

        for ig in range(IC):
            gb = ig % 2
            bGQ = B(f'D_gq{gb}'); bUQ = B(f'D_uq{gb}')
            for c in range(DC):
                db = c % 2
                bGW = B(f'D_gw{db}'); bUW = B(f'D_uw{db}')
                p.dma('sync', tGW2[db][:, :], i_gw[c * 128:(c + 1) * 128, ig * 512:(ig + 1) * 512], writes=[bGW])
                p.dma('gpsimd', tUW2[db][:, :], i_uw[c * 128:(c + 1) * 128, ig * 512:(ig + 1) * 512], writes=[bUW])
                quant_wslab(db, tGW2[db], bGW, d_gsco, B('d_gsco'), c, ig, tGQ[gb][:, c * 512:(c + 1) * 512], bGQ, 512)
                quant_wslab(db, tUW2[db], bUW, d_usco, B('d_usco'), c, ig, tUQ[gb][:, c * 512:(c + 1) * 512], bUQ, 512)
            for tt in range(8):
                for c in range(DC):
                    p.do('tensor', lambda e, tt=tt, c=c, gb=gb: e.matmul(
                        out=psF[:, 0:512], lhsT=tX3F[:, c * TL + tt * 128: c * TL + (tt + 1) * 128],
                        rhs=tGQ[gb][:, c * 512:(c + 1) * 512], start=(c == 0), stop=(c == DC - 1)),
                        reads=[bX3F, bGQ], writes=[pbank[0]])
                    p.do('tensor', lambda e, tt=tt, c=c, gb=gb: e.matmul(
                        out=psF[:, 512:1024], lhsT=tX3F[:, c * TL + tt * 128: c * TL + (tt + 1) * 128],
                        rhs=tUQ[gb][:, c * 512:(c + 1) * 512], start=(c == 0), stop=(c == DC - 1)),
                        reads=[bX3F, bUQ], writes=[pbank[1]])
                p.do('scalar', lambda e: e.activation(out=tGf[:, :], in_=psF[:, 0:512], func=ACT.Silu),
                     reads=[pbank[0]], writes=[bGf])
                p.do('scalar', lambda e: e.activation(out=tUf[:, :], in_=psF[:, 512:1024], func=ACT.Copy),
                     reads=[pbank[1]], writes=[bUf])
                p.do('vector', lambda e, tt=tt: e.tensor_tensor(out=tHf[:, tt * 512:(tt + 1) * 512], in0=tGf[:, :], in1=tUf[:, :], op=ALU.mult),
                     reads=[bGf, bUf], writes=[bHf])
            p.do('vector', lambda e: e.tensor_reduce(out=tBMh[:, 0:128],
                 in_=tHf[:, :].rearrange("p (g b) -> p g b", b=32), axis=AX.X, op=ALU.max,
                 apply_absolute_value=True), reads=[bHf], writes=[bBMh])
            p.pad('v', 2)
            p.do('vector', lambda e: e.tensor_scalar(out=tBMh[:, :], in0=tBMh[:, :], scalar1=1.0 / 448.0, scalar2=None, op0=ALU.mult),
                 reads=[bBMh], writes=[bBMh])
            p.pad('v', 2)
            p.do('vector', lambda e: e.tensor_scalar(out=tBMh[:, :], in0=tBMh[:, :], scalar1=1e-30, scalar2=None, op0=ALU.max),
                 reads=[bBMh], writes=[bBMh])
            p.pad('v', 2)
            p.do('vector', lambda e: e.tensor_scalar(out=tS2h[:, :], in0=tBMh[:, :], scalar1=2.0, scalar2=None, op0=ALU.mult),
                 reads=[bBMh], writes=[bS2h])
            p.do('vector', lambda e: e.reciprocal(out=tR2h[:, :], in_=tBMh[:, :]), reads=[bBMh], writes=[bR2h])
            p.pad('v', 2)
            p.do('vector', lambda e: e.tensor_scalar(out=tR2h[:, :], in0=tR2h[:, :], scalar1=0.5, scalar2=None, op0=ALU.mult),
                 reads=[bR2h], writes=[bR2h])
            p.pad('v', 2)
            p.do('vector', lambda e: e.tensor_tensor(
                out=tQ8h[:, :].rearrange("p (g b) -> p g b", b=32),
                in0=tHf[:, :].rearrange("p (g b) -> p g b", b=32),
                in1=tR2h[:, :].to_broadcast((128, 128, 32)), op=ALU.mult),
                reads=[bHf, bR2h], writes=[bQ8h])
            p.do('vector', lambda e: e.tensor_tensor(
                out=tHQ[:, :].rearrange("p (g b) -> p g b", b=32),
                in0=tQ8h[:, :].rearrange("p (g b) -> p g b", b=32),
                in1=tS2h[:, :].to_broadcast((128, 128, 32)), op=ALU.mult),
                reads=[bQ8h, bS2h], writes=[bHQ])
            for tt in range(8):
                for sub in range(4):
                    p.do('tensor', lambda e, tt=tt, sub=sub: e.transpose(
                        out=psB[:, 0:128], in_=tHQ[:, tt * 512 + sub * 128: tt * 512 + (sub + 1) * 128], identity=t_idb[:, :]),
                        reads=[bHQ, B('idb')], writes=[pbB])
                    p.do('vector', lambda e: e.tensor_copy(out=tHT[:, :], in_=psB[:, 0:128]),
                         reads=[pbB], writes=[bHT])
                    p.dma('sync', d_hq[ig * 512 + sub * 128: ig * 512 + (sub + 1) * 128, tt * 128:(tt + 1) * 128],
                          tHT[:, :], reads=[bHT], writes=[B('d_hq')])
        if 'Dtaps' in dbg:
            for kk in range(II // 128):
                p.dma('gpsimd', dbg_outs['o_hq'][kk * 128:(kk + 1) * 128, :], d_hq[kk * 128:(kk + 1) * 128, :],
                      reads=[B('d_hq')], writes=[B('o_hq')])
            p.dma('gpsimd', dbg_outs['o_x3f'][:, :], tX3F[:, :], reads=[bX3F], writes=[B('o_x3f')])
        p.fence()
        arD3.free()

        arD4 = AutoArena(nc)
        tDWf = [arD4.sbuf_tensor(f"D_dwf{i}", [128, 256], F32) for i in range(2)]
        tDWQ = [arD4.sbuf_tensor(f"D_dwq{i}", [128, 256], BF16) for i in range(2)]
        tHTm = [arD4.sbuf_tensor(f"D_htm{i}", [128, TL], BF16) for i in range(2)]
        tQ8d = arD4.sbuf_tensor("D_q8d", [128, 256], FP8)
        tSC4d = arD4.sbuf_tensor("D_sc4d", [4, 256], F32)
        tSbd = arD4.sbuf_tensor("D_sbd", [128, 256], F32)
        tRbd = arD4.sbuf_tensor("D_rbd", [128, 256], F32)
        tResD = [arD4.sbuf_tensor(f"D_res{i}", [128, TL], F32) for i in range(2)]
        tScD = [arD4.sbuf_tensor(f"D_scd{i}", [128, TL], F32) for i in range(2)]
        arD4.allocate()
        bQ8d = B('D_q8d'); bSC4d = B('D_sc4d'); bSbd = B('D_sbd'); bRbd = B('D_rbd')
        for mg in range(8):
            for k in range(II // 128):
                db = k % 2
                bDWf = B(f'D_dwf{db}'); bDWQ = B(f'D_dwq{db}'); bHTm = B(f'D_htm{db}')
                p.dma('sync', tDWf[db][:, :], i_dw[k * 128:(k + 1) * 128, mg * 256:(mg + 1) * 256], writes=[bDWf])
                p.dma('gpsimd', tHTm[db][:, :], d_hq[k * 128:(k + 1) * 128, :], reads=[B('d_hq')], writes=[bHTm])
                for (colbase, dstT, bdst) in ((0, tSbd, bSbd), (256, tRbd, bRbd)):
                    p.dma('gpsimd', tSC4d[:, 0:256],
                          d_dsco[mg * 256:(mg + 1) * 256, colbase + k * 4: colbase + k * 4 + 4].rearrange("i g -> g i"),
                          reads=[B('d_dsco')], writes=[bSC4d], slow=True)
                    p.do('tensor', lambda e: e.matmul(out=psF[:, 2048:2048 + 256], lhsT=tG2P[:, :],
                         rhs=tSC4d[:, 0:256], start=True, stop=True), reads=[bSC4d, B('D_g2p')], writes=[pbank[4]])
                    p.do('vector', lambda e, dstT=dstT: e.tensor_copy(out=dstT[:, :], in_=psF[:, 2048:2048 + 256]),
                         reads=[pbank[4]], writes=[bdst])
                p.do('vector', lambda e, db=db: e.tensor_tensor(out=tQ8d[:, :], in0=tDWf[db][:, :], in1=tRbd[:, :], op=ALU.mult),
                     reads=[bDWf, bRbd], writes=[bQ8d])
                p.do('vector', lambda e, db=db: e.tensor_tensor(out=tDWQ[db][:, :], in0=tQ8d[:, :], in1=tSbd[:, :], op=ALU.mult),
                     reads=[bQ8d, bSbd], writes=[bDWQ])
                for mm in range(2):
                    for nn in range(2):
                        pb = pbank[mm * 2 + nn]
                        p.do('tensor', lambda e, k=k, mm=mm, nn=nn, db=db: e.matmul(
                            out=psF[:, (mm * 2 + nn) * 512:(mm * 2 + nn + 1) * 512],
                            lhsT=tDWQ[db][:, mm * 128:(mm + 1) * 128],
                            rhs=tHTm[db][:, nn * 512:(nn + 1) * 512],
                            start=(k == 0), stop=(k == II // 128 - 1)),
                            reads=[bDWQ, bHTm], writes=[pb])
            for mm in range(2):
                m = mg * 2 + mm
                db = mm % 2
                bResD = B(f'D_res{db}'); bScD = B(f'D_scd{db}')
                p.dma('gpsimd', tResD[db][:, :], d_s3[m * 128:(m + 1) * 128, :], reads=[B('d_s3')], writes=[bResD])
                for nn in range(2):
                    pb = pbank[mm * 2 + nn]
                    p.do('vector', lambda e, mm=mm, nn=nn, db=db: e.tensor_tensor(
                        out=tScD[db][:, nn * 512:(nn + 1) * 512],
                        in0=psF[:, (mm * 2 + nn) * 512:(mm * 2 + nn + 1) * 512],
                        in1=tResD[db][:, nn * 512:(nn + 1) * 512], op=ALU.add),
                        reads=[pb, bResD], writes=[bScD])
                p.dma('sync', final_out[m * 128:(m + 1) * 128, :], tScD[db][:, :], reads=[bScD], writes=[B('out')])
        p.fence()
        arD4.free()
        arDL.free()

    if phases[-1] == 'A':
        for m in range(DC):
            p.dma('sync', final_out[m * 128:(m + 1) * 128, :], d_s1[m * 128:(m + 1) * 128, :],
                  reads=[B('d_s1')], writes=[B('out')])
    if phases[-1] == 'B':
        for m in range(DC):
            p.dma('sync', final_out[m * 128:(m + 1) * 128, :], d_s2[m * 128:(m + 1) * 128, :],
                  reads=[B('d_s2')], writes=[B('out')])
    if phases[-1] == 'C':
        for m in range(DC):
            p.dma('sync', final_out[m * 128:(m + 1) * 128, :], d_s3[m * 128:(m + 1) * 128, :],
                  reads=[B('d_s3')], writes=[B('out')])

    finals = p.barrier_tok()
    with nc.Block() as block:
        for eng in ENGS:
            its = p.items[eng]
            if not its:
                continue
            getattr(block, eng)(p.mk_emitter(its, sems, finals))

    stack.close()
    return nc, out_names


def _prep_inputs(inputs):
    x = np.asarray(inputs['L0_attn_x'])[0]
    xT = np.ascontiguousarray(x.T)
    z64 = np.zeros((D, HIST), np.float32)
    piw = np.ascontiguousarray(np.asarray(inputs['L0_attn_proj_in_w']).T)
    pow_ = np.ascontiguousarray(np.asarray(inputs['L0_attn_proj_out_w']).T)
    pib = np.ascontiguousarray(np.asarray(inputs['L0_attn_proj_in_b']).reshape(D, 1))
    pob = np.ascontiguousarray(np.asarray(inputs['L0_attn_proj_out_b']).reshape(D, 1))
    A = np.ascontiguousarray(np.asarray(inputs['L0_attn_A']))
    BwT = np.ascontiguousarray(np.asarray(inputs['L0_attn_B_w']).T)
    C2 = np.ascontiguousarray(np.tile(np.asarray(inputs['L0_attn_C_w']).T, (LF, 1)))
    Dp = np.ascontiguousarray(np.asarray(inputs['L0_attn_D_param']).reshape(D, 1))
    idf = np.eye(128, dtype=np.float32)
    idb = np.eye(128, dtype=ml_dtypes.bfloat16)
    consts = np.zeros((128, 8), np.float32)
    consts[:, 0] = EPS_LN
    rep16 = (np.arange(128)[None, :] % 16 == np.arange(16)[:, None]).astype(np.float32)
    onesm = np.ones((128, 128), np.float32)
    wg = np.ascontiguousarray(np.asarray(inputs['L0_ffn_wq_gate']).T)
    wu = np.ascontiguousarray(np.asarray(inputs['L0_ffn_wq_up']).T)
    wd = np.ascontiguousarray(np.asarray(inputs['L0_ffn_wq_down']).T)
    sg = np.ascontiguousarray(np.asarray(inputs['L0_ffn_s_gate']).reshape(HID, 1))
    su = np.ascontiguousarray(np.asarray(inputs['L0_ffn_s_up']).reshape(HID, 1))
    sd2 = np.ascontiguousarray(np.asarray(inputs['L0_ffn_s_down']).reshape(D, 1))
    wqT = np.ascontiguousarray(np.asarray(inputs['L1_attn_wq']).T)
    wkT = np.ascontiguousarray(np.asarray(inputs['L1_attn_wk']).T)
    wvT = np.ascontiguousarray(np.asarray(inputs['L1_attn_wv']).T)
    woT = np.ascontiguousarray(np.asarray(inputs['L1_attn_wo']).T)
    iarr = (np.arange(0, HD, 2).astype(np.float32) / np.float32(HD)).astype(np.float32)
    inv32 = (np.float32(1.0) / np.power(np.float32(10000.0), iarr, dtype=np.float32)).astype(np.float32)
    tpos = np.arange(T, dtype=np.float32)
    freqs = (tpos[:, None] * inv32[None, :]).astype(np.float32)
    emb = np.concatenate([freqs, freqs], axis=1)
    cosT = np.ascontiguousarray(np.cos(emb).astype(np.float32).T)
    sinT = np.ascontiguousarray(np.sin(emb).astype(np.float32).T)
    gwN = np.ascontiguousarray(np.asarray(inputs['L1_ffn_gate_w']))
    uwN = np.ascontiguousarray(np.asarray(inputs['L1_ffn_up_w']))
    dwN = np.ascontiguousarray(np.asarray(inputs['L1_ffn_down_w']))
    gwT = np.ascontiguousarray(gwN.T); uwT = np.ascontiguousarray(uwN.T); dwT = np.ascontiguousarray(dwN.T)
    g2p = (np.arange(128)[None, :] // 32 == np.arange(4)[:, None]).astype(np.float32)
    maps = []
    for c in range(NCORE):
        lo = c * TL
        if c == 0:
            xloc = np.ascontiguousarray(np.concatenate([z64, xT[:, 0:TL]], axis=1))
            hm = np.zeros((S, HIST), np.float32)
        else:
            xloc = np.ascontiguousarray(xT[:, lo - HIST: lo + TL])
            hm = np.ones((S, HIST), np.float32)
        maps.append({
            'i_x': xloc, 'i_piw': piw, 'i_pib': pib, 'i_pow': pow_, 'i_pob': pob,
            'i_A': A, 'i_BwT': BwT, 'i_C2': C2, 'i_Dp': Dp, 'i_hm': hm,
            'i_idf': idf, 'i_idb': idb, 'i_const': consts, 'i_rep16': rep16, 'i_ones': onesm,
            'i_wg': wg, 'i_wu': wu, 'i_wd': wd, 'i_sg': sg, 'i_su': su, 'i_sd2': sd2,
            'i_wq': wqT, 'i_wk': wkT, 'i_wv': wvT, 'i_wo': woT,
            'i_cos': cosT[:, lo:lo + TL].copy(), 'i_sin': sinT[:, lo:lo + TL].copy(),
            'i_gw': gwT, 'i_uw': uwT, 'i_dw': dwT,
            'i_gwn': gwN[lo:lo + TL].copy(), 'i_uwn': uwN[lo:lo + TL].copy(),
            'i_dwn': dwN[c * (D // NCORE):(c + 1) * (D // NCORE)].copy(), 'i_g2p': g2p,
        })
    return maps


_PHASES = ('A', 'B', 'C', 'D')

def kernel(**inputs):
    maps = _prep_inputs(inputs)
    nc, out_names = build_program(phases=_PHASES)
    res = run_bass_kernel_spmd(nc, maps, core_ids=list(range(NCORE)))
    shards = [np.asarray(r['out_shard']) for r in res.results]
    out = np.concatenate([s.T for s in shards], axis=0).reshape(1, T, D).astype(np.float32)
    return out
